# version 02
# baseline (speedup 1.0000x reference)
import numpy as np
import ml_dtypes
import concourse.bass as bass
import concourse.mybir as mybir
from concourse.bass_utils import run_bass_kernel_spmd

F32 = mybir.dt.float32
BF16 = mybir.dt.bfloat16
I32 = mybir.dt.int32
AF = mybir.ActivationFunctionType
ALU = mybir.AluOpType
AX = mybir.AxisListType

B, S, D = 4, 4096, 4096
H, KVH, G, DH = 32, 8, 4, 128
NBLK = S // 128
OWN = S // 2
NT = OWN // 128
NMEG, MEG = 4, 1024
SCALE = DH ** -0.5
ESH = 30.0
NEGB = -1.0e9
EPS = 1e-5

PI_POS = np.array([j // 2 if j % 2 == 0 else 64 + j // 2 for j in range(DH)])
PI_INV = np.argsort(PI_POS)


def prep_inputs(inp):
    w_qkv = np.asarray(inp["L0_attn_w_qkv"]); w_o = np.asarray(inp["L0_attn_w_o"])
    wffn = np.asarray(inp["L0_ffn_weight"])
    k_w1 = np.asarray(inp["L0_attn_k_w1"]); k_b1 = np.asarray(inp["L0_attn_k_b1"])
    k_w2 = np.asarray(inp["L0_attn_k_w2"]); k_b2 = np.asarray(inp["L0_attn_k_b2"])
    v_w1 = np.asarray(inp["L0_attn_v_w1"]); v_b1 = np.asarray(inp["L0_attn_v_b1"])
    v_w2 = np.asarray(inp["L0_attn_v_w2"]); v_b2 = np.asarray(inp["L0_attn_v_b2"])
    k_pos = np.asarray(inp["L0_attn_k_pos"]); v_pos = np.asarray(inp["L0_attn_v_pos"])
    mem_kv = np.asarray(inp["L0_attn_mem_kv"])
    w_gate = np.asarray(inp["L0_attn_w_gate"]); b_gate = np.asarray(inp["L0_attn_b_gate"])
    x = np.asarray(inp["L0_attn_hidden_states"])

    bf = ml_dtypes.bfloat16
    C = np.ascontiguousarray

    def tile_oc(wT, ocw):
        Din, O = wT.shape
        return C(wT.reshape(Din // 128, 128, O // ocw, ocw)
                 .transpose(2, 1, 0, 3).reshape(O // ocw, 128, -1))

    def tile_flat(wT, ocw):
        Din, O = wT.shape
        return C(wT.reshape(Din // 128, 128, O).transpose(1, 0, 2)
                 .reshape(128, -1))

    shared = {
        "wffnT": tile_oc(wffn.T, 256),
        "wqT": tile_oc(w_qkv[0:H * DH].T, 256),
        "wkT": tile_oc(w_qkv[H * DH:H * DH + KVH * DH].T, 256),
        "wvT": tile_oc(w_qkv[H * DH + KVH * DH:].T, 256),
        "wgT": tile_flat(w_gate.T, 96),
        "woT": tile_oc(w_o.T, 256),
        "w1kT": tile_oc(k_w1.T, 128),
        "w1vT": tile_oc(v_w1.T, 128),
        "w2kT": tile_flat(k_w2.T, 128),
        "w2vT": tile_flat(v_w2.T, 128),
        "kb1": C(k_b1.reshape(16, 128).T),
        "vb1": C(v_b1.reshape(16, 128).T),
        "kb2p": C(k_b2.reshape(128, 1)),
        "vb2": C(v_b2.reshape(128, 1)),
        "kpT": C(np.transpose(k_pos, (2, 0, 1)).reshape(128, 128)),
        "vpT": C(np.transpose(v_pos, (2, 0, 1)).reshape(128, 128)),
        "memk": C(mem_kv[0][:, 0, :].T),
        "memv": C(mem_kv[1][:, 0, :].reshape(1, 8 * 128)),
        "bgate": C(b_gate.reshape(1, 96)),
        "identb": np.eye(128, dtype=bf),
        "identf": np.eye(128, dtype=np.float32),
    }
    kk = np.arange(128)[:, None]; qq = np.arange(128)[None, :]
    diag = ((kk <= qq) & (kk >= qq - 64)).astype(np.float32)
    prev = (kk >= qq + 64).astype(np.float32)
    zero = np.zeros((128, 128), np.float32)
    shared["fmdiag"] = (((kk // 16) == (qq // 16)) & (kk <= qq)).astype(bf)

    inv = 1.0 / 10000.0 ** (np.arange(0, DH, 2, dtype=np.float64) / DH)
    per_core = []
    for c in range(8):
        b, hh = c // 2, c % 2
        if hh == 0:
            tb = np.arange(NBLK)
        else:
            tb = np.arange(NBLK).reshape(-1, 2)[:, ::-1].reshape(-1)
        xp = C(x[b].reshape(NBLK, 128, D)[tb].reshape(S, D))
        pos = (tb[:, None] * 128 + np.arange(128)[None, :]).reshape(-1)
        ang = pos[:, None].astype(np.float64) * inv[None, :]
        cosT = C(np.cos(ang).astype(bf)); sinT = C(np.sin(ang).astype(bf))
        gq = pos.reshape(NBLK, 128)[0::2].reshape(-1)
        jb = np.arange(256)
        tstart = tb[jb // 8] * 128 + (jb % 8) * 16
        vis = (tstart[:, None] + 16 <= gq[None, :])
        cmask = np.ones((257, OWN), np.float32); cmask[1:] = vis
        cneg = np.where(vis, 0.0, NEGB).astype(bf)
        per_core.append({
            "x_full": xp, "cosT": cosT, "sinT": sinT,
            "cmask": cmask.astype(bf), "cneg": C(cneg),
            "smA": (prev if hh == 0 else zero).astype(bf),
            "smB": diag.astype(bf),
            "smC": (zero if hh == 0 else prev).astype(bf),
        })
    return shared, per_core


ENGINES = ("sync", "tensor", "vector", "scalar", "gpsimd")


class Em:
    def __init__(self, nc):
        self.nc = nc
        self.ops = {e: [] for e in ENGINES}
        self.sem = {}
        self.cnt = {}
        for s in ("LD", "GQ", "MM", "VE", "SE", "PO"):
            self.sem[s] = nc.alloc_semaphore(s)
            self.cnt[s] = 0
        self.last_wait = {e: {} for e in ENGINES}

    def on(self, eng, fn):
        self.ops[eng].append(fn)

    def wait(self, eng, sem, val):
        if val is None or val <= 0:
            return
        lw = self.last_wait[eng]
        if lw.get(sem, 0) >= val:
            return
        lw[sem] = val
        s = self.sem[sem]
        self.on(eng, lambda E: E.wait_ge(s, val))

    def emit(self, eng, build, sem=None, n=1):
        if sem is None or n is None:
            self.on(eng, lambda E: build(E))
            return None
        s = self.sem[sem]
        self.cnt[sem] += n
        v = self.cnt[sem]
        self.on(eng, lambda E: build(E).then_inc(s, n))
        return v

    def barrier(self):
        vals = dict(self.cnt)
        for e in ENGINES:
            for s, v in vals.items():
                self.wait(e, s, v)

    def finalize(self):
        with self.nc.Block() as blk:
            blk.sync(lambda E: [f(E) for f in self.ops["sync"]])
            blk.tensor(lambda E: [f(E) for f in self.ops["tensor"]])
            blk.vector(lambda E: [f(E) for f in self.ops["vector"]])
            blk.scalar(lambda E: [f(E) for f in self.ops["scalar"]])
            blk.gpsimd(lambda E: [f(E) for f in self.ops["gpsimd"]])


class Rot:
    def __init__(self, em, n):
        self.em, self.n = em, n
        self.rel = [None] * n

    def acquire(self, i, eng):
        r = self.rel[i % self.n]
        if r is not None:
            self.em.wait(eng, r[0], r[1])

    def release(self, i, sem, val):
        self.rel[i % self.n] = (sem, val)


def build_program(dbg=(), upto=5):
    nc = bass.Bass("TRN2", target_bir_lowering=False)
    dbg = set(dbg)
    em = Em(nc)

    ap = {}
    def ein(name, shape, dtype):
        ap[name] = nc.dram_tensor(name, shape, dtype, kind="ExternalInput").ap()
    ein("x_full", [S, D], F32)
    ein("wffnT", [16, 128, 32 * 256], I32)
    ein("wqT", [16, 128, 32 * 256], F32); ein("wkT", [4, 128, 32 * 256], F32)
    ein("wvT", [4, 128, 32 * 256], F32); ein("wgT", [128, 32 * 96], F32)
    ein("woT", [16, 128, 32 * 256], F32)
    ein("w1kT", [16, 128, 16 * 128], F32); ein("w1vT", [16, 128, 16 * 128], F32)
    ein("w2kT", [128, 16 * 128], F32); ein("w2vT", [128, 16 * 128], F32)
    ein("kb1", [128, 16], F32); ein("vb1", [128, 16], F32)
    ein("kb2p", [128, 1], F32); ein("vb2", [128, 1], F32)
    ein("kpT", [128, 128], F32); ein("vpT", [128, 128], F32)
    ein("memk", [128, 8], F32); ein("memv", [1, 8 * 128], F32)
    ein("bgate", [1, 96], F32)
    ein("cosT", [S, 64], BF16); ein("sinT", [S, 64], BF16)
    ein("identb", [128, 128], BF16); ein("identf", [128, 128], F32)
    ein("smA", [128, 128], BF16); ein("smB", [128, 128], BF16)
    ein("smC", [128, 128], BF16); ein("fmdiag", [128, 128], BF16)
    ein("cmask", [257, OWN], BF16); ein("cneg", [256, OWN], BF16)

    out = nc.dram_tensor("out", [OWN, D], F32, kind="ExternalOutput").ap()
    x1d = nc.dram_tensor("x1d", [S, D], F32, kind="Internal").ap()
    x2d = nc.dram_tensor("x2d", [OWN, D], F32, kind="Internal").ap()
    hs2Td = nc.dram_tensor("hs2Td", [NMEG, 128, 32 * MEG], BF16, kind="Internal").ap()
    qrod = nc.dram_tensor("qrod", [H, 128, OWN], BF16, kind="Internal").ap()
    qnod = nc.dram_tensor("qnod", [H, 128, OWN], BF16, kind="Internal").ap()
    rkd = nc.dram_tensor("rkd", [KVH, 128, S], BF16, kind="Internal").ap()
    vtokd = nc.dram_tensor("vtokd", [KVH, S, DH], BF16, kind="Internal").ap()
    maskfd = nc.dram_tensor("maskfd", [KVH, 256, OWN], BF16, kind="Internal").ap()

    dbg_out = {}
    def dout(name, shape, dtype=F32):
        if name in dbg:
            dbg_out[name] = nc.dram_tensor("dbg_" + name, shape, dtype,
                                           kind="ExternalOutput").ap()
        return dbg_out.get(name)
    d_x1 = dout("x1", [S, D])
    d_hs0 = dout("hs0", [128, D], BF16)
    d_hsT0 = dout("hsT0", [128, 32 * MEG], BF16)
    d_w0 = dout("w0", [128, 32 * 256], BF16)
    d_q = dout("q", [H, 128, OWN], BF16)
    d_rk = dout("rk", [KVH, 128, S], BF16)
    d_vtok = dout("vtok", [KVH, S, DH], BF16)
    d_ck = dout("ck", [128, KVH * 256], BF16)
    d_cv = dout("cv", [128, KVH * 256], BF16)
    d_gates = dout("gates", [128, NT * 96])
    d_maskf = dout("maskf", [KVH, 256, OWN], BF16)
    d_impq = dout("impq", [KVH, 8, 128, 512])
    d_csim = dout("csim", [G, 2, 128, 256])
    d_mix = dout("mix", [128, NT * D], BF16)
    d_x2 = dout("x2", [OWN, D])

    AR = lambda name, shape, dt: nc.alloc_sbuf_tensor(name, shape, dt, side="right")
    identb = AR("c_identb", [128, 128], BF16)
    identf = AR("c_identf", [128, 128], F32)
    onesrow = AR("c_onesrow", [1, 128], BF16)
    gates = AR("c_gates", [128, NT * 96], F32)
    ck_fm = AR("c_ck_fm", [128, KVH * 256], BF16)
    cv_fm = AR("c_cv_fm", [128, KVH * 256], BF16)
    memkb = AR("c_memkb", [128, 8], BF16)
    memvaug = AR("c_memvaug", [1, 8 * 132], BF16)
    eshv = AR("c_eshv", [128, 1], F32)

    VS = nc.alloc_semaphore("VS")
    SS = nc.alloc_semaphore("SS")
    _vsn = [0]; _ssn = [0]

    def vch(E, inst):
        _vsn[0] += 1
        inst.then_inc(VS, 1)
        E.wait_ge(VS, _vsn[0])
        return inst

    def sch(E, inst):
        _ssn[0] += 1
        inst.then_inc(SS, 1)
        E.wait_ge(SS, _ssn[0])
        return inst

    PSF = [nc.alloc_psum_tensor(f"psf{i}", [128, 512], F32) for i in range(5)]
    PTB = [nc.alloc_psum_tensor(f"ptb{i}", [128, 128], BF16) for i in range(2)]
    PTF = nc.alloc_psum_tensor("ptf", [128, 128], F32)
    tb_slots = Rot(em, 2)
    tf_slots = Rot(em, 1)

    em.emit("sync", lambda E: E.dma_start(identb[:, :], ap["identb"]), "LD", 16)
    em.emit("sync", lambda E: E.dma_start(identf[:, :], ap["identf"]), "LD", 16)
    em.emit("gpsimd", lambda E: E.dma_start(memkb[:, :], ap["memk"]), "GQ", 16)
    em.emit("gpsimd", lambda E: E.dma_start(
        memvaug.ap().rearrange("p (h c) -> p h c", c=132)[:, :, 0:128],
        ap["memv"].rearrange("p (h c) -> p h c", c=128)), "GQ", 16)
    def _ones(E):
        E.memset(onesrow[:, :], 1.0)
        E.memset(eshv[:, :], -ESH)
        for hh in range(8):
            E.memset(memvaug[:, hh * 132 + 128:hh * 132 + 129], 1.0)
        return E.engine_nop()
    em.emit("gpsimd", _ones, "PO", 1)
    em.barrier()

    def emit_ln(xin_t, hout_t, stat_t, sq_t, prev_se):
        if prev_se is not None:
            em.wait("vector", "SE", prev_se)
        def vstats(E):
            vch(E, E.reduce_sum(stat_t[:, 0:1], xin_t[:, :], axis=AX.X))
            vch(E, E.scalar_tensor_tensor(sq_t[:, :], xin_t[:, :], 1.0, xin_t[:, :],
                                          ALU.bypass, ALU.mult,
                                          accum_out=stat_t[:, 1:2]))
            E.tensor_scalar_mul(stat_t[:, 0:1], stat_t[:, 0:1], 1.0 / D)
            vch(E, E.tensor_scalar_mul(stat_t[:, 1:2], stat_t[:, 1:2], 1.0 / D))
            vch(E, E.scalar_tensor_tensor(stat_t[:, 2:3], stat_t[:, 0:1], 1.0,
                                          stat_t[:, 0:1], ALU.bypass, ALU.mult))
            vch(E, E.tensor_sub(stat_t[:, 1:2], stat_t[:, 1:2], stat_t[:, 2:3]))
            return E.tensor_scalar_add(stat_t[:, 1:2], stat_t[:, 1:2], EPS)
        ve = em.emit("vector", vstats, "VE", 1)
        em.wait("scalar", "VE", ve)
        def sr0(E):
            sch(E, E.activation(stat_t[:, 3:4], stat_t[:, 1:2], AF.Ln))
            return E.activation(stat_t[:, 3:4], stat_t[:, 3:4], AF.Exp, scale=-0.5)
        se0 = em.emit("scalar", sr0, "SE", 1)
        em.wait("vector", "SE", se0)
        def newton(E):
            vch(E, E.tensor_mul(stat_t[:, 5:6], stat_t[:, 3:4], stat_t[:, 3:4]))
            vch(E, E.tensor_mul(stat_t[:, 5:6], stat_t[:, 5:6], stat_t[:, 1:2]))
            vch(E, E.tensor_scalar(stat_t[:, 5:6], stat_t[:, 5:6], -0.5, 1.5,
                                   ALU.mult, ALU.add))
            return E.tensor_mul(stat_t[:, 3:4], stat_t[:, 3:4], stat_t[:, 5:6])
        ven = em.emit("vector", newton, "VE", 1)
        em.wait("scalar", "VE", ven)
        def sapply(E):
            sch(E, E.mul(stat_t[:, 4:5], stat_t[:, 0:1], -1.0))
            sch(E, E.mul(stat_t[:, 4:5], stat_t[:, 4:5], stat_t[:, 3:4]))
            return E.activation(hout_t[:, :], xin_t[:, :], AF.Identity,
                                bias=stat_t[:, 4:5], scale=stat_t[:, 3:4])
        return em.emit("scalar", sapply, "SE", 1)

    def emit_T(src, dst, se_wait=None, f32=False):
        if f32:
            slots, ident = tf_slots, identf
            i = emit_T.ctr[True] = emit_T.ctr.get(True, 0) + 1
            psum = PTF
        else:
            slots, ident = tb_slots, identb
            i = emit_T.ctr[False] = emit_T.ctr.get(False, 0) + 1
            psum = PTB[i % 2]
        slots.acquire(i, "tensor")
        if se_wait is not None:
            em.wait("tensor", "SE", se_wait)
        mmv = em.emit("tensor", lambda E, s=src, ps=psum, idt=ident:
                      E.transpose(ps[:, :], s, idt[:, :]), "MM", 1)
        em.wait("vector", "MM", mmv)
        ve = em.emit("vector", lambda E, d=dst, ps=psum:
                     E.tensor_copy(d, ps[:, :]), "VE", 1)
        slots.release(i, "VE", ve)
        return ve
    emit_T.ctr = {}

    def phase_ffn(src_ap, dst_ap, nmeg, store_hsT=None, d_tap=None, tag="f"):
        OCW, NOC = 256, 16
        xin = [nc.sbuf_tensor(f"{tag}_xin{i}", [128, D], F32) for i in range(2)]
        hs = [nc.sbuf_tensor(f"{tag}_hs{i}", [128, D], BF16) for i in range(2)]
        sqf = nc.sbuf_tensor(f"{tag}_sqf", [128, D], F32)
        stat = nc.sbuf_tensor(f"{tag}_stat", [128, 8], F32)
        hsT = nc.sbuf_tensor(f"{tag}_hsT", [128, 32 * MEG], BF16)
        wcol = [nc.sbuf_tensor(f"{tag}_w{i}", [128, 32 * OCW], BF16) for i in range(2)]
        xr = [nc.sbuf_tensor(f"{tag}_xr{i}", [128, OCW], F32) for i in range(4)]
        x1s = [nc.sbuf_tensor(f"{tag}_x1{i}", [128, OCW], F32) for i in range(4)]
        ctxs = xin + hs + [sqf, stat, hsT] + wcol + xr + x1s
        tens = [c.__enter__() for c in ctxs]
        (xin0, xin1, hs0, hs1b, sqf, stat, hsT, w0, w1, xr0, xr1, xr2, xr3,
         s0, s1, s2, s3) = tens
        xin = [xin0, xin1]; hs = [hs0, hs1b]; wcol = [w0, w1]
        xr = [xr0, xr1, xr2, xr3]; x1s = [s0, s1, s2, s3]

        xin_r = Rot(em, 2); hs_r = Rot(em, 2); w_r = Rot(em, 2)
        xr_r = Rot(em, 4); x1_r = Rot(em, 4)
        prev_se = None
        for m in range(nmeg):
            em.wait("vector", "MM", em.cnt["MM"])
            for r in range(8):
                i = m * 8 + r
                row = m * MEG + r * 128
                xin_r.acquire(i, "sync")
                ld = em.emit("sync", lambda E, row=row, b=xin[i % 2]:
                             E.dma_start(b[:, :], src_ap[row:row + 128, :]), "LD", 16)
                em.wait("vector", "LD", ld)
                hs_r.acquire(i, "vector")
                hs_r.acquire(i, "scalar")
                prev_se = emit_ln(xin[i % 2], hs[i % 2], stat, sqf, prev_se)
                xin_r.release(i, "SE", prev_se)
                if i == 0 and tag == "f" and dbg_out.get("hs0") is not None:
                    em.wait("sync", "SE", prev_se)
                    em.emit("sync", lambda E: E.dma_start(dbg_out["hs0"], hs[0][:, :]),
                            "LD", 16)
                lastve = None
                for dc in range(32):
                    lastve = emit_T(hs[i % 2][:, dc * 128:(dc + 1) * 128],
                                    hsT[:, dc * MEG + r * 128:dc * MEG + r * 128 + 128],
                                    se_wait=prev_se)
                hs_r.release(i, "VE", lastve)
            hsT_ready = em.cnt["VE"]
            if m == 0 and tag == "f" and dbg_out.get("hsT0") is not None:
                em.wait("sync", "VE", hsT_ready)
                em.emit("sync", lambda E: E.dma_start(dbg_out["hsT0"], hsT[:, :]),
                        "LD", 16)
            if store_hsT is not None:
                em.wait("sync", "VE", hsT_ready)
                em.emit("sync", lambda E, m=m: E.dma_start(store_hsT[m], hsT[:, :]),
                        "LD", 16)
            for oc in range(NOC):
                wb = wcol[oc % 2]
                w_r.acquire(m * NOC + oc, "gpsimd")
                src_w = ap["wffnT"][oc]
                gq = em.emit("gpsimd", lambda E, wb=wb, s=src_w: E.dma_start(
                    wb[:, :], s), "GQ", 16)
                if m == 0 and oc == 0 and tag == "f" and dbg_out.get("w0") is not None:
                    em.wait("sync", "GQ", gq)
                    em.emit("sync", lambda E, wb=wb: E.dma_start(
                        dbg_out["w0"], wb[:, :]), "LD", 16)
                em.wait("tensor", "GQ", gq)
                em.wait("tensor", "VE", hsT_ready)
                for r in range(8):
                    i = (m * NOC + oc) * 8 + r
                    pb = PSF[r % 4]
                    row = m * MEG + r * 128
                    xr_r.acquire(i, "sync")
                    ldr = em.emit("sync", lambda E, b=xr[r % 4], row=row, oc=oc:
                                  E.dma_start(b[:, :], src_ap[row:row + 128,
                                                             oc * OCW:(oc + 1) * OCW]),
                                  "LD", 16)
                    xr_r.acquire(i, "tensor")
                    for dc in range(32):
                        em.emit("tensor", lambda E, pb=pb, dc=dc, r=r, wb=wb:
                                E.matmul(pb[:, 0:OCW],
                                         hsT[:, dc * MEG + r * 128:dc * MEG + r * 128 + 128],
                                         wb[:, dc * OCW:(dc + 1) * OCW],
                                         start=(dc == 0), stop=(dc == 31)),
                                "MM", 1 if dc == 31 else None)
                    mmv = em.cnt["MM"]
                    em.wait("vector", "MM", mmv)
                    em.wait("vector", "LD", ldr)
                    x1_r.acquire(i, "vector")
                    ve = em.emit("vector", lambda E, d=x1s[r % 4], pb=pb, b=xr[r % 4]:
                                 E.scalar_tensor_tensor(d[:, :], pb[:, 0:OCW], 0.5,
                                                        b[:, :], ALU.mult, ALU.add),
                                 "VE", 1)
                    xr_r.release(i, "VE", ve)
                    em.wait("sync", "VE", ve)
                    st = em.emit("sync", lambda E, d=x1s[r % 4], row=row, oc=oc:
                                 E.dma_start(dst_ap[row:row + 128,
                                                    oc * OCW:(oc + 1) * OCW], d[:, :]),
                                 "LD", 16)
                    if d_tap is not None:
                        st = em.emit("sync", lambda E, d=x1s[r % 4], row=row, oc=oc:
                                     E.dma_start(d_tap[row:row + 128,
                                                       oc * OCW:(oc + 1) * OCW], d[:, :]),
                                     "LD", 16)
                    x1_r.release(i, "LD", st)
                w_r.release(m * NOC + oc, "MM", em.cnt["MM"])
        em.barrier()
        for c in reversed(ctxs):
            c.__exit__(None, None, None)

    phase_ffn(ap["x_full"], x1d, NMEG, d_tap=d_x1)
    if upto < 2:
        em.wait("sync", "LD", em.cnt["LD"])
        em.finalize()
        return nc, dbg_out

    def phase2():
        cm = [
            nc.sbuf_tensor("p2_xin", [128, D], F32),
            nc.sbuf_tensor("p2_hs", [128, D], BF16),
            nc.sbuf_tensor("p2_stat", [128, 8], F32),
            nc.sbuf_tensor("p2_hsT", [128, 32 * MEG], BF16),
            nc.sbuf_tensor("p2_w", [128, 32 * 256], BF16),
            nc.sbuf_tensor("p2_kt0", [128, 256], BF16),
            nc.sbuf_tensor("p2_kt1", [128, 256], BF16),
            nc.sbuf_tensor("p2_rkr", [128, 256], BF16),
            nc.sbuf_tensor("p2_cosm", [128, 8 * 128], BF16),
            nc.sbuf_tensor("p2_kfm", [128, KVH * MEG], BF16),
            nc.sbuf_tensor("p2_vfm", [128, KVH * MEG], BF16),
            nc.sbuf_tensor("p2_cs", [128, 16 * 512], BF16),
            nc.sbuf_tensor("p2_hid", [128, 8 * 512], BF16),
            nc.sbuf_tensor("p2_w10", [128, 16 * 128], BF16),
            nc.sbuf_tensor("p2_w11", [128, 16 * 128], BF16),
            nc.sbuf_tensor("p2_w2k", [128, 16 * 128], BF16),
            nc.sbuf_tensor("p2_w2v", [128, 16 * 128], BF16),
            nc.sbuf_tensor("p2_ckst", [128, 512], BF16),
            nc.sbuf_tensor("p2_stg", [128, 256], BF16),
            nc.sbuf_tensor("p2_gt", [128, 96], F32),
            nc.sbuf_tensor("p2_wg", [128, 32 * 96], BF16),
            nc.sbuf_tensor("p2_bg", [1, 96], BF16),
            nc.sbuf_tensor("p2_kb", [128, 40], F32),
            nc.sbuf_tensor("p2_kp", [128, 128], F32),
            nc.sbuf_tensor("p2_vp", [128, 128], F32),
        ]
        t = [c.__enter__() for c in cm]
        (xin, hs, stat, hsT, wcb, kt0, kt1, rkr, cosm, kfm, vfm, cs, hid,
         w10, w11, w2k, w2v, ckst, stg, gt, wgTb, bgateb, kbs, kpT, vpT) = t
        w1b = [w10, w11]
        ktb = [kt0, kt1]
        em.emit("sync", lambda E: E.dma_start(kbs[:, 0:16], ap["kb1"]), "LD", 16)
        em.emit("sync", lambda E: E.dma_start(kbs[:, 16:32], ap["vb1"]), "LD", 16)
        em.emit("sync", lambda E: E.dma_start(kbs[:, 32:33], ap["kb2p"]), "LD", 16)
        em.emit("sync", lambda E: E.dma_start(kbs[:, 33:34], ap["vb2"]), "LD", 16)
        em.emit("sync", lambda E: E.dma_start(kpT[:, :], ap["kpT"]), "LD", 16)
        em.emit("sync", lambda E: E.dma_start(vpT[:, :], ap["vpT"]), "LD", 16)
        em.emit("gpsimd", lambda E: E.dma_start(bgateb[:, :], ap["bgate"]), "GQ", 16)
        em.emit("gpsimd", lambda E: E.dma_start(wgTb[:, :], ap["wgT"]), "GQ", 16)
        for nm, tt in [("w2kT", w2k), ("w2vT", w2v)]:
            em.emit("gpsimd", lambda E, tt=tt, nm=nm: E.dma_start(
                tt[:, :], ap[nm]), "GQ", 16)
        em.barrier()

        w1_r = Rot(em, 2)
        prev_se = [None]
        mlp1_se = [None, None]
        mlp2_mm = [0]
        ck_evac_se = [None]
        cs_mm_guard = [0]
        w_mm = [0]
        kt_mm = [0, 0]
        rkr_mm = [0]
        ktt_ld = [0, 0]
        stg_ld = [0]
        gse_ = [0, 0]

        def rope_tile(E, dstt, srct, nh, cosAP, sinAP):
            sv = srct.ap().rearrange("p (h i t) -> p h t i", i=64, t=2)
            dv = dstt.ap().rearrange("p (h i t) -> p h t i", i=64, t=2)
            cb = cosAP.unsqueeze(1).broadcast_to([128, nh, 64])
            sb = sinAP.unsqueeze(1).broadcast_to([128, nh, 64])
            e = sv[:, :, 0, :]; o = sv[:, :, 1, :]
            rt_ = rtmp.ap().rearrange("p (h i) -> p h i", i=64)[:, 0:nh]
            E.tensor_tensor(dv[:, :, 0, :], e, cb, ALU.mult)
            vch(E, E.tensor_tensor(rt_, o, sb, ALU.mult))
            vch(E, E.tensor_sub(dv[:, :, 0, :], dv[:, :, 0, :], rt_))
            E.tensor_tensor(dv[:, :, 1, :], o, cb, ALU.mult)
            vch(E, E.tensor_tensor(rt_, e, sb, ALU.mult))
            return E.tensor_add(dv[:, :, 1, :], dv[:, :, 1, :], rt_)

        def compress(m, side):
            fmsrc = kfm if side == "k" else vfm
            posT = kpT if side == "k" else vpT
            b1col = kbs[:, 0:16] if side == "k" else kbs[:, 16:32]
            b2b = kbs[:, 32:33] if side == "k" else kbs[:, 33:34]
            b2w = w2k if side == "k" else w2v
            dst = ck_fm if side == "k" else cv_fm
            em.wait("vector", "MM", cs_mm_guard[0])
            def csb(E):
                last = None
                for u in range(16):
                    for hh2 in range(KVH):
                        s2 = fmsrc.ap().rearrange("p (h a b) -> p h a b",
                                                  h=KVH, b=16)[:, hh2, :, u]
                        last = E.tensor_scalar_add(
                            cs[:, u * 512 + hh2 * 64:u * 512 + (hh2 + 1) * 64], s2,
                            posT[:, hh2 * 16 + u:hh2 * 16 + u + 1])
                return last
            csv = em.emit("vector", csb, "VE", 1)
            pb2 = PSF[4]
            for half in range(2):
                for oci in range(8):
                    oc = half * 8 + oci
                    wb1 = w1b[oc % 2]
                    w1_r.acquire(oc, "gpsimd")
                    src_w = ap["w1kT" if side == "k" else "w1vT"][oc]
                    gq = em.emit("gpsimd", lambda E, wb1=wb1, s=src_w: E.dma_start(
                        wb1[:, :], s), "GQ", 16)
                    pb = PSF[2 + oc % 2]
                    em.wait("tensor", "GQ", gq)
                    em.wait("tensor", "VE", csv)
                    em.wait("tensor", "SE", mlp1_se[oc % 2] or 0)
                    for u in range(16):
                        em.emit("tensor", lambda E, pb=pb, u=u, wb1=wb1: E.matmul(
                            pb[:, :], wb1[:, u * 128:(u + 1) * 128],
                            cs[:, u * 512:(u + 1) * 512],
                            start=(u == 0), stop=(u == 15)),
                            "MM", 1 if u == 15 else None)
                    w1_r.release(oc, "MM", em.cnt["MM"])
                    em.wait("scalar", "MM", em.cnt["MM"])
                    em.wait("scalar", "MM", mlp2_mm[0])
                    se = em.emit("scalar", lambda E, oci=oci, pb=pb, b1col=b1col,
                                 oc=oc: E.activation(
                                     hid[:, oci * 512:(oci + 1) * 512], pb[:, :],
                                     AF.Relu, bias=b1col[:, oc:oc + 1]), "SE", 1)
                    mlp1_se[oc % 2] = se
                em.wait("tensor", "SE", em.cnt["SE"])
                if half == 0:
                    em.wait("tensor", "SE", ck_evac_se[0] or 0)
                for oci in range(8):
                    oc = half * 8 + oci
                    em.emit("tensor", lambda E, pb2=pb2, oci=oci, oc=oc, b2w=b2w:
                            E.matmul(pb2[:, :], b2w[:, oc * 128:(oc + 1) * 128],
                                     hid[:, oci * 512:(oci + 1) * 512],
                                     start=(oc == 0), stop=(oc == 15)),
                            "MM", 1 if oc == 15 else None)
                mlp2_mm[0] = em.cnt["MM"]
            em.wait("scalar", "MM", em.cnt["MM"])
            se = em.emit("scalar", lambda E, pb2=pb2, b2b=b2b: E.activation(
                ckst[:, :], pb2[:, :], AF.Identity, bias=b2b), "SE", 1)
            ck_evac_se[0] = se
            em.wait("vector", "SE", se)
            def scat(E, dst=dst, m=m):
                last = None
                for hh2 in range(KVH):
                    last = E.tensor_copy(
                        dst[:, hh2 * 256 + m * 64:hh2 * 256 + (m + 1) * 64],
                        ckst[:, hh2 * 64:(hh2 + 1) * 64])
                return last
            em.emit("vector", scat, "VE", 1)
            cs_mm_guard[0] = em.cnt["MM"]

        for m in range(NMEG):
            em.wait("vector", "MM", em.cnt["MM"])
            cm_ld = em.emit("sync", lambda E, m=m: E.dma_start(
                cosm.ap()[:, 0:512].rearrange("p (r i) -> p r i", i=64),
                ap["cosT"][m * MEG:(m + 1) * MEG].rearrange(
                    "(r p) i -> p r i", p=128)), "LD", 16)
            cm_ld = em.emit("sync", lambda E, m=m: E.dma_start(
                cosm.ap()[:, 512:1024].rearrange("p (r i) -> p r i", i=64),
                ap["sinT"][m * MEG:(m + 1) * MEG].rearrange(
                    "(r p) i -> p r i", p=128)), "LD", 16)
            for r in range(8):
                row = m * MEG + r * 128
                if prev_se[0] is not None:
                    em.wait("sync", "SE", prev_se[0])
                ld = em.emit("sync", lambda E, row=row: E.dma_start(
                    xin[:, :], x1d[row:row + 128, :]), "LD", 16)
                em.wait("vector", "LD", ld)
                em.wait("vector", "MM", em.cnt["MM"])
                em.wait("scalar", "MM", em.cnt["MM"])
                prev_se[0] = emit_ln(xin, hs, stat, cs.ap().bitcast(F32),
                                     prev_se[0])
                for dc in range(32):
                    emit_T(hs[:, dc * 128:(dc + 1) * 128],
                           hsT[:, dc * MEG + r * 128:dc * MEG + r * 128 + 128],
                           se_wait=prev_se[0])
            hsT_ready = em.cnt["VE"]

            for kvc in range(8):
                src_w = (ap["wkT"][kvc] if kvc < 4 else ap["wvT"][kvc - 4])
                em.wait("gpsimd", "MM", w_mm[0])
                gq = em.emit("gpsimd", lambda E, s=src_w: E.dma_start(
                    wcb[:, :], s), "GQ", 16)
                em.wait("tensor", "GQ", gq)
                em.wait("tensor", "VE", hsT_ready)
                for r in range(8):
                    pb = PSF[r % 2]
                    ktt = ktb[r % 2]
                    em.wait("tensor", "VE", kv_evac[r % 2] or 0)
                    for dc in range(32):
                        em.emit("tensor", lambda E, pb=pb, dc=dc, r=r: E.matmul(
                            pb[:, 0:256],
                            hsT[:, dc * MEG + r * 128:dc * MEG + r * 128 + 128],
                            wcb[:, dc * 256:(dc + 1) * 256],
                            start=(dc == 0), stop=(dc == 31)),
                            "MM", 1 if dc == 31 else None)
                    w_mm[0] = em.cnt["MM"]
                    em.wait("vector", "MM", em.cnt["MM"])
                    em.wait("vector", "MM", kt_mm[r % 2])
                    em.wait("vector", "LD", ktt_ld[r % 2])
                    em.wait("vector", "LD", cm_ld)
                    ve = em.emit("vector", lambda E, pb=pb, ktt=ktt:
                                 E.tensor_copy(ktt[:, :], pb[:, 0:256]), "VE", 1)
                    kv_evac[r % 2] = ve
                    h0 = 2 * kvc if kvc < 4 else 2 * (kvc - 4)
                    if kvc < 4:
                        em.wait("vector", "VE", ve)
                        em.wait("vector", "MM", rkr_mm[0])
                        ve2 = em.emit("vector", lambda E, ktt=ktt, r=r:
                                      rope_tile(E, rkr, ktt, 2,
                                                cosm.ap()[:, r * 64:r * 64 + 64],
                                                cosm.ap()[:, 512 + r * 64:
                                                          512 + r * 64 + 64]),
                                      "VE", 1)
                        em.wait("tensor", "VE", ve)
                        for j in range(2):
                            emit_T(ktt[:, j * 128:(j + 1) * 128],
                                   kfm[:, (h0 + j) * MEG + r * 128:
                                       (h0 + j) * MEG + r * 128 + 128])
                        em.wait("tensor", "VE", ve2)
                        em.wait("vector", "LD", stg_ld[0])
                        for j in range(2):
                            lastve = emit_T(rkr[:, j * 128:(j + 1) * 128],
                                            stg[:, j * 128:(j + 1) * 128])
                        rkr_mm[0] = em.cnt["MM"]
                        em.wait("sync", "VE", lastve)
                        st = em.emit("sync", lambda E, h0=h0, m=m, r=r: E.dma_start(
                            rkd[h0:h0 + 2, :, m * MEG + r * 128:
                                m * MEG + r * 128 + 128].rearrange(
                                    "h p c -> p h c"),
                            stg.ap().rearrange("p (h c) -> p h c", c=128)), "LD", 16)
                        stg_ld[0] = st
                        kt_mm[r % 2] = em.cnt["MM"]
                    else:
                        em.wait("sync", "VE", ve)
                        st = em.emit("sync", lambda E, h0=h0, m=m, r=r, ktt=ktt:
                                     E.dma_start(
                                         vtokd[h0:h0 + 2,
                                               m * MEG + r * 128:m * MEG + r * 128 + 128,
                                               :].rearrange("h p d -> p h d"),
                                         ktt.ap().rearrange("p (h d) -> p h d",
                                                            d=128)), "LD", 16)
                        em.wait("tensor", "VE", ve)
                        for j in range(2):
                            emit_T(ktt[:, j * 128:(j + 1) * 128],
                                   vfm[:, (h0 + j) * MEG + r * 128:
                                       (h0 + j) * MEG + r * 128 + 128])
                        kt_mm[r % 2] = em.cnt["MM"]
                        ktt_ld[r % 2] = st
            compress(m, "k")
            compress(m, "v")
            for qc in range(16):
                src_w = ap["wqT"][qc]
                em.wait("gpsimd", "MM", w_mm[0])
                gq = em.emit("gpsimd", lambda E, s=src_w: E.dma_start(
                    wcb[:, :], s), "GQ", 16)
                em.wait("tensor", "GQ", gq)
                for a in range(4):
                    r = 2 * a
                    tloc = m * 4 + a
                    pb = PSF[a % 2]
                    ktt = ktb[a % 2]
                    em.wait("tensor", "VE", kv_evac[a % 2] or 0)
                    for dc in range(32):
                        em.emit("tensor", lambda E, pb=pb, dc=dc, r=r: E.matmul(
                            pb[:, 0:256],
                            hsT[:, dc * MEG + r * 128:dc * MEG + r * 128 + 128],
                            wcb[:, dc * 256:(dc + 1) * 256],
                            start=(dc == 0), stop=(dc == 31)),
                            "MM", 1 if dc == 31 else None)
                    w_mm[0] = em.cnt["MM"]
                    em.wait("vector", "MM", em.cnt["MM"])
                    em.wait("vector", "MM", kt_mm[a % 2])
                    em.wait("vector", "LD", ktt_ld[a % 2])
                    ve = em.emit("vector", lambda E, pb=pb, ktt=ktt:
                                 E.tensor_copy(ktt[:, :], pb[:, 0:256]), "VE", 1)
                    kv_evac[a % 2] = ve
                    em.wait("vector", "VE", ve)
                    em.wait("vector", "MM", rkr_mm[0])
                    ve2 = em.emit("vector", lambda E, ktt=ktt, r=r:
                                  rope_tile(E, rkr, ktt, 2,
                                            cosm.ap()[:, r * 64:r * 64 + 64],
                                            cosm.ap()[:, 512 + r * 64:
                                                      512 + r * 64 + 64]),
                                  "VE", 1)
                    em.wait("tensor", "VE", ve2)
                    em.wait("vector", "LD", stg_ld[0])
                    for j in range(2):
                        lastve = emit_T(rkr[:, j * 128:(j + 1) * 128],
                                        stg[:, j * 128:(j + 1) * 128])
                    rkr_mm[0] = em.cnt["MM"]
                    em.wait("sync", "VE", lastve)
                    st = em.emit("sync", lambda E, qc=qc, tloc=tloc: E.dma_start(
                        qrod[2 * qc:2 * qc + 2, :,
                             tloc * 128:(tloc + 1) * 128].rearrange(
                                 "h p c -> p h c"),
                        stg.ap().rearrange("p (h c) -> p h c", c=128)), "LD", 16)
                    stg_ld[0] = st
                    em.wait("tensor", "VE", ve)
                    em.wait("vector", "LD", st)
                    for j in range(2):
                        lastve = emit_T(ktt[:, j * 128:(j + 1) * 128],
                                        stg[:, j * 128:(j + 1) * 128])
                    em.wait("sync", "VE", lastve)
                    st = em.emit("sync", lambda E, qc=qc, tloc=tloc: E.dma_start(
                        qnod[2 * qc:2 * qc + 2, :,
                             tloc * 128:(tloc + 1) * 128].rearrange(
                                 "h p c -> p h c"),
                        stg.ap().rearrange("p (h c) -> p h c", c=128)), "LD", 16)
                    stg_ld[0] = st
                    kt_mm[a % 2] = em.cnt["MM"]
            for a in range(4):
                tloc = m * 4 + a
                pb = PSF[4]
                em.wait("tensor", "VE", gse_[0])
                em.wait("tensor", "SE", ck_evac_se[0] or 0)
                for dc in range(32):
                    em.emit("tensor", lambda E, pb=pb, dc=dc, a=a: E.matmul(
                        pb[:, 0:96],
                        hsT[:, dc * MEG + 2 * a * 128:dc * MEG + 2 * a * 128 + 128],
                        wgTb[:, dc * 96:(dc + 1) * 96],
                        start=(dc == 0), stop=False), "MM", None)
                em.emit("tensor", lambda E, pb=pb: E.matmul(
                    pb[:, 0:96], onesrow[0:1, :], bgateb[0:1, :],
                    start=False, stop=True), "MM", 1)
                em.wait("scalar", "MM", em.cnt["MM"])
                em.wait("scalar", "VE", gse_[1])
                se = em.emit("scalar", lambda E, pb=pb: E.activation(
                    gt[:, :], pb[:, 0:96], AF.Exp, scale=-1.0), "SE", 1)
                em.wait("vector", "SE", se)
                def sig(E, tloc=tloc):
                    vch(E, E.tensor_scalar_add(gt[:, :], gt[:, :], 1.0))
                    return E.reciprocal(gates[:, tloc * 96:(tloc + 1) * 96], gt[:, :])
                gse_[1] = em.emit("vector", sig, "VE", 1)
                gse_[0] = gse_[1]
        em.barrier()
        if d_rk is not None:
            em.emit("sync", lambda E: E.dma_start(d_rk, rkd), "LD", 16)
        if d_vtok is not None:
            em.emit("sync", lambda E: E.dma_start(d_vtok, vtokd), "LD", 16)
        if d_ck is not None:
            em.emit("sync", lambda E: E.dma_start(d_ck, ck_fm[:, :]), "LD", 16)
        if d_cv is not None:
            em.emit("sync", lambda E: E.dma_start(d_cv, cv_fm[:, :]), "LD", 16)
        if d_q is not None:
            em.emit("sync", lambda E: E.dma_start(d_q, qrod), "LD", 16)
        if d_gates is not None:
            em.emit("sync", lambda E: E.dma_start(d_gates, gates[:, :]), "LD", 16)
        em.barrier()
        for c in reversed(cm):
            c.__exit__(None, None, None)

    kv_evac = [None, None]
    rtmp = nc.alloc_sbuf_tensor("c_rtmp", [128, 128], BF16, side="right")
    phase2()
    if upto < 3:
        em.wait("sync", "LD", em.cnt["LD"])
        em.finalize()
        return nc, dbg_out

    mixcm = nc.sbuf_tensor("mix", [128, NT * D], BF16)
    mix = mixcm.__enter__()

    def phase3():
        cm = [
            nc.sbuf_tensor("a_rk", [128, S], BF16),
            nc.sbuf_tensor("a_vaug", [128, 32 * 132], BF16),
            nc.sbuf_tensor("a_cvaug", [128, 2 * 132], BF16),
            nc.sbuf_tensor("a_rq", [128, G * 256], BF16),
            nc.sbuf_tensor("a_rqu", [128, G * 256], BF16),
            nc.sbuf_tensor("a_pr0", [128, 256], BF16),
            nc.sbuf_tensor("a_pr1", [128, 256], BF16),
            nc.sbuf_tensor("a_pmem", [1, 256], BF16),
            nc.sbuf_tensor("a_pfin", [128, 256], BF16),
            nc.sbuf_tensor("a_psli", [128, 256], BF16),
            nc.sbuf_tensor("a_maskc", [128, 32 * 256], BF16),
            nc.sbuf_tensor("a_impac", [128, 2 * 256], F32),
            nc.sbuf_tensor("a_impq", [128, 512], F32),
            nc.sbuf_tensor("a_sele", [128, 256], F32),
            nc.sbuf_tensor("a_selw", [128, 256], F32),
            nc.sbuf_tensor("a_selm", [128, 8], F32),
            nc.sbuf_tensor("a_selq", [128, 512], BF16),
            nc.sbuf_tensor("a_mfT", [128, 512], BF16),
            nc.sbuf_tensor("a_nrm", [128, 8], F32),
            nc.sbuf_tensor("a_mixt", [128, 256], F32),
            nc.sbuf_tensor("a_cmq", [128, 512], BF16),
            nc.sbuf_tensor("a_cnq", [128, 512], BF16),
            nc.sbuf_tensor("a_sm", [128, 4 * 128], BF16),
            nc.sbuf_tensor("a_csimtap", [128, 256], F32),
        ]
        t = [c.__enter__() for c in cm]
        (rk, vaug, cvaug, rq, rqu, pr0, pr1, pmem, pfin, psli, maskc, impac, impq,
         sele, selw, selm, selq, mfT, nrm, mixt, cmq, cnq, sm, csimtap) = t
        probs = [pr0, pr1]

        em.emit("sync", lambda E: E.dma_start(sm[:, 0:128], ap["smA"]), "LD", 16)
        em.emit("sync", lambda E: E.dma_start(sm[:, 128:256], ap["smB"]), "LD", 16)
        em.emit("sync", lambda E: E.dma_start(sm[:, 256:384], ap["smC"]), "LD", 16)
        em.emit("sync", lambda E: E.dma_start(sm[:, 384:512], ap["fmdiag"]), "LD", 16)
        def zmix(E):
            last = None
            for z in range(4):
                last = E.memset(mix[:, z * 16384:(z + 1) * 16384], 0.0)
            return last
        em.emit("gpsimd", zmix, "PO", 1)
        em.barrier()

        for hh in range(KVH):
            em.wait("sync", "MM", em.cnt["MM"])
            em.wait("sync", "SE", em.cnt["SE"])
            ldk = em.emit("sync", lambda E, hh=hh: E.dma_start(rk[:, :], rkd[hh]),
                          "LD", 16)
            ldv = em.emit("sync", lambda E, hh=hh: E.dma_start(
                vaug.ap().rearrange("p (c w) -> p c w", w=132)[:, :, 0:128],
                vtokd[hh].rearrange("(c p) q -> p c q", p=128)), "LD", 16)
            em.wait("gpsimd", "LD", ldv)
            def vones(E):
                for c in range(32):
                    E.memset(vaug[:, c * 132 + 128:c * 132 + 129], 1.0)
                E.memset(cvaug[:, 128:129], 1.0)
                return E.memset(cvaug[:, 132 + 128:132 + 129], 1.0)
            pov = em.emit("gpsimd", vones, "PO", 1)
            for sec in range(2):
                emit_T(cv_fm[:, hh * 256 + sec * 128:hh * 256 + (sec + 1) * 128],
                       cvaug[:, sec * 132:sec * 132 + 128])
            cv_ready = em.cnt["VE"]
            em.wait("tensor", "LD", ldk)
            em.wait("tensor", "LD", ldv)
            em.wait("tensor", "PO", em.cnt["PO"])
            em.wait("tensor", "VE", cv_ready)

            for qt in range(8):
                j0, j1 = 2 * qt, 2 * qt + 1
                qcol = qt * 256
                for g in range(G):
                    head = hh * G + g
                    em.wait("sync", "MM", rq_guard[0])
                    em.emit("sync", lambda E, head=head, g=g, qcol=qcol:
                            E.dma_start(rq[:, g * 256:(g + 1) * 256],
                                        qrod[head][:, qcol:qcol + 256]), "LD", 16)
                    em.emit("sync", lambda E, head=head, g=g, qcol=qcol:
                            E.dma_start(rqu[:, g * 256:(g + 1) * 256],
                                        qnod[head][:, qcol:qcol + 256]), "LD", 16)
                em.wait("sync", "VE", cmq_ve[0])
                for sec in range(2):
                    em.emit("sync", lambda E, sec=sec, qcol=qcol: E.dma_start(
                        cmq[:, sec * 256:(sec + 1) * 256],
                        ap["cmask"][1 + sec * 128:1 + (sec + 1) * 128,
                                    qcol:qcol + 256]), "LD", 16)
                    em.emit("sync", lambda E, sec=sec, qcol=qcol: E.dma_start(
                        cnq[:, sec * 256:(sec + 1) * 256],
                        ap["cneg"][sec * 128:(sec + 1) * 128, qcol:qcol + 256]),
                        "LD", 16)
                rq_ld = em.cnt["LD"]
                em.wait("tensor", "LD", rq_ld)
                em.wait("vector", "LD", rq_ld)

                qk_se = [0, 0]
                qk_ve = [0, 0]
                for g in range(G):
                    pvmm = 0
                    for sec in range(2):
                        pb = PSF[sec]
                        em.wait("tensor", "SE", qk_se[sec])
                        em.wait("tensor", "VE", qk_ve[sec])
                        em.emit("tensor", lambda E, pb=pb, hh=hh, sec=sec, g=g:
                                E.matmul(pb[:, 0:256],
                                         ck_fm[:, hh * 256 + sec * 128:
                                               hh * 256 + (sec + 1) * 128],
                                         rqu[:, g * 256:(g + 1) * 256],
                                         start=True, stop=True), "MM", 1)
                        mmv = em.cnt["MM"]
                        em.wait("scalar", "MM", mmv)
                        em.wait("scalar", "VE", prb_ve[sec])
                        se = em.emit("scalar", lambda E, pb=pb, sec=sec: E.activation(
                            probs[sec][:, :], pb[:, 0:256], AF.Exp, scale=SCALE,
                            bias=eshv[:, :]), "SE", 1)
                        qk_se[sec] = se
                        em.wait("vector", "MM", mmv)
                        em.wait("vector", "SE", se)
                        if dbg_out.get("csim") is not None and hh == 0 and qt == 7:
                            em.wait("vector", "LD", em.cnt["LD"])
                            vv = em.emit("vector", lambda E, pb=pb, g=g, sec=sec:
                                         E.tensor_copy(csimtap[:, :], pb[:, 0:256]),
                                         "VE", 1)
                            em.wait("sync", "VE", vv)
                            em.emit("sync", lambda E, g=g, sec=sec: E.dma_start(
                                dbg_out["csim"][g, sec], csimtap[:, :]), "LD", 16)
                        em.wait("vector", "VE", impac_ve[sec])
                        if g == 0:
                            impac_ve[sec] = em.emit(
                                "vector", lambda E, pb=pb, sec=sec: E.tensor_copy(
                                    impac[:, sec * 256:(sec + 1) * 256], pb[:, 0:256]),
                                "VE", 1)
                        else:
                            impac_ve[sec] = em.emit(
                                "vector", lambda E, pb=pb, sec=sec: E.tensor_add(
                                    impac[:, sec * 256:(sec + 1) * 256],
                                    impac[:, sec * 256:(sec + 1) * 256], pb[:, 0:256]),
                                "VE", 1)
                        ve = em.emit("vector", lambda E, sec=sec:
                                     E.tensor_mul(probs[sec][:, :], probs[sec][:, :],
                                                  cmq[:, sec * 256:(sec + 1) * 256]),
                                     "VE", 1)
                        qk_ve[sec] = ve
                        prb_ve[sec] = ve
                    pbm = PSF[2]
                    em.wait("tensor", "SE", memse[0])
                    em.emit("tensor", lambda E, pbm=pbm, hh=hh, g=g: E.matmul(
                        pbm[0:1, 0:256], memkb[:, hh:hh + 1],
                        rqu[:, g * 256:(g + 1) * 256], start=True, stop=True), "MM", 1)
                    em.wait("scalar", "MM", em.cnt["MM"])
                    memse[0] = em.emit("scalar", lambda E, pbm=pbm: E.activation(
                        pmem[0:1, :], pbm[0:1, 0:256], AF.Exp, scale=SCALE,
                        bias=eshv[0:1, :]), "SE", 1)
                    em.wait("tensor", "VE", max(qk_ve))
                    em.wait("tensor", "SE", memse[0])
                    em.wait("tensor", "VE", cpv_ve[0])
                    for sj in range(2):
                        pbo = PSF[3 + sj]
                        for sec in range(2):
                            em.emit("tensor", lambda E, pbo=pbo, sec=sec, sj=sj:
                                    E.matmul(pbo[:, 0:129],
                                             probs[sec][:, sj * 128:(sj + 1) * 128],
                                             cvaug[:, sec * 132:sec * 132 + 129],
                                             start=(sec == 0), stop=False), "MM", None)
                        em.emit("tensor", lambda E, pbo=pbo, sj=sj, hh=hh: E.matmul(
                            pbo[:, 0:129], pmem[0:1, sj * 128:(sj + 1) * 128],
                            memvaug[0:1, hh * 132:hh * 132 + 129],
                            start=False, stop=True), "MM", 1 if sj == 1 else None)
                    mmv = em.cnt["MM"]
                    em.wait("vector", "MM", mmv)
                    for sj, sub in enumerate((j0, j1)):
                        pbo = PSF[3 + sj]
                        head = hh * G + g
                        def cnorm(E, pbo=pbo, head=head, sub=sub):
                            vch(E, E.reciprocal(nrm[:, 0:1], pbo[:, 128:129]))
                            vch(E, E.tensor_mul(nrm[:, 1:2], nrm[:, 0:1],
                                                gates[:, sub * 96 + 3 * head:
                                                      sub * 96 + 3 * head + 1]))
                            mcol = sub * D + head * 128
                            return E.tensor_scalar(mix[:, mcol:mcol + 128],
                                                   pbo[:, 0:128],
                                                   nrm[:, 1:2], None, ALU.mult)
                        ve = em.emit("vector", cnorm, "VE", 1)
                    cpv_ve[0] = em.cnt["VE"]

                for sec in range(2):
                    em.wait("vector", "VE", impac_ve[sec])
                    em.emit("vector", lambda E, sec=sec:
                            E.scalar_tensor_tensor(
                                impac[:, sec * 256:(sec + 1) * 256],
                                impac[:, sec * 256:(sec + 1) * 256], SCALE / G,
                                cnq[:, sec * 256:(sec + 1) * 256], ALU.mult,
                                ALU.add), "VE", 1)
                selv = em.cnt["VE"]
                cmq_ve[0] = selv
                em.wait("tensor", "VE", selv)
                for sj in range(2):
                    for sec in range(2):
                        emit_T(impac[:, sec * 256 + sj * 128:sec * 256 + sj * 128 + 128],
                               impq[:, sj * 256 + sec * 128:sj * 256 + sec * 128 + 128],
                               f32=True)
                if dbg_out.get("impq") is not None:
                    em.wait("sync", "VE", em.cnt["VE"])
                    em.emit("sync", lambda E, hh=hh, qt=qt: E.dma_start(
                        dbg_out["impq"][hh, qt], impq[:, :]), "LD", 16)
                for sj in range(2):
                    iq = impq[:, sj * 256:(sj + 1) * 256]
                    em.wait("vector", "VE", em.cnt["VE"])
                    def topk(E, iq=iq):
                        vch(E, E.tensor_copy(sele[:, :], iq))
                        for it in range(4):
                            vch(E, E.reduce_max(selm[:, it:it + 1], sele[:, :],
                                                axis=AX.X))
                            if it < 3:
                                vch(E, E.tensor_scalar(selw[:, :], sele[:, :],
                                                       selm[:, it:it + 1], None,
                                                       ALU.is_equal))
                                vch(E, E.scalar_tensor_tensor(
                                    sele[:, :], selw[:, :], NEGB,
                                    sele[:, :], ALU.mult, ALU.add))
                        return E.engine_nop()
                    ve = em.emit("vector", topk, "VE", 1)
                    em.wait("scalar", "VE", ve)
                    se = em.emit("scalar", lambda E, iq=iq: E.activation(
                        sele[:, :], iq, AF.Exp, bias=eshv[:, :],
                        accum_out=selm[:, 4:5]), "SE", 1)
                    em.wait("vector", "SE", se)
                    def fmask(E, iq=iq, sj=sj):
                        vch(E, E.tensor_scalar_mul(selm[:, 5:6], selm[:, 4:5], 1e-10))
                        vch(E, E.tensor_scalar(selw[:, :], sele[:, :], selm[:, 5:6],
                                               None, ALU.is_gt))
                        vch(E, E.tensor_scalar(selq[:, sj * 256:(sj + 1) * 256], iq,
                                               selm[:, 3:4], None, ALU.is_ge))
                        return E.tensor_mul(selq[:, sj * 256:(sj + 1) * 256],
                                            selq[:, sj * 256:(sj + 1) * 256],
                                            selw[:, :])
                    em.emit("vector", fmask, "VE", 1)
                em.wait("tensor", "VE", em.cnt["VE"])
                for sec in range(2):
                    for sj in range(2):
                        emit_T(selq[:, sj * 256 + sec * 128:sj * 256 + sec * 128 + 128],
                               mfT[:, sec * 256 + sj * 128:sec * 256 + sj * 128 + 128])
                em.wait("sync", "VE", em.cnt["VE"])
                stm = em.emit("sync", lambda E, hh=hh, qcol=qcol: E.dma_start(
                    maskfd[hh][:, qcol:qcol + 256].rearrange(
                        "(s p) q -> p s q", s=2),
                    mfT.ap().rearrange("p (s q) -> p s q", s=2)), "LD", 16)

                ncH = 4 * qt + 4
                em.wait("sync", "LD", stm)
                em.wait("sync", "VE", maskc_ve[0])
                for bb in range(8):
                    src = maskfd[hh].rearrange("(c b) q -> b c q", b=8)[bb, 0:ncH]
                    src = src.unsqueeze(0).broadcast_to([16, ncH, OWN])[
                        :, :, qcol:qcol + 256]
                    em.emit("sync", lambda E, src=src, bb=bb, ncH=ncH: E.dma_start(
                        maskc[16 * bb:16 * (bb + 1), :].rearrange(
                            "p (c q) -> p c q", q=256)[:, 0:ncH], src), "LD", 16)
                ldm = em.cnt["LD"]
                em.wait("vector", "LD", ldm)
                def adddiag(E, qt=qt):
                    for sj in range(2):
                        c = 4 * qt + 2 * sj
                        E.tensor_add(
                            maskc[:, c * 256 + sj * 128:c * 256 + sj * 128 + 128],
                            maskc[:, c * 256 + sj * 128:c * 256 + sj * 128 + 128],
                            sm[:, 384:512])
                    return E.engine_nop()
                dgv = em.emit("vector", adddiag, "VE", 1)

                for g in range(G):
                    fsmm = [0, 0]
                    for c in range(ncH):
                        qlo = 0 if c <= 4 * qt + 1 else 1
                        pb = PSF[c % 2]
                        em.wait("tensor", "SE", qk_se[c % 2])
                        em.emit("tensor", lambda E, pb=pb, c=c, g=g, qlo=qlo:
                                E.matmul(pb[:, 0:256 - qlo * 128],
                                         rk[:, c * 128:(c + 1) * 128],
                                         rq[:, g * 256 + qlo * 128:(g + 1) * 256],
                                         start=True, stop=True), "MM", 1)
                        mmv = em.cnt["MM"]
                        em.wait("scalar", "MM", mmv)
                        em.wait("scalar", "VE", prb_ve[c % 2])
                        prb = probs[c % 2]
                        se = em.emit("scalar", lambda E, pb=pb, prb=prb, qlo=qlo:
                                     E.activation(prb[:, qlo * 128:256],
                                                  pb[:, 0:256 - qlo * 128], AF.Exp,
                                                  scale=SCALE, bias=eshv[:, :]), "SE", 1)
                        qk_se[c % 2] = se
                        em.wait("vector", "SE", se)
                        em.wait("vector", "VE", dgv)
                        em.wait("vector", "MM", fsmm[0])
                        ve = em.emit("vector", lambda E, prb=prb, c=c, qlo=qlo:
                                     E.tensor_mul(pfin[:, qlo * 128:256],
                                                  prb[:, qlo * 128:256],
                                                  maskc[:, c * 256 + qlo * 128:
                                                        (c + 1) * 256]), "VE", 1)
                        prb_ve[c % 2] = ve
                        svs = []
                        for sj in range(2):
                            base = 4 * qt + 2 * sj
                            which = None
                            if c == base - 1:
                                which = sm[:, 0:128]
                            elif c == base:
                                which = sm[:, 128:256]
                            elif c == base + 1:
                                which = sm[:, 256:384]
                            if which is not None:
                                sv = em.emit("vector",
                                             lambda E, prb=prb, sj=sj, w=which:
                                             E.tensor_mul(
                                                 psli[:, sj * 128:(sj + 1) * 128],
                                                 prb[:, sj * 128:(sj + 1) * 128],
                                                 w), "VE", 1)
                                prb_ve[c % 2] = sv
                                svs.append((sj, sv))
                        em.wait("tensor", "VE", em.cnt["VE"])
                        for sj in range(2):
                            if qlo <= sj and c <= 4 * qt + 2 * sj + 1:
                                pbo = PSF[2 + sj]
                                em.emit("tensor", lambda E, pbo=pbo, sj=sj, c=c:
                                        E.matmul(pbo[:, 0:129],
                                                 pfin[:, sj * 128:(sj + 1) * 128],
                                                 vaug[:, c * 132:c * 132 + 129],
                                                 start=(c == 0), stop=False,
                                                 skip_group_check=True), "MM", None)
                        for sj, sv in svs:
                            pbo = PSF[2 + sj]
                            em.emit("tensor", lambda E, pbo=pbo, sj=sj, c=c:
                                    E.matmul(pbo[:, 132:261],
                                             psli[:, sj * 128:(sj + 1) * 128],
                                             vaug[:, c * 132:c * 132 + 129],
                                             start=False, stop=False,
                                             skip_group_check=True), "MM", None)
                        fsmm[0] = em.cnt["MM"] + 1
                        em.emit("tensor", lambda E: E.nop(), "MM", 1)
                    mmv = em.cnt["MM"]
                    em.wait("vector", "MM", mmv)
                    for sj, sub in enumerate((j0, j1)):
                        pbo = PSF[2 + sj]
                        head = hh * G + g
                        def fsev(E, pbo=pbo, sub=sub, head=head):
                            gb = sub * 96 + 3 * head
                            vch(E, E.reciprocal(nrm[:, 0:1], pbo[:, 128:129]))
                            vch(E, E.tensor_mul(nrm[:, 1:2], nrm[:, 0:1],
                                                gates[:, gb + 1:gb + 2]))
                            vch(E, E.reciprocal(nrm[:, 2:3], pbo[:, 260:261]))
                            vch(E, E.tensor_mul(nrm[:, 3:4], nrm[:, 2:3],
                                                gates[:, gb + 2:gb + 3]))
                            vch(E, E.tensor_scalar(mixt[:, 0:128], pbo[:, 0:128],
                                                   nrm[:, 1:2], None, ALU.mult))
                            vch(E, E.tensor_scalar(mixt[:, 128:256], pbo[:, 132:260],
                                                   nrm[:, 3:4], None, ALU.mult))
                            vch(E, E.tensor_add(mixt[:, 0:128], mixt[:, 0:128],
                                                mixt[:, 128:256]))
                            mcol = sub * D + head * 128
                            return E.tensor_add(mix[:, mcol:mcol + 128],
                                                mix[:, mcol:mcol + 128],
                                                mixt[:, 0:128])
                        ve = em.emit("vector", fsev, "VE", 1)
                    em.wait("tensor", "VE", em.cnt["VE"])
                maskc_ve[0] = em.cnt["VE"]
                rq_guard[0] = em.cnt["MM"]
        em.barrier()
        if d_maskf is not None:
            em.emit("sync", lambda E: E.dma_start(d_maskf, maskfd), "LD", 16)
        if d_mix is not None:
            em.emit("sync", lambda E: E.dma_start(d_mix, mix[:, :]), "LD", 16)
        em.barrier()
        for c in reversed(cm):
            c.__exit__(None, None, None)

    rq_guard = [0]
    cmq_ve = [0]
    impac_ve = [0, 0]
    prb_ve = [0, 0]
    memse = [0]
    cpv_ve = [0]
    maskc_ve = [0]
    phase3()
    if upto < 4:
        em.wait("sync", "LD", em.cnt["LD"])
        em.finalize()
        return nc, dbg_out

    def phase4():
        cm = [
            nc.sbuf_tensor("p4_mixT", [128, 32 * 128], BF16),
            nc.sbuf_tensor("p4_w0", [128, 32 * 256], BF16),
            nc.sbuf_tensor("p4_w1", [128, 32 * 256], BF16),
            nc.sbuf_tensor("p4_xr0", [128, 256], F32),
            nc.sbuf_tensor("p4_xr1", [128, 256], F32),
            nc.sbuf_tensor("p4_xr2", [128, 256], F32),
            nc.sbuf_tensor("p4_xr3", [128, 256], F32),
            nc.sbuf_tensor("p4_x0", [128, 256], F32),
            nc.sbuf_tensor("p4_x1", [128, 256], F32),
            nc.sbuf_tensor("p4_x2", [128, 256], F32),
            nc.sbuf_tensor("p4_x3", [128, 256], F32),
        ]
        t = [c.__enter__() for c in cm]
        mixT, w0, w1, xr0, xr1, xr2, xr3, s0, s1, s2, s3 = t
        wcol = [w0, w1]; xr = [xr0, xr1, xr2, xr3]; x1s = [s0, s1, s2, s3]
        w_r = Rot(em, 2); xr_r = Rot(em, 4); x1_r = Rot(em, 4)
        for r in range(NT):
            em.wait("vector", "MM", mixT_guard[0])
            lastve = None
            for dc in range(32):
                lastve = emit_T(mix[:, r * D + dc * 128:r * D + dc * 128 + 128],
                                mixT[:, dc * 128:(dc + 1) * 128])
            mixT_ready = em.cnt["VE"]
            grow = 2 * r * 128
            for oc in range(16):
                i = r * 16 + oc
                wb = wcol[oc % 2]
                w_r.acquire(i, "gpsimd")
                src_w = ap["woT"][oc]
                gq = em.emit("gpsimd", lambda E, wb=wb, s=src_w: E.dma_start(
                    wb[:, :], s), "GQ", 16)
                pb = PSF[oc % 4]
                xr_r.acquire(i, "sync")
                ldr = em.emit("sync", lambda E, b=xr[oc % 4], grow=grow, oc=oc:
                              E.dma_start(b[:, :], x1d[grow:grow + 128,
                                                       oc * 256:(oc + 1) * 256]),
                              "LD", 16)
                em.wait("tensor", "GQ", gq)
                em.wait("tensor", "VE", mixT_ready)
                xr_r.acquire(i, "tensor")
                for dc in range(32):
                    em.emit("tensor", lambda E, pb=pb, dc=dc, wb=wb: E.matmul(
                        pb[:, 0:256], mixT[:, dc * 128:(dc + 1) * 128],
                        wb[:, dc * 256:(dc + 1) * 256],
                        start=(dc == 0), stop=(dc == 31)), "MM", 1 if dc == 31 else None)
                w_r.release(i, "MM", em.cnt["MM"])
                em.wait("vector", "MM", em.cnt["MM"])
                em.wait("vector", "LD", ldr)
                x1_r.acquire(i, "vector")
                ve = em.emit("vector", lambda E, d=x1s[oc % 4], pb=pb, b=xr[oc % 4]:
                             E.scalar_tensor_tensor(d[:, :], pb[:, 0:256], 1.0,
                                                    b[:, :], ALU.mult, ALU.add),
                             "VE", 1)
                xr_r.release(i, "VE", ve)
                em.wait("sync", "VE", ve)
                st = em.emit("sync", lambda E, d=x1s[oc % 4], r=r, oc=oc: E.dma_start(
                    x2d[r * 128:(r + 1) * 128, oc * 256:(oc + 1) * 256], d[:, :]),
                    "LD", 16)
                if d_x2 is not None:
                    st = em.emit("sync", lambda E, d=x1s[oc % 4], r=r, oc=oc:
                                 E.dma_start(d_x2[r * 128:(r + 1) * 128,
                                                  oc * 256:(oc + 1) * 256], d[:, :]),
                                 "LD", 16)
                x1_r.release(i, "LD", st)
            mixT_guard[0] = em.cnt["MM"]
        em.barrier()
        for c in reversed(cm):
            c.__exit__(None, None, None)

    mixT_guard = [0]
    phase4()
    mixcm.__exit__(None, None, None)
    if upto < 5:
        em.wait("sync", "LD", em.cnt["LD"])
        em.finalize()
        return nc, dbg_out

    phase_ffn(x2d, out, 2, tag="f5")
    em.wait("sync", "LD", em.cnt["LD"])
    em.finalize()
    return nc, dbg_out


def build_once(dbg=()):
    return build_program(dbg)


def kernel(**inputs):
    shared, per_core = prep_inputs(inputs)
    nc, _ = build_program()
    in_maps = []
    for c in range(8):
        m = dict(shared)
        m.update(per_core[c])
        in_maps.append(m)
    res = run_bass_kernel_spmd(nc, in_maps, core_ids=list(range(8)))
    outp = np.empty((B, S, D), dtype=np.float32)
    for c in range(8):
        b, hh = c // 2, c % 2
        o = np.asarray(res.results[c]["out"], dtype=np.float32)
        tb = (np.arange(NBLK) if hh == 0
              else np.arange(NBLK).reshape(-1, 2)[:, ::-1].reshape(-1))
        own_true = tb[0::2]
        outp[b].reshape(NBLK, 128, D)[own_true] = o.reshape(NT, 128, D)
    return outp


# version 04
# speedup vs baseline: 1.2212x; 1.2212x over previous
import numpy as np
import ml_dtypes
import concourse.bass as bass
import concourse.mybir as mybir
from concourse.bass_utils import run_bass_kernel_spmd

F32 = mybir.dt.float32
BF16 = mybir.dt.bfloat16
I32 = mybir.dt.int32
AF = mybir.ActivationFunctionType
ALU = mybir.AluOpType
AX = mybir.AxisListType

B, S, D = 4, 4096, 4096
H, KVH, G, DH = 32, 8, 4, 128
NBLK = S // 128
OWN = S // 2
NT = OWN // 128
NMEG, MEG = 4, 1024
SCALE = DH ** -0.5
ESH = 30.0
NEGB = -1.0e9
EPS = 1e-5

PI_POS = np.array([j // 2 if j % 2 == 0 else 64 + j // 2 for j in range(DH)])
PI_INV = np.argsort(PI_POS)


def prep_inputs(inp):
    w_qkv = np.asarray(inp["L0_attn_w_qkv"]); w_o = np.asarray(inp["L0_attn_w_o"])
    wffn = np.asarray(inp["L0_ffn_weight"])
    k_w1 = np.asarray(inp["L0_attn_k_w1"]); k_b1 = np.asarray(inp["L0_attn_k_b1"])
    k_w2 = np.asarray(inp["L0_attn_k_w2"]); k_b2 = np.asarray(inp["L0_attn_k_b2"])
    v_w1 = np.asarray(inp["L0_attn_v_w1"]); v_b1 = np.asarray(inp["L0_attn_v_b1"])
    v_w2 = np.asarray(inp["L0_attn_v_w2"]); v_b2 = np.asarray(inp["L0_attn_v_b2"])
    k_pos = np.asarray(inp["L0_attn_k_pos"]); v_pos = np.asarray(inp["L0_attn_v_pos"])
    mem_kv = np.asarray(inp["L0_attn_mem_kv"])
    w_gate = np.asarray(inp["L0_attn_w_gate"]); b_gate = np.asarray(inp["L0_attn_b_gate"])
    x = np.asarray(inp["L0_attn_hidden_states"])

    bf = ml_dtypes.bfloat16
    C = np.ascontiguousarray

    def tile_oc(wT, ocw):
        Din, O = wT.shape
        return C(wT.reshape(Din // 128, 128, O // ocw, ocw)
                 .transpose(2, 1, 0, 3).reshape(O // ocw, 128, -1))

    def tile_flat(wT, ocw):
        Din, O = wT.shape
        return C(wT.reshape(Din // 128, 128, O).transpose(1, 0, 2)
                 .reshape(128, -1))

    shared = {
        "wffnT": tile_oc(wffn.T, 256),
        "wqT": tile_oc(w_qkv[0:H * DH].T, 256),
        "wkT": tile_oc(w_qkv[H * DH:H * DH + KVH * DH].T, 256),
        "wvT": tile_oc(w_qkv[H * DH + KVH * DH:].T, 256),
        "wgT": tile_flat(w_gate.T, 96),
        "woT": tile_oc(w_o.T, 256),
        "w1kT": tile_oc(k_w1.T, 128),
        "w1vT": tile_oc(v_w1.T, 128),
        "w2kT": tile_flat(k_w2.T, 128),
        "w2vT": tile_flat(v_w2.T, 128),
        "kb1": C(k_b1.reshape(16, 128).T),
        "vb1": C(v_b1.reshape(16, 128).T),
        "kb2p": C(k_b2.reshape(128, 1)),
        "vb2": C(v_b2.reshape(128, 1)),
        "kpT": C(np.transpose(k_pos, (2, 0, 1)).reshape(128, 128)),
        "vpT": C(np.transpose(v_pos, (2, 0, 1)).reshape(128, 128)),
        "memk": C(mem_kv[0][:, 0, :].T),
        "memv": C(mem_kv[1][:, 0, :].reshape(1, 8 * 128)),
        "bgate": C(b_gate.reshape(1, 96)),
        "identb": np.eye(128, dtype=bf),
        "identf": np.eye(128, dtype=np.float32),
    }
    kk = np.arange(128)[:, None]; qq = np.arange(128)[None, :]
    diag = ((kk <= qq) & (kk >= qq - 64)).astype(np.float32)
    prev = (kk >= qq + 64).astype(np.float32)
    zero = np.zeros((128, 128), np.float32)
    shared["fmdiag"] = (((kk // 16) == (qq // 16)) & (kk <= qq)).astype(bf)

    inv = 1.0 / 10000.0 ** (np.arange(0, DH, 2, dtype=np.float64) / DH)
    per_core = []
    for c in range(8):
        b, hh = c // 2, c % 2
        if hh == 0:
            tb = np.arange(NBLK)
        else:
            tb = np.arange(NBLK).reshape(-1, 2)[:, ::-1].reshape(-1)
        xp = C(x[b].reshape(NBLK, 128, D)[tb].reshape(S, D))
        pos = (tb[:, None] * 128 + np.arange(128)[None, :]).reshape(-1)
        ang = pos[:, None].astype(np.float64) * inv[None, :]
        cosT = C(np.cos(ang).astype(bf)); sinT = C(np.sin(ang).astype(bf))
        gq = pos.reshape(NBLK, 128)[0::2].reshape(-1)
        jb = np.arange(256)
        tstart = tb[jb // 8] * 128 + (jb % 8) * 16
        vis = (tstart[:, None] + 16 <= gq[None, :])
        cmask = np.ones((257, OWN), np.float32); cmask[1:] = vis
        cneg = np.where(vis, 0.0, NEGB).astype(bf)
        per_core.append({
            "x_full": xp, "cosT": cosT, "sinT": sinT,
            "cmask": cmask.astype(bf), "cneg": C(cneg),
            "smA": (prev if hh == 0 else zero).astype(bf),
            "smB": diag.astype(bf),
            "smC": (zero if hh == 0 else prev).astype(bf),
        })
    return shared, per_core


ENGINES = ("sync", "tensor", "vector", "scalar", "gpsimd")


class Em:
    def __init__(self, nc):
        self.nc = nc
        self.ops = {e: [] for e in ENGINES}
        self.sem = {}
        self.cnt = {}
        for s in ("LD", "GQ", "MM", "VE", "SE", "PO"):
            self.sem[s] = nc.alloc_semaphore(s)
            self.cnt[s] = 0
        self.last_wait = {e: {} for e in ENGINES}

    def on(self, eng, fn):
        self.ops[eng].append(fn)

    def wait(self, eng, sem, val):
        if val is None or val <= 0:
            return
        lw = self.last_wait[eng]
        if lw.get(sem, 0) >= val:
            return
        lw[sem] = val
        s = self.sem[sem]
        self.on(eng, lambda E: E.wait_ge(s, val))

    def emit(self, eng, build, sem=None, n=1):
        if sem is None or n is None:
            self.on(eng, lambda E: build(E))
            return None
        s = self.sem[sem]
        self.cnt[sem] += n
        v = self.cnt[sem]
        self.on(eng, lambda E: build(E).then_inc(s, n))
        return v

    def barrier(self):
        vals = dict(self.cnt)
        for e in ENGINES:
            for s, v in vals.items():
                self.wait(e, s, v)

    def finalize(self):
        with self.nc.Block() as blk:
            blk.sync(lambda E: [f(E) for f in self.ops["sync"]])
            blk.tensor(lambda E: [f(E) for f in self.ops["tensor"]])
            blk.vector(lambda E: [f(E) for f in self.ops["vector"]])
            blk.scalar(lambda E: [f(E) for f in self.ops["scalar"]])
            blk.gpsimd(lambda E: [f(E) for f in self.ops["gpsimd"]])


class Rot:
    def __init__(self, em, n):
        self.em, self.n = em, n
        self.rel = [None] * n

    def acquire(self, i, eng):
        r = self.rel[i % self.n]
        if r is not None:
            self.em.wait(eng, r[0], r[1])

    def release(self, i, sem, val):
        self.rel[i % self.n] = (sem, val)


def build_program(dbg=(), upto=5):
    nc = bass.Bass("TRN2", target_bir_lowering=False)
    dbg = set(dbg)
    em = Em(nc)

    ap = {}
    def ein(name, shape, dtype):
        ap[name] = nc.dram_tensor(name, shape, dtype, kind="ExternalInput").ap()
    ein("x_full", [S, D], F32)
    ein("wffnT", [16, 128, 32 * 256], I32)
    ein("wqT", [16, 128, 32 * 256], F32); ein("wkT", [4, 128, 32 * 256], F32)
    ein("wvT", [4, 128, 32 * 256], F32); ein("wgT", [128, 32 * 96], F32)
    ein("woT", [16, 128, 32 * 256], F32)
    ein("w1kT", [16, 128, 16 * 128], F32); ein("w1vT", [16, 128, 16 * 128], F32)
    ein("w2kT", [128, 16 * 128], F32); ein("w2vT", [128, 16 * 128], F32)
    ein("kb1", [128, 16], F32); ein("vb1", [128, 16], F32)
    ein("kb2p", [128, 1], F32); ein("vb2", [128, 1], F32)
    ein("kpT", [128, 128], F32); ein("vpT", [128, 128], F32)
    ein("memk", [128, 8], F32); ein("memv", [1, 8 * 128], F32)
    ein("bgate", [1, 96], F32)
    ein("cosT", [S, 64], BF16); ein("sinT", [S, 64], BF16)
    ein("identb", [128, 128], BF16); ein("identf", [128, 128], F32)
    ein("smA", [128, 128], BF16); ein("smB", [128, 128], BF16)
    ein("smC", [128, 128], BF16); ein("fmdiag", [128, 128], BF16)
    ein("cmask", [257, OWN], BF16); ein("cneg", [256, OWN], BF16)

    out = nc.dram_tensor("out", [OWN, D], F32, kind="ExternalOutput").ap()
    x1d = nc.dram_tensor("x1d", [S, D], F32, kind="Internal").ap()
    x2d = nc.dram_tensor("x2d", [OWN, D], F32, kind="Internal").ap()
    hs2Td = nc.dram_tensor("hs2Td", [NMEG, 128, 32 * MEG], BF16, kind="Internal").ap()
    qro3 = nc.dram_tensor("qro3", [8, KVH, 128, 1024], BF16, kind="Internal").ap()
    qno3 = nc.dram_tensor("qno3", [8, KVH, 128, 1024], BF16, kind="Internal").ap()
    rkd = nc.dram_tensor("rkd", [KVH, 128, S], BF16, kind="Internal").ap()
    vtok3d = nc.dram_tensor("vtok3", [KVH, 128, 32 * 132], BF16, kind="Internal").ap()
    maskfd = nc.dram_tensor("maskfd", [KVH, 256, OWN], BF16, kind="Internal").ap()

    dbg_out = {}
    def dout(name, shape, dtype=F32):
        if name in dbg:
            dbg_out[name] = nc.dram_tensor("dbg_" + name, shape, dtype,
                                           kind="ExternalOutput").ap()
        return dbg_out.get(name)
    d_x1 = dout("x1", [S, D])
    d_hs0 = dout("hs0", [128, D], BF16)
    d_hsT0 = dout("hsT0", [128, 32 * MEG], BF16)
    d_w0 = dout("w0", [128, 32 * 256], BF16)
    d_q = dout("q", [8, KVH, 128, 1024], BF16)
    d_rk = dout("rk", [KVH, 128, S], BF16)
    d_vtok = dout("vtok", [KVH, 128, 32 * 132], BF16)
    d_ck = dout("ck", [128, KVH * 256], BF16)
    d_cv = dout("cv", [128, KVH * 256], BF16)
    d_gates = dout("gates", [128, NT * 96])
    d_maskf = dout("maskf", [KVH, 256, OWN], BF16)
    d_impq = dout("impq", [KVH, 8, 128, 512])
    d_csim = dout("csim", [G, 2, 128, 256])
    d_mix = dout("mix", [128, NT * D], BF16)
    d_x2 = dout("x2", [OWN, D])

    AR = lambda name, shape, dt: nc.alloc_sbuf_tensor(name, shape, dt, side="right")
    identb = AR("c_identb", [128, 128], BF16)
    identf = AR("c_identf", [128, 128], F32)
    onesrow = AR("c_onesrow", [1, 128], BF16)
    gates = AR("c_gates", [128, NT * 96], F32)
    ck_fm = AR("c_ck_fm", [128, KVH * 256], BF16)
    cv_fm = AR("c_cv_fm", [128, KVH * 256], BF16)
    memkb = AR("c_memkb", [128, 8], BF16)
    memvaug = AR("c_memvaug", [1, 8 * 132], BF16)
    eshv = AR("c_eshv", [128, 1], F32)

    VS = nc.alloc_semaphore("VS")
    SS = nc.alloc_semaphore("SS")
    _vsn = [0]; _ssn = [0]

    def vch(E, inst):
        _vsn[0] += 1
        inst.then_inc(VS, 1)
        E.wait_ge(VS, _vsn[0])
        return inst

    def sch(E, inst):
        _ssn[0] += 1
        inst.then_inc(SS, 1)
        E.wait_ge(SS, _ssn[0])
        return inst

    PSF = [nc.alloc_psum_tensor(f"psf{i}", [128, 512], F32) for i in range(5)]
    PTB = [nc.alloc_psum_tensor(f"ptb{i}", [128, 128], BF16) for i in range(2)]
    PTF = nc.alloc_psum_tensor("ptf", [128, 128], F32)
    tb_slots = Rot(em, 2)
    tf_slots = Rot(em, 1)

    em.emit("sync", lambda E: E.dma_start(identb[:, :], ap["identb"]), "LD", 16)
    em.emit("sync", lambda E: E.dma_start(identf[:, :], ap["identf"]), "LD", 16)
    em.emit("gpsimd", lambda E: E.dma_start(memkb[:, :], ap["memk"]), "GQ", 16)
    em.emit("gpsimd", lambda E: E.dma_start(
        memvaug.ap().rearrange("p (h c) -> p h c", c=132)[:, :, 0:128],
        ap["memv"].rearrange("p (h c) -> p h c", c=128)), "GQ", 16)
    def _ones(E):
        E.memset(onesrow[:, :], 1.0)
        E.memset(eshv[:, :], -ESH)
        for hh in range(8):
            E.memset(memvaug[:, hh * 132 + 128:hh * 132 + 129], 1.0)
        return E.engine_nop()
    em.emit("gpsimd", _ones, "PO", 1)
    em.barrier()

    def emit_ln(xin_t, hout_t, stat_t, sq_t, prev_se):
        if prev_se is not None:
            em.wait("vector", "SE", prev_se)
        def vstats(E):
            vch(E, E.reduce_sum(stat_t[:, 0:1], xin_t[:, :], axis=AX.X))
            vch(E, E.scalar_tensor_tensor(sq_t[:, :], xin_t[:, :], 1.0, xin_t[:, :],
                                          ALU.bypass, ALU.mult,
                                          accum_out=stat_t[:, 1:2]))
            E.tensor_scalar_mul(stat_t[:, 0:1], stat_t[:, 0:1], 1.0 / D)
            vch(E, E.tensor_scalar_mul(stat_t[:, 1:2], stat_t[:, 1:2], 1.0 / D))
            vch(E, E.scalar_tensor_tensor(stat_t[:, 2:3], stat_t[:, 0:1], 1.0,
                                          stat_t[:, 0:1], ALU.bypass, ALU.mult))
            vch(E, E.tensor_sub(stat_t[:, 1:2], stat_t[:, 1:2], stat_t[:, 2:3]))
            return E.tensor_scalar_add(stat_t[:, 1:2], stat_t[:, 1:2], EPS)
        ve = em.emit("vector", vstats, "VE", 1)
        em.wait("scalar", "VE", ve)
        def sr0(E):
            sch(E, E.activation(stat_t[:, 3:4], stat_t[:, 1:2], AF.Ln))
            return E.activation(stat_t[:, 3:4], stat_t[:, 3:4], AF.Exp, scale=-0.5)
        se0 = em.emit("scalar", sr0, "SE", 1)
        em.wait("vector", "SE", se0)
        def newton(E):
            vch(E, E.tensor_mul(stat_t[:, 5:6], stat_t[:, 3:4], stat_t[:, 3:4]))
            vch(E, E.tensor_mul(stat_t[:, 5:6], stat_t[:, 5:6], stat_t[:, 1:2]))
            vch(E, E.tensor_scalar(stat_t[:, 5:6], stat_t[:, 5:6], -0.5, 1.5,
                                   ALU.mult, ALU.add))
            return E.tensor_mul(stat_t[:, 3:4], stat_t[:, 3:4], stat_t[:, 5:6])
        ven = em.emit("vector", newton, "VE", 1)
        em.wait("scalar", "VE", ven)
        def sapply(E):
            sch(E, E.mul(stat_t[:, 4:5], stat_t[:, 0:1], -1.0))
            sch(E, E.mul(stat_t[:, 4:5], stat_t[:, 4:5], stat_t[:, 3:4]))
            return E.activation(hout_t[:, :], xin_t[:, :], AF.Identity,
                                bias=stat_t[:, 4:5], scale=stat_t[:, 3:4])
        return em.emit("scalar", sapply, "SE", 1)

    def emit_T(src, dst, se_wait=None, f32=False):
        if f32:
            slots, ident = tf_slots, identf
            i = emit_T.ctr[True] = emit_T.ctr.get(True, 0) + 1
            psum = PTF
        else:
            slots, ident = tb_slots, identb
            i = emit_T.ctr[False] = emit_T.ctr.get(False, 0) + 1
            psum = PTB[i % 2]
        slots.acquire(i, "tensor")
        if se_wait is not None:
            em.wait("tensor", "SE", se_wait)
        mmv = em.emit("tensor", lambda E, s=src, ps=psum, idt=ident:
                      E.transpose(ps[:, :], s, idt[:, :]), "MM", 1)
        em.wait("vector", "MM", mmv)
        ve = em.emit("vector", lambda E, d=dst, ps=psum:
                     E.tensor_copy(d, ps[:, :]), "VE", 1)
        slots.release(i, "VE", ve)
        return ve
    emit_T.ctr = {}

    def phase_ffn(src_ap, dst_ap, nmeg, store_hsT=None, d_tap=None, tag="f"):
        OCW, NOC = 256, 16
        xin = [nc.sbuf_tensor(f"{tag}_xin{i}", [128, D], F32) for i in range(2)]
        hs = [nc.sbuf_tensor(f"{tag}_hs{i}", [128, D], BF16) for i in range(2)]
        sqf = nc.sbuf_tensor(f"{tag}_sqf", [128, D], F32)
        stat = nc.sbuf_tensor(f"{tag}_stat", [128, 8], F32)
        hsT = nc.sbuf_tensor(f"{tag}_hsT", [128, 32 * MEG], BF16)
        wcol = [nc.sbuf_tensor(f"{tag}_w{i}", [128, 32 * OCW], BF16) for i in range(2)]
        xr = [nc.sbuf_tensor(f"{tag}_xr{i}", [128, OCW], F32) for i in range(4)]
        x1s = [nc.sbuf_tensor(f"{tag}_x1{i}", [128, OCW], F32) for i in range(4)]
        ctxs = xin + hs + [sqf, stat, hsT] + wcol + xr + x1s
        tens = [c.__enter__() for c in ctxs]
        (xin0, xin1, hs0, hs1b, sqf, stat, hsT, w0, w1, xr0, xr1, xr2, xr3,
         s0, s1, s2, s3) = tens
        xin = [xin0, xin1]; hs = [hs0, hs1b]; wcol = [w0, w1]
        xr = [xr0, xr1, xr2, xr3]; x1s = [s0, s1, s2, s3]

        xin_r = Rot(em, 2); hs_r = Rot(em, 2); w_r = Rot(em, 2)
        xr_r = Rot(em, 4); x1_r = Rot(em, 4)
        prev_se = None
        for m in range(nmeg):
            em.wait("vector", "MM", em.cnt["MM"])
            for r in range(8):
                i = m * 8 + r
                row = m * MEG + r * 128
                xin_r.acquire(i, "sync")
                ld = em.emit("sync", lambda E, row=row, b=xin[i % 2]:
                             E.dma_start(b[:, :], src_ap[row:row + 128, :]), "LD", 16)
                em.wait("vector", "LD", ld)
                hs_r.acquire(i, "vector")
                hs_r.acquire(i, "scalar")
                prev_se = emit_ln(xin[i % 2], hs[i % 2], stat, sqf, prev_se)
                xin_r.release(i, "SE", prev_se)
                if i == 0 and tag == "f" and dbg_out.get("hs0") is not None:
                    em.wait("sync", "SE", prev_se)
                    em.emit("sync", lambda E: E.dma_start(dbg_out["hs0"], hs[0][:, :]),
                            "LD", 16)
                lastve = None
                for dc in range(32):
                    lastve = emit_T(hs[i % 2][:, dc * 128:(dc + 1) * 128],
                                    hsT[:, dc * MEG + r * 128:dc * MEG + r * 128 + 128],
                                    se_wait=prev_se)
                hs_r.release(i, "VE", lastve)
            hsT_ready = em.cnt["VE"]
            if m == 0 and tag == "f" and dbg_out.get("hsT0") is not None:
                em.wait("sync", "VE", hsT_ready)
                em.emit("sync", lambda E: E.dma_start(dbg_out["hsT0"], hsT[:, :]),
                        "LD", 16)
            if store_hsT is not None:
                em.wait("sync", "VE", hsT_ready)
                em.emit("sync", lambda E, m=m: E.dma_start(store_hsT[m], hsT[:, :]),
                        "LD", 16)
            for oc in range(NOC):
                wb = wcol[oc % 2]
                w_r.acquire(m * NOC + oc, "gpsimd")
                src_w = ap["wffnT"][oc]
                gq = em.emit("gpsimd", lambda E, wb=wb, s=src_w: E.dma_start(
                    wb[:, :], s), "GQ", 16)
                if m == 0 and oc == 0 and tag == "f" and dbg_out.get("w0") is not None:
                    em.wait("sync", "GQ", gq)
                    em.emit("sync", lambda E, wb=wb: E.dma_start(
                        dbg_out["w0"], wb[:, :]), "LD", 16)
                em.wait("tensor", "GQ", gq)
                em.wait("tensor", "VE", hsT_ready)
                for r in range(8):
                    i = (m * NOC + oc) * 8 + r
                    pb = PSF[r % 4]
                    row = m * MEG + r * 128
                    xr_r.acquire(i, "sync")
                    ldr = em.emit("sync", lambda E, b=xr[r % 4], row=row, oc=oc:
                                  E.dma_start(b[:, :], src_ap[row:row + 128,
                                                             oc * OCW:(oc + 1) * OCW]),
                                  "LD", 16)
                    xr_r.acquire(i, "tensor")
                    for dc in range(32):
                        em.emit("tensor", lambda E, pb=pb, dc=dc, r=r, wb=wb:
                                E.matmul(pb[:, 0:OCW],
                                         hsT[:, dc * MEG + r * 128:dc * MEG + r * 128 + 128],
                                         wb[:, dc * OCW:(dc + 1) * OCW],
                                         start=(dc == 0), stop=(dc == 31)),
                                "MM", 1 if dc == 31 else None)
                    mmv = em.cnt["MM"]
                    em.wait("vector", "MM", mmv)
                    em.wait("vector", "LD", ldr)
                    x1_r.acquire(i, "vector")
                    ve = em.emit("vector", lambda E, d=x1s[r % 4], pb=pb, b=xr[r % 4]:
                                 E.scalar_tensor_tensor(d[:, :], pb[:, 0:OCW], 0.5,
                                                        b[:, :], ALU.mult, ALU.add),
                                 "VE", 1)
                    xr_r.release(i, "VE", ve)
                    em.wait("sync", "VE", ve)
                    st = em.emit("sync", lambda E, d=x1s[r % 4], row=row, oc=oc:
                                 E.dma_start(dst_ap[row:row + 128,
                                                    oc * OCW:(oc + 1) * OCW], d[:, :]),
                                 "LD", 16)
                    if d_tap is not None:
                        st = em.emit("sync", lambda E, d=x1s[r % 4], row=row, oc=oc:
                                     E.dma_start(d_tap[row:row + 128,
                                                       oc * OCW:(oc + 1) * OCW], d[:, :]),
                                     "LD", 16)
                    x1_r.release(i, "LD", st)
                w_r.release(m * NOC + oc, "MM", em.cnt["MM"])
        em.barrier()
        for c in reversed(ctxs):
            c.__exit__(None, None, None)

    phase_ffn(ap["x_full"], x1d, NMEG, d_tap=d_x1)
    if upto < 2:
        em.wait("sync", "LD", em.cnt["LD"])
        em.finalize()
        return nc, dbg_out

    def phase2():
        cm = [
            nc.sbuf_tensor("p2_xin", [128, D], F32),
            nc.sbuf_tensor("p2_hs", [128, D], BF16),
            nc.sbuf_tensor("p2_stat", [128, 8], F32),
            nc.sbuf_tensor("p2_hsT", [128, 32 * MEG], BF16),
            nc.sbuf_tensor("p2_w", [128, 32 * 256], BF16),
            nc.sbuf_tensor("p2_kt0", [128, 256], BF16),
            nc.sbuf_tensor("p2_kt1", [128, 256], BF16),
            nc.sbuf_tensor("p2_rkr", [128, 256], BF16),
            nc.sbuf_tensor("p2_cosm", [128, 8 * 128], BF16),
            nc.sbuf_tensor("p2_kfm", [128, KVH * MEG], BF16),
            nc.sbuf_tensor("p2_vfm", [128, KVH * MEG], BF16),
            nc.sbuf_tensor("p2_cs", [128, 16 * 512], BF16),
            nc.sbuf_tensor("p2_hid", [128, 8 * 512], BF16),
            nc.sbuf_tensor("p2_w10", [128, 16 * 128], BF16),
            nc.sbuf_tensor("p2_w11", [128, 16 * 128], BF16),
            nc.sbuf_tensor("p2_w2k", [128, 16 * 128], BF16),
            nc.sbuf_tensor("p2_w2v", [128, 16 * 128], BF16),
            nc.sbuf_tensor("p2_ckst", [128, 512], BF16),
            nc.sbuf_tensor("p2_stg", [128, 256], BF16),
            nc.sbuf_tensor("p2_gt", [128, 96], F32),
            nc.sbuf_tensor("p2_wg", [128, 32 * 96], BF16),
            nc.sbuf_tensor("p2_bg", [1, 96], BF16),
            nc.sbuf_tensor("p2_kb", [128, 40], F32),
            nc.sbuf_tensor("p2_kp", [128, 128], F32),
            nc.sbuf_tensor("p2_vp", [128, 128], F32),
        ]
        t = [c.__enter__() for c in cm]
        (xin, hs, stat, hsT, wcb, kt0, kt1, rkr, cosm, kfm, vfm, cs, hid,
         w10, w11, w2k, w2v, ckst, stg, gt, wgTb, bgateb, kbs, kpT, vpT) = t
        w1b = [w10, w11]
        ktb = [kt0, kt1]
        em.emit("sync", lambda E: E.dma_start(kbs[:, 0:16], ap["kb1"]), "LD", 16)
        em.emit("sync", lambda E: E.dma_start(kbs[:, 16:32], ap["vb1"]), "LD", 16)
        em.emit("sync", lambda E: E.dma_start(kbs[:, 32:33], ap["kb2p"]), "LD", 16)
        em.emit("sync", lambda E: E.dma_start(kbs[:, 33:34], ap["vb2"]), "LD", 16)
        em.emit("sync", lambda E: E.dma_start(kpT[:, :], ap["kpT"]), "LD", 16)
        em.emit("sync", lambda E: E.dma_start(vpT[:, :], ap["vpT"]), "LD", 16)
        em.emit("gpsimd", lambda E: E.dma_start(bgateb[:, :], ap["bgate"]), "GQ", 16)
        em.emit("gpsimd", lambda E: E.dma_start(wgTb[:, :], ap["wgT"]), "GQ", 16)
        for nm, tt in [("w2kT", w2k), ("w2vT", w2v)]:
            em.emit("gpsimd", lambda E, tt=tt, nm=nm: E.dma_start(
                tt[:, :], ap[nm]), "GQ", 16)
        em.barrier()

        w1_r = Rot(em, 2)
        prev_se = [None]
        mlp1_se = [None, None]
        mlp2_mm = [0]
        ck_evac_se = [None]
        cs_mm_guard = [0]
        w_mm = [0]
        kt_mm = [0, 0]
        rkr_mm = [0]
        ktt_ld = [0, 0]
        stg_ld = [0]
        gse_ = [0, 0]

        def rope_tile(E, dstt, srct, nh, cosAP, sinAP):
            sv = srct.ap().rearrange("p (h i t) -> p h t i", i=64, t=2)
            dv = dstt.ap().rearrange("p (h i t) -> p h t i", i=64, t=2)
            cb = cosAP.unsqueeze(1).broadcast_to([128, nh, 64])
            sb = sinAP.unsqueeze(1).broadcast_to([128, nh, 64])
            e = sv[:, :, 0, :]; o = sv[:, :, 1, :]
            rt_ = rtmp.ap().rearrange("p (h i) -> p h i", i=64)[:, 0:nh]
            E.tensor_tensor(dv[:, :, 0, :], e, cb, ALU.mult)
            vch(E, E.tensor_tensor(rt_, o, sb, ALU.mult))
            vch(E, E.tensor_sub(dv[:, :, 0, :], dv[:, :, 0, :], rt_))
            E.tensor_tensor(dv[:, :, 1, :], o, cb, ALU.mult)
            vch(E, E.tensor_tensor(rt_, e, sb, ALU.mult))
            return E.tensor_add(dv[:, :, 1, :], dv[:, :, 1, :], rt_)

        def compress(m, side):
            fmsrc = kfm if side == "k" else vfm
            posT = kpT if side == "k" else vpT
            b1col = kbs[:, 0:16] if side == "k" else kbs[:, 16:32]
            b2b = kbs[:, 32:33] if side == "k" else kbs[:, 33:34]
            b2w = w2k if side == "k" else w2v
            dst = ck_fm if side == "k" else cv_fm
            em.wait("vector", "MM", cs_mm_guard[0])
            def csb(E):
                last = None
                for u in range(16):
                    for hh2 in range(KVH):
                        s2 = fmsrc.ap().rearrange("p (h a b) -> p h a b",
                                                  h=KVH, b=16)[:, hh2, :, u]
                        last = E.tensor_scalar_add(
                            cs[:, u * 512 + hh2 * 64:u * 512 + (hh2 + 1) * 64], s2,
                            posT[:, hh2 * 16 + u:hh2 * 16 + u + 1])
                return last
            csv = em.emit("vector", csb, "VE", 1)
            pb2 = PSF[4]
            for half in range(2):
                for oci in range(8):
                    oc = half * 8 + oci
                    wb1 = w1b[oc % 2]
                    w1_r.acquire(oc, "gpsimd")
                    src_w = ap["w1kT" if side == "k" else "w1vT"][oc]
                    gq = em.emit("gpsimd", lambda E, wb1=wb1, s=src_w: E.dma_start(
                        wb1[:, :], s), "GQ", 16)
                    pb = PSF[2 + oc % 2]
                    em.wait("tensor", "GQ", gq)
                    em.wait("tensor", "VE", csv)
                    em.wait("tensor", "SE", mlp1_se[oc % 2] or 0)
                    for u in range(16):
                        em.emit("tensor", lambda E, pb=pb, u=u, wb1=wb1: E.matmul(
                            pb[:, :], wb1[:, u * 128:(u + 1) * 128],
                            cs[:, u * 512:(u + 1) * 512],
                            start=(u == 0), stop=(u == 15)),
                            "MM", 1 if u == 15 else None)
                    w1_r.release(oc, "MM", em.cnt["MM"])
                    em.wait("scalar", "MM", em.cnt["MM"])
                    em.wait("scalar", "MM", mlp2_mm[0])
                    se = em.emit("scalar", lambda E, oci=oci, pb=pb, b1col=b1col,
                                 oc=oc: E.activation(
                                     hid[:, oci * 512:(oci + 1) * 512], pb[:, :],
                                     AF.Relu, bias=b1col[:, oc:oc + 1]), "SE", 1)
                    mlp1_se[oc % 2] = se
                em.wait("tensor", "SE", em.cnt["SE"])
                if half == 0:
                    em.wait("tensor", "SE", ck_evac_se[0] or 0)
                for oci in range(8):
                    oc = half * 8 + oci
                    em.emit("tensor", lambda E, pb2=pb2, oci=oci, oc=oc, b2w=b2w:
                            E.matmul(pb2[:, :], b2w[:, oc * 128:(oc + 1) * 128],
                                     hid[:, oci * 512:(oci + 1) * 512],
                                     start=(oc == 0), stop=(oc == 15)),
                            "MM", 1 if oc == 15 else None)
                mlp2_mm[0] = em.cnt["MM"]
            em.wait("scalar", "MM", em.cnt["MM"])
            se = em.emit("scalar", lambda E, pb2=pb2, b2b=b2b: E.activation(
                ckst[:, :], pb2[:, :], AF.Identity, bias=b2b), "SE", 1)
            ck_evac_se[0] = se
            em.wait("vector", "SE", se)
            def scat(E, dst=dst, m=m):
                last = None
                for hh2 in range(KVH):
                    last = E.tensor_copy(
                        dst[:, hh2 * 256 + m * 64:hh2 * 256 + (m + 1) * 64],
                        ckst[:, hh2 * 64:(hh2 + 1) * 64])
                return last
            em.emit("vector", scat, "VE", 1)
            cs_mm_guard[0] = em.cnt["MM"]

        for m in range(NMEG):
            em.wait("vector", "MM", em.cnt["MM"])
            cm_ld = em.emit("sync", lambda E, m=m: E.dma_start(
                cosm.ap()[:, 0:512].rearrange("p (r i) -> p r i", i=64),
                ap["cosT"][m * MEG:(m + 1) * MEG].rearrange(
                    "(r p) i -> p r i", p=128)), "LD", 16)
            cm_ld = em.emit("sync", lambda E, m=m: E.dma_start(
                cosm.ap()[:, 512:1024].rearrange("p (r i) -> p r i", i=64),
                ap["sinT"][m * MEG:(m + 1) * MEG].rearrange(
                    "(r p) i -> p r i", p=128)), "LD", 16)
            for r in range(8):
                row = m * MEG + r * 128
                if prev_se[0] is not None:
                    em.wait("sync", "SE", prev_se[0])
                ld = em.emit("sync", lambda E, row=row: E.dma_start(
                    xin[:, :], x1d[row:row + 128, :]), "LD", 16)
                em.wait("vector", "LD", ld)
                em.wait("vector", "MM", em.cnt["MM"])
                em.wait("scalar", "MM", em.cnt["MM"])
                prev_se[0] = emit_ln(xin, hs, stat, cs.ap().bitcast(F32),
                                     prev_se[0])
                for dc in range(32):
                    emit_T(hs[:, dc * 128:(dc + 1) * 128],
                           hsT[:, dc * MEG + r * 128:dc * MEG + r * 128 + 128],
                           se_wait=prev_se[0])
            hsT_ready = em.cnt["VE"]

            for kvc in range(8):
                src_w = (ap["wkT"][kvc] if kvc < 4 else ap["wvT"][kvc - 4])
                em.wait("gpsimd", "MM", w_mm[0])
                gq = em.emit("gpsimd", lambda E, s=src_w: E.dma_start(
                    wcb[:, :], s), "GQ", 16)
                em.wait("tensor", "GQ", gq)
                em.wait("tensor", "VE", hsT_ready)
                for r in range(8):
                    pb = PSF[r % 2]
                    ktt = ktb[r % 2]
                    em.wait("tensor", "VE", kv_evac[r % 2] or 0)
                    for dc in range(32):
                        em.emit("tensor", lambda E, pb=pb, dc=dc, r=r: E.matmul(
                            pb[:, 0:256],
                            hsT[:, dc * MEG + r * 128:dc * MEG + r * 128 + 128],
                            wcb[:, dc * 256:(dc + 1) * 256],
                            start=(dc == 0), stop=(dc == 31)),
                            "MM", 1 if dc == 31 else None)
                    w_mm[0] = em.cnt["MM"]
                    em.wait("vector", "MM", em.cnt["MM"])
                    em.wait("vector", "MM", kt_mm[r % 2])
                    em.wait("vector", "LD", ktt_ld[r % 2])
                    em.wait("vector", "LD", cm_ld)
                    ve = em.emit("vector", lambda E, pb=pb, ktt=ktt:
                                 E.tensor_copy(ktt[:, :], pb[:, 0:256]), "VE", 1)
                    kv_evac[r % 2] = ve
                    h0 = 2 * kvc if kvc < 4 else 2 * (kvc - 4)
                    if kvc < 4:
                        em.wait("vector", "VE", ve)
                        em.wait("vector", "MM", rkr_mm[0])
                        ve2 = em.emit("vector", lambda E, ktt=ktt, r=r:
                                      rope_tile(E, rkr, ktt, 2,
                                                cosm.ap()[:, r * 64:r * 64 + 64],
                                                cosm.ap()[:, 512 + r * 64:
                                                          512 + r * 64 + 64]),
                                      "VE", 1)
                        em.wait("tensor", "VE", ve)
                        for j in range(2):
                            emit_T(ktt[:, j * 128:(j + 1) * 128],
                                   kfm[:, (h0 + j) * MEG + r * 128:
                                       (h0 + j) * MEG + r * 128 + 128])
                        em.wait("tensor", "VE", ve2)
                        em.wait("vector", "LD", stg_ld[0])
                        for j in range(2):
                            lastve = emit_T(rkr[:, j * 128:(j + 1) * 128],
                                            stg[:, j * 128:(j + 1) * 128])
                        rkr_mm[0] = em.cnt["MM"]
                        em.wait("sync", "VE", lastve)
                        st = em.emit("sync", lambda E, h0=h0, m=m, r=r: E.dma_start(
                            rkd[h0:h0 + 2, :, m * MEG + r * 128:
                                m * MEG + r * 128 + 128].rearrange(
                                    "h p c -> p h c"),
                            stg.ap().rearrange("p (h c) -> p h c", c=128)), "LD", 16)
                        stg_ld[0] = st
                        kt_mm[r % 2] = em.cnt["MM"]
                    else:
                        em.wait("sync", "VE", ve)
                        cpos = m * 8 + r
                        st = em.emit("sync", lambda E, h0=h0, cpos=cpos, ktt=ktt:
                                     E.dma_start(
                                         vtok3d[h0:h0 + 2, :,
                                                cpos * 132:cpos * 132 + 128]
                                         .rearrange("h p d -> p h d"),
                                         ktt.ap().rearrange("p (h d) -> p h d",
                                                            d=128)), "LD", 16)
                        em.wait("tensor", "VE", ve)
                        for j in range(2):
                            emit_T(ktt[:, j * 128:(j + 1) * 128],
                                   vfm[:, (h0 + j) * MEG + r * 128:
                                       (h0 + j) * MEG + r * 128 + 128])
                        kt_mm[r % 2] = em.cnt["MM"]
                        ktt_ld[r % 2] = st
            compress(m, "k")
            compress(m, "v")
            for qc in range(16):
                src_w = ap["wqT"][qc]
                em.wait("gpsimd", "MM", w_mm[0])
                gq = em.emit("gpsimd", lambda E, s=src_w: E.dma_start(
                    wcb[:, :], s), "GQ", 16)
                em.wait("tensor", "GQ", gq)
                for a in range(4):
                    r = 2 * a
                    tloc = m * 4 + a
                    pb = PSF[a % 2]
                    ktt = ktb[a % 2]
                    em.wait("tensor", "VE", kv_evac[a % 2] or 0)
                    for dc in range(32):
                        em.emit("tensor", lambda E, pb=pb, dc=dc, r=r: E.matmul(
                            pb[:, 0:256],
                            hsT[:, dc * MEG + r * 128:dc * MEG + r * 128 + 128],
                            wcb[:, dc * 256:(dc + 1) * 256],
                            start=(dc == 0), stop=(dc == 31)),
                            "MM", 1 if dc == 31 else None)
                    w_mm[0] = em.cnt["MM"]
                    em.wait("vector", "MM", em.cnt["MM"])
                    em.wait("vector", "MM", kt_mm[a % 2])
                    em.wait("vector", "LD", ktt_ld[a % 2])
                    ve = em.emit("vector", lambda E, pb=pb, ktt=ktt:
                                 E.tensor_copy(ktt[:, :], pb[:, 0:256]), "VE", 1)
                    kv_evac[a % 2] = ve
                    em.wait("vector", "VE", ve)
                    em.wait("vector", "MM", rkr_mm[0])
                    ve2 = em.emit("vector", lambda E, ktt=ktt, r=r:
                                  rope_tile(E, rkr, ktt, 2,
                                            cosm.ap()[:, r * 64:r * 64 + 64],
                                            cosm.ap()[:, 512 + r * 64:
                                                      512 + r * 64 + 64]),
                                  "VE", 1)
                    em.wait("tensor", "VE", ve2)
                    em.wait("vector", "LD", stg_ld[0])
                    for j in range(2):
                        lastve = emit_T(rkr[:, j * 128:(j + 1) * 128],
                                        stg[:, j * 128:(j + 1) * 128])
                    rkr_mm[0] = em.cnt["MM"]
                    em.wait("sync", "VE", lastve)
                    hh2 = (2 * qc) // 4
                    g0 = (2 * qc) % 4
                    qt2 = tloc // 2
                    hlf = tloc % 2
                    qdst = qro3[qt2, hh2].rearrange(
                        "p (g q) -> p g q", q=256)[:, g0:g0 + 2,
                                                   hlf * 128:hlf * 128 + 128]
                    st = em.emit("sync", lambda E, qdst=qdst: E.dma_start(
                        qdst, stg.ap().rearrange("p (h c) -> p h c", c=128)),
                        "LD", 16)
                    stg_ld[0] = st
                    em.wait("tensor", "VE", ve)
                    em.wait("vector", "LD", st)
                    for j in range(2):
                        lastve = emit_T(ktt[:, j * 128:(j + 1) * 128],
                                        stg[:, j * 128:(j + 1) * 128])
                    em.wait("sync", "VE", lastve)
                    qdst = qno3[qt2, hh2].rearrange(
                        "p (g q) -> p g q", q=256)[:, g0:g0 + 2,
                                                   hlf * 128:hlf * 128 + 128]
                    st = em.emit("sync", lambda E, qdst=qdst: E.dma_start(
                        qdst, stg.ap().rearrange("p (h c) -> p h c", c=128)),
                        "LD", 16)
                    stg_ld[0] = st
                    kt_mm[a % 2] = em.cnt["MM"]
            for a in range(4):
                tloc = m * 4 + a
                pb = PSF[4]
                em.wait("tensor", "VE", gse_[0])
                em.wait("tensor", "SE", ck_evac_se[0] or 0)
                for dc in range(32):
                    em.emit("tensor", lambda E, pb=pb, dc=dc, a=a: E.matmul(
                        pb[:, 0:96],
                        hsT[:, dc * MEG + 2 * a * 128:dc * MEG + 2 * a * 128 + 128],
                        wgTb[:, dc * 96:(dc + 1) * 96],
                        start=(dc == 0), stop=False), "MM", None)
                em.emit("tensor", lambda E, pb=pb: E.matmul(
                    pb[:, 0:96], onesrow[0:1, :], bgateb[0:1, :],
                    start=False, stop=True), "MM", 1)
                em.wait("scalar", "MM", em.cnt["MM"])
                em.wait("scalar", "VE", gse_[1])
                se = em.emit("scalar", lambda E, pb=pb: E.activation(
                    gt[:, :], pb[:, 0:96], AF.Exp, scale=-1.0), "SE", 1)
                em.wait("vector", "SE", se)
                def sig(E, tloc=tloc):
                    vch(E, E.tensor_scalar_add(gt[:, :], gt[:, :], 1.0))
                    return E.reciprocal(gates[:, tloc * 96:(tloc + 1) * 96], gt[:, :])
                gse_[1] = em.emit("vector", sig, "VE", 1)
                gse_[0] = gse_[1]
        em.barrier()
        if d_rk is not None:
            em.emit("sync", lambda E: E.dma_start(d_rk, rkd), "LD", 16)
        if d_vtok is not None:
            em.emit("sync", lambda E: E.dma_start(d_vtok, vtok3d), "LD", 16)
        if d_ck is not None:
            em.emit("sync", lambda E: E.dma_start(d_ck, ck_fm[:, :]), "LD", 16)
        if d_cv is not None:
            em.emit("sync", lambda E: E.dma_start(d_cv, cv_fm[:, :]), "LD", 16)
        if d_q is not None:
            em.emit("sync", lambda E: E.dma_start(d_q, qro3), "LD", 16)
        if d_gates is not None:
            em.emit("sync", lambda E: E.dma_start(d_gates, gates[:, :]), "LD", 16)
        em.barrier()
        for c in reversed(cm):
            c.__exit__(None, None, None)

    kv_evac = [None, None]
    rtmp = nc.alloc_sbuf_tensor("c_rtmp", [128, 128], BF16, side="right")
    phase2()
    if upto < 3:
        em.wait("sync", "LD", em.cnt["LD"])
        em.finalize()
        return nc, dbg_out

    mixcm = nc.sbuf_tensor("mix", [128, NT * D], BF16)
    mix = mixcm.__enter__()

    def phase3():
        cm = [
            nc.sbuf_tensor("a_rk", [128, S], BF16),
            nc.sbuf_tensor("a_vaug", [128, 32 * 132], BF16),
            nc.sbuf_tensor("a_cvaug", [128, 2 * 132], BF16),
            nc.sbuf_tensor("a_rq0", [128, G * 256], BF16),
            nc.sbuf_tensor("a_rq1", [128, G * 256], BF16),
            nc.sbuf_tensor("a_rqu", [128, G * 256], BF16),
            nc.sbuf_tensor("a_pr0", [128, 256], BF16),
            nc.sbuf_tensor("a_pr1", [128, 256], BF16),
            nc.sbuf_tensor("a_pr2", [128, 256], BF16),
            nc.sbuf_tensor("a_pr3", [128, 256], BF16),
            nc.sbuf_tensor("a_pf0", [128, 256], BF16),
            nc.sbuf_tensor("a_pf1", [128, 256], BF16),
            nc.sbuf_tensor("a_ps0", [128, 256], BF16),
            nc.sbuf_tensor("a_ps1", [128, 256], BF16),
            nc.sbuf_tensor("a_pmem", [1, 2 * 256], BF16),
            nc.sbuf_tensor("a_maskc", [128, 32 * 256], BF16),
            nc.sbuf_tensor("a_impac", [128, 2 * 256], F32),
            nc.sbuf_tensor("a_impq", [128, 512], F32),
            nc.sbuf_tensor("a_sele", [128, 256], F32),
            nc.sbuf_tensor("a_selw", [128, 256], F32),
            nc.sbuf_tensor("a_selm", [128, 8], F32),
            nc.sbuf_tensor("a_selq", [128, 512], BF16),
            nc.sbuf_tensor("a_mfT", [128, 512], BF16),
            nc.sbuf_tensor("a_nrm", [128, 8], F32),
            nc.sbuf_tensor("a_mixt", [128, 256], F32),
            nc.sbuf_tensor("a_cmq", [128, 512], BF16),
            nc.sbuf_tensor("a_cnq", [128, 512], BF16),
            nc.sbuf_tensor("a_sm", [128, 4 * 128], BF16),
        ]
        t = [c.__enter__() for c in cm]
        (rk, vaug, cvaug, rq0, rq1, rqu, pr0, pr1, pr2, pr3,
         pf0, pf1, ps0, ps1, pmem, maskc, impac, impq,
         sele, selw, selm, selq, mfT, nrm, mixt, cmq, cnq, sm) = t
        rq2 = [rq0, rq1]
        prb = [pr0, pr1, pr2, pr3]
        pfin = [pf0, pf1]
        psli = [ps0, ps1]

        em.emit("sync", lambda E: E.dma_start(sm[:, 0:128], ap["smA"]), "LD", 16)
        em.emit("sync", lambda E: E.dma_start(sm[:, 128:256], ap["smB"]), "LD", 16)
        em.emit("sync", lambda E: E.dma_start(sm[:, 256:384], ap["smC"]), "LD", 16)
        em.emit("sync", lambda E: E.dma_start(sm[:, 384:512], ap["fmdiag"]), "LD", 16)
        em.barrier()

        st = {
            "qk_se": [0, 0], "qk_ve": [0, 0],
            "prb_mve": [0, 0, 0, 0], "prb_mm": [0, 0, 0, 0],
            "pf_ve": [0, 0], "pf_mm": [0, 0],
            "pmem_se": [0, 0], "pmem_mm": [0, 0],
            "bank": {2: ("VE", 0), 3: ("VE", 0), 4: ("VE", 0)},
            "impacT_mm": 0, "selq_mm": 0, "stm_ld": 0, "mask_ve": 0,
            "rq_mm": [0, 0], "rq_ld": [0, 0],
            "rqu_mm": 0, "rqu_ld": 0, "cmq_ve": 0, "cmq_ld": 0,
            "cnq_ve": 0, "cnq_ld": 0,
            "rk_mm": 0, "vaug_mm": 0, "cvaug_mm": 0,
            "cv_ve": 0, "pov": 0, "vaug_gq": 0, "rk_ld": 0,
            "dgv": 0, "fmask_ve": 0,
        }

        def wbank(eng, b):
            sname, v = st["bank"][b]
            em.wait(eng, sname, v)

        def comp(hh, qt):
            s = qt % 2
            qcol = qt * 256
            em.wait("sync", "MM", st["rq_mm"][s])
            st["rq_ld"][s] = em.emit("sync", lambda E, hh=hh, qt=qt, s=s:
                                     E.dma_start(rq2[s][:, :], qro3[qt, hh]),
                                     "LD", 16)
            em.wait("sync", "MM", st["rqu_mm"])
            st["rqu_ld"] = em.emit("sync", lambda E, hh=hh, qt=qt: E.dma_start(
                rqu[:, :], qno3[qt, hh]), "LD", 16)
            em.wait("sync", "VE", st["cmq_ve"])
            for sec in range(2):
                em.emit("sync", lambda E, sec=sec, qcol=qcol: E.dma_start(
                    cmq[:, sec * 256:(sec + 1) * 256],
                    ap["cmask"][1 + sec * 128:1 + (sec + 1) * 128,
                                qcol:qcol + 256]), "LD", 16)
            st["cmq_ld"] = em.cnt["LD"]
            em.wait("sync", "VE", st["cnq_ve"])
            for sec in range(2):
                em.emit("sync", lambda E, sec=sec, qcol=qcol: E.dma_start(
                    cnq[:, sec * 256:(sec + 1) * 256],
                    ap["cneg"][sec * 128:(sec + 1) * 128, qcol:qcol + 256]),
                    "LD", 16)
            st["cnq_ld"] = em.cnt["LD"]

            def comp_pv(g):
                psl = (g % 2) * 2
                sl = g % 2
                em.wait("tensor", "VE", st["prb_mve"][psl])
                em.wait("tensor", "VE", st["prb_mve"][psl + 1])
                em.wait("tensor", "SE", st["pmem_se"][sl])
                em.wait("tensor", "VE", st["cv_ve"])
                em.wait("tensor", "PO", st["pov"])
                wbank("tensor", 3)
                wbank("tensor", 4)
                blds = []
                for sj in range(2):
                    pbo = PSF[3 + sj]
                    for sec in range(2):
                        blds.append(lambda E, pbo=pbo, sj=sj, sec=sec, psl=psl:
                                    E.matmul(pbo[:, 0:129],
                                             prb[psl + sec][:, sj * 128:(sj + 1) * 128],
                                             cvaug[:, sec * 132:sec * 132 + 129],
                                             start=(sec == 0), stop=False,
                                             skip_group_check=True))
                    blds.append(lambda E, pbo=pbo, sj=sj, sl=sl, hh=hh: E.matmul(
                        pbo[:, 0:129],
                        pmem[0:1, sl * 256 + sj * 128:sl * 256 + (sj + 1) * 128],
                        memvaug[0:1, hh * 132:hh * 132 + 129],
                        start=False, stop=(sj == 1), skip_group_check=True))
                for b in blds[:-1]:
                    em.emit("tensor", b, "MM", None)
                mmv = em.emit("tensor", blds[-1], "MM", 1)
                st["prb_mm"][psl] = mmv
                st["prb_mm"][psl + 1] = mmv
                st["pmem_mm"][sl] = mmv
                st["cvaug_mm"] = mmv
                em.wait("vector", "MM", mmv)
                ves = []
                for sj, sub in enumerate((2 * qt, 2 * qt + 1)):
                    pbo = PSF[3 + sj]
                    head = hh * G + g
                    def cnorm(E, pbo=pbo, head=head, sub=sub):
                        vch(E, E.reciprocal(nrm[:, 0:1], pbo[:, 128:129]))
                        vch(E, E.tensor_mul(nrm[:, 1:2], nrm[:, 0:1],
                                            gates[:, sub * 96 + 3 * head:
                                                  sub * 96 + 3 * head + 1]))
                        mcol = sub * D + head * 128
                        return E.tensor_scalar(mix[:, mcol:mcol + 128],
                                               pbo[:, 0:128],
                                               nrm[:, 1:2], None, ALU.mult)
                    ves.append(em.emit("vector", cnorm, "VE", 1))
                st["bank"][3] = ("VE", ves[0])
                st["bank"][4] = ("VE", ves[1])

            for g in range(G):
                psl = (g % 2) * 2
                sl = g % 2
                for sec in range(2):
                    pb = PSF[sec]
                    em.wait("tensor", "SE", st["qk_se"][sec])
                    em.wait("tensor", "VE", st["qk_ve"][sec])
                    em.wait("tensor", "LD", st["rqu_ld"])
                    mmv = em.emit("tensor", lambda E, pb=pb, hh=hh, sec=sec, g=g:
                                  E.matmul(pb[:, 0:256],
                                           ck_fm[:, hh * 256 + sec * 128:
                                                 hh * 256 + (sec + 1) * 128],
                                           rqu[:, g * 256:(g + 1) * 256],
                                           start=True, stop=True), "MM", 1)
                    em.wait("scalar", "MM", mmv)
                    em.wait("scalar", "MM", st["prb_mm"][psl + sec])
                    em.wait("scalar", "VE", st["prb_mve"][psl + sec])
                    se = em.emit("scalar", lambda E, pb=pb, psl=psl, sec=sec:
                                 E.activation(prb[psl + sec][:, :], pb[:, 0:256],
                                              AF.Exp, scale=SCALE,
                                              bias=eshv[:, :]), "SE", 1)
                    st["qk_se"][sec] = se
                    em.wait("vector", "MM", mmv)
                    em.wait("vector", "SE", se)
                    if g == 0:
                        em.wait("vector", "MM", st["impacT_mm"])
                        em.wait("vector", "LD", st["cmq_ld"])
                        iv = em.emit("vector", lambda E, pb=pb, sec=sec:
                                     E.tensor_copy(impac[:, sec * 256:(sec + 1) * 256],
                                                   pb[:, 0:256]), "VE", 1)
                    else:
                        iv = em.emit("vector", lambda E, pb=pb, sec=sec:
                                     E.tensor_add(impac[:, sec * 256:(sec + 1) * 256],
                                                  impac[:, sec * 256:(sec + 1) * 256],
                                                  pb[:, 0:256]), "VE", 1)
                    mv = em.emit("vector", lambda E, psl=psl, sec=sec:
                                 E.tensor_mul(prb[psl + sec][:, :],
                                              prb[psl + sec][:, :],
                                              cmq[:, sec * 256:(sec + 1) * 256]),
                                 "VE", 1)
                    st["qk_ve"][sec] = iv
                    st["prb_mve"][psl + sec] = mv
                pbm = PSF[2]
                if g == 0:
                    wbank("tensor", 2)
                em.wait("tensor", "SE", st["pmem_se"][1 - sl])
                mmv = em.emit("tensor", lambda E, pbm=pbm, hh=hh, g=g: E.matmul(
                    pbm[0:1, 0:256], memkb[:, hh:hh + 1],
                    rqu[:, g * 256:(g + 1) * 256], start=True, stop=True), "MM", 1)
                em.wait("scalar", "MM", mmv)
                em.wait("scalar", "MM", st["pmem_mm"][sl])
                se = em.emit("scalar", lambda E, pbm=pbm, sl=sl: E.activation(
                    pmem[0:1, sl * 256:(sl + 1) * 256], pbm[0:1, 0:256],
                    AF.Exp, scale=SCALE, bias=eshv[0:1, :]), "SE", 1)
                st["pmem_se"][sl] = se
                if g >= 1:
                    comp_pv(g - 1)
            comp_pv(G - 1)
            st["bank"][2] = ("SE", st["pmem_se"][(G - 1) % 2])
            st["rqu_mm"] = em.cnt["MM"]
            st["cmq_ve"] = em.cnt["VE"]

        def selA(hh, qt):
            em.wait("vector", "LD", st["cnq_ld"])
            for sec in range(2):
                em.emit("vector", lambda E, sec=sec: E.scalar_tensor_tensor(
                    impac[:, sec * 256:(sec + 1) * 256],
                    impac[:, sec * 256:(sec + 1) * 256],
                    SCALE / G, cnq[:, sec * 256:(sec + 1) * 256],
                    ALU.mult, ALU.add), "VE", 1)
            fin_ve = em.cnt["VE"]
            st["cnq_ve"] = fin_ve
            em.wait("tensor", "VE", fin_ve)
            for sj in range(2):
                for sec in range(2):
                    emit_T(impac[:, sec * 256 + sj * 128:sec * 256 + sj * 128 + 128],
                           impq[:, sj * 256 + sec * 128:sj * 256 + sec * 128 + 128],
                           f32=True)
            st["impacT_mm"] = em.cnt["MM"]
            em.wait("vector", "MM", st["selq_mm"])
            for sj in range(2):
                iq = impq[:, sj * 256:(sj + 1) * 256]
                em.wait("vector", "VE", em.cnt["VE"])
                def topk(E, iq=iq):
                    vch(E, E.tensor_copy(sele[:, :], iq))
                    for it in range(4):
                        vch(E, E.reduce_max(selm[:, it:it + 1], sele[:, :],
                                            axis=AX.X))
                        if it < 3:
                            vch(E, E.tensor_scalar(selw[:, :], sele[:, :],
                                                   selm[:, it:it + 1], None,
                                                   ALU.is_equal))
                            vch(E, E.scalar_tensor_tensor(
                                sele[:, :], selw[:, :], NEGB,
                                sele[:, :], ALU.mult, ALU.add))
                    return E.engine_nop()
                ve = em.emit("vector", topk, "VE", 1)
                em.wait("scalar", "VE", ve)
                se = em.emit("scalar", lambda E, iq=iq: E.activation(
                    sele[:, :], iq, AF.Exp, bias=eshv[:, :],
                    accum_out=selm[:, 4:5]), "SE", 1)
                em.wait("vector", "SE", se)
                def fmask(E, iq=iq, sj=sj):
                    vch(E, E.tensor_scalar_mul(selm[:, 5:6], selm[:, 4:5], 1e-10))
                    vch(E, E.tensor_scalar(selw[:, :], sele[:, :], selm[:, 5:6],
                                           None, ALU.is_gt))
                    vch(E, E.tensor_scalar(selq[:, sj * 256:(sj + 1) * 256], iq,
                                           selm[:, 3:4], None, ALU.is_ge))
                    return E.tensor_mul(selq[:, sj * 256:(sj + 1) * 256],
                                        selq[:, sj * 256:(sj + 1) * 256],
                                        selw[:, :])
                em.emit("vector", fmask, "VE", 1)
            st["fmask_ve"] = em.cnt["VE"]

        def selB(hh, qt):
            qcol = qt * 256
            ncH = 4 * qt + 4
            em.wait("tensor", "VE", st["fmask_ve"])
            em.wait("tensor", "LD", st["stm_ld"])
            em.wait("vector", "LD", st["stm_ld"])
            for sec in range(2):
                for sj in range(2):
                    emit_T(selq[:, sj * 256 + sec * 128:sj * 256 + sec * 128 + 128],
                           mfT[:, sec * 256 + sj * 128:sec * 256 + sj * 128 + 128])
            st["selq_mm"] = em.cnt["MM"]
            em.wait("sync", "VE", em.cnt["VE"])
            stm = em.emit("sync", lambda E, hh=hh, qcol=qcol: E.dma_start(
                maskfd[hh][:, qcol:qcol + 256].rearrange(
                    "(s p) q -> p s q", s=2),
                mfT.ap().rearrange("p (s q) -> p s q", s=2)), "LD", 16)
            st["stm_ld"] = stm
            em.wait("sync", "LD", stm)
            em.wait("sync", "VE", st["mask_ve"])
            em.wait("gpsimd", "VE", st["mask_ve"])
            em.wait("gpsimd", "LD", stm)
            for bb in range(8):
                src = maskfd[hh].rearrange("(c b) q -> b c q", b=8)[bb, 0:ncH]
                src = src.unsqueeze(0).broadcast_to([16, ncH, OWN])[
                    :, :, qcol:qcol + 256]
                eng, sem = ("sync", "LD") if bb < 4 else ("gpsimd", "GQ")
                em.emit(eng, lambda E, src=src, bb=bb, ncH=ncH: E.dma_start(
                    maskc[16 * bb:16 * (bb + 1), :].rearrange(
                        "p (c q) -> p c q", q=256)[:, 0:ncH], src), sem, 16)
            ldm_ld, ldm_gq = em.cnt["LD"], em.cnt["GQ"]
            em.wait("vector", "LD", ldm_ld)
            em.wait("vector", "GQ", ldm_gq)
            def adddiag(E, qt=qt):
                for sj in range(2):
                    c = 4 * qt + 2 * sj
                    E.tensor_add(
                        maskc[:, c * 256 + sj * 128:c * 256 + sj * 128 + 128],
                        maskc[:, c * 256 + sj * 128:c * 256 + sj * 128 + 128],
                        sm[:, 384:512])
                return E.engine_nop()
            st["dgv"] = em.emit("vector", adddiag, "VE", 1)

        def fine(hh, qt):
            s = qt % 2
            ncH = 4 * qt + 4

            def slide_hits(c):
                hits = []
                for sj in range(2):
                    base = 4 * qt + 2 * sj
                    if c == base - 1:
                        hits.append((sj, sm[:, 0:128]))
                    elif c == base:
                        hits.append((sj, sm[:, 128:256]))
                    elif c == base + 1:
                        hits.append((sj, sm[:, 256:384]))
                return hits

            for g in range(G):
                def fine_pv(c):
                    cs = c % 2
                    qlo = 0 if c <= 4 * qt + 1 else 1
                    em.wait("tensor", "VE", st["pf_ve"][cs])
                    em.wait("tensor", "GQ", st["vaug_gq"])
                    em.wait("tensor", "PO", st["pov"])
                    if c == 0:
                        wbank("tensor", 2)
                        wbank("tensor", 3)
                    blds = []
                    for sj in range(2):
                        if qlo <= sj and c <= 4 * qt + 2 * sj + 1:
                            blds.append(lambda E, sj=sj, c=c, cs=cs: E.matmul(
                                PSF[2 + sj][:, 0:129],
                                pfin[cs][:, sj * 128:(sj + 1) * 128],
                                vaug[:, c * 132:c * 132 + 129],
                                start=(c == 0), stop=False,
                                skip_group_check=True))
                    for sj, _w in slide_hits(c):
                        blds.append(lambda E, sj=sj, c=c, cs=cs: E.matmul(
                            PSF[2 + sj][:, 132:261],
                            psli[cs][:, sj * 128:(sj + 1) * 128],
                            vaug[:, c * 132:c * 132 + 129],
                            start=False, stop=False, skip_group_check=True))
                    for b in blds[:-1]:
                        em.emit("tensor", b, "MM", None)
                    mmv = em.emit("tensor", blds[-1], "MM", 1)
                    st["pf_mm"][cs] = mmv

                for c in range(ncH):
                    cs = c % 2
                    qlo = 0 if c <= 4 * qt + 1 else 1
                    pb = PSF[cs]
                    em.wait("tensor", "SE", st["qk_se"][cs])
                    em.wait("tensor", "VE", st["qk_ve"][cs])
                    em.wait("tensor", "LD", st["rq_ld"][s])
                    mmv = em.emit("tensor", lambda E, pb=pb, c=c, g=g, qlo=qlo, s=s:
                                  E.matmul(pb[:, 0:256 - qlo * 128],
                                           rk[:, c * 128:(c + 1) * 128],
                                           rq2[s][:, g * 256 + qlo * 128:
                                                  (g + 1) * 256],
                                           start=True, stop=True), "MM", 1)
                    em.wait("scalar", "MM", mmv)
                    em.wait("scalar", "MM", st["prb_mm"][cs])
                    em.wait("scalar", "VE", st["prb_mve"][cs])
                    se = em.emit("scalar", lambda E, pb=pb, cs=cs, qlo=qlo:
                                 E.activation(prb[cs][:, qlo * 128:256],
                                              pb[:, 0:256 - qlo * 128], AF.Exp,
                                              scale=SCALE, bias=eshv[:, :]),
                                 "SE", 1)
                    st["qk_se"][cs] = se
                    em.wait("vector", "SE", se)
                    em.wait("vector", "VE", st["dgv"])
                    em.wait("vector", "MM", st["pf_mm"][cs])
                    ve = em.emit("vector", lambda E, cs=cs, c=c, qlo=qlo:
                                 E.tensor_mul(pfin[cs][:, qlo * 128:256],
                                              prb[cs][:, qlo * 128:256],
                                              maskc[:, c * 256 + qlo * 128:
                                                    (c + 1) * 256]), "VE", 1)
                    for sj, w in slide_hits(c):
                        ve = em.emit("vector", lambda E, cs=cs, sj=sj, w=w:
                                     E.tensor_mul(psli[cs][:, sj * 128:(sj + 1) * 128],
                                                  prb[cs][:, sj * 128:(sj + 1) * 128],
                                                  w), "VE", 1)
                    st["prb_mve"][cs] = ve
                    st["pf_ve"][cs] = ve
                    if c >= 1:
                        fine_pv(c - 1)
                fine_pv(ncH - 1)
                mmv = em.cnt["MM"]
                em.wait("vector", "MM", mmv)
                ves = []
                for sj, sub in enumerate((2 * qt, 2 * qt + 1)):
                    pbo = PSF[2 + sj]
                    head = hh * G + g
                    def fsev(E, pbo=pbo, sub=sub, head=head):
                        gb = sub * 96 + 3 * head
                        vch(E, E.reciprocal(nrm[:, 0:1], pbo[:, 128:129]))
                        vch(E, E.tensor_mul(nrm[:, 1:2], nrm[:, 0:1],
                                            gates[:, gb + 1:gb + 2]))
                        vch(E, E.reciprocal(nrm[:, 2:3], pbo[:, 260:261]))
                        vch(E, E.tensor_mul(nrm[:, 3:4], nrm[:, 2:3],
                                            gates[:, gb + 2:gb + 3]))
                        vch(E, E.tensor_scalar(mixt[:, 0:128], pbo[:, 0:128],
                                               nrm[:, 1:2], None, ALU.mult))
                        vch(E, E.tensor_scalar(mixt[:, 128:256], pbo[:, 132:260],
                                               nrm[:, 3:4], None, ALU.mult))
                        vch(E, E.tensor_add(mixt[:, 0:128], mixt[:, 0:128],
                                            mixt[:, 128:256]))
                        mcol = sub * D + head * 128
                        return E.tensor_add(mix[:, mcol:mcol + 128],
                                            mix[:, mcol:mcol + 128],
                                            mixt[:, 0:128])
                    ves.append(em.emit("vector", fsev, "VE", 1))
                st["bank"][2] = ("VE", ves[0])
                st["bank"][3] = ("VE", ves[1])
            st["mask_ve"] = em.cnt["VE"]
            st["rq_mm"][s] = em.cnt["MM"]
            st["rk_mm"] = em.cnt["MM"]
            st["vaug_mm"] = em.cnt["MM"]

        for hh in range(KVH):
            em.wait("sync", "MM", st["rk_mm"])
            st["rk_ld"] = em.emit("sync", lambda E, hh=hh: E.dma_start(
                rk[:, :], rkd[hh]), "LD", 16)
            em.wait("gpsimd", "MM", st["vaug_mm"])
            st["vaug_gq"] = em.emit("gpsimd", lambda E, hh=hh: E.dma_start(
                vaug[:, :], vtok3d[hh]), "GQ", 16)
            em.wait("gpsimd", "GQ", st["vaug_gq"])
            def vones(E):
                for c in range(32):
                    E.memset(vaug[:, c * 132 + 128:c * 132 + 129], 1.0)
                E.memset(cvaug[:, 128:129], 1.0)
                return E.memset(cvaug[:, 132 + 128:132 + 129], 1.0)
            st["pov"] = em.emit("gpsimd", vones, "PO", 1)
            em.wait("vector", "MM", st["cvaug_mm"])
            for sec in range(2):
                emit_T(cv_fm[:, hh * 256 + sec * 128:hh * 256 + (sec + 1) * 128],
                       cvaug[:, sec * 132:sec * 132 + 128])
            st["cv_ve"] = em.cnt["VE"]

            comp(hh, 0)
            selA(hh, 0)
            for qt in range(8):
                if qt < 7:
                    comp(hh, qt + 1)
                selB(hh, qt)
                fine(hh, qt)
                if qt < 7:
                    selA(hh, qt + 1)
        em.barrier()
        if d_maskf is not None:
            em.emit("sync", lambda E: E.dma_start(d_maskf, maskfd), "LD", 16)
        if d_mix is not None:
            em.emit("sync", lambda E: E.dma_start(d_mix, mix[:, :]), "LD", 16)
        em.barrier()
        for c in reversed(cm):
            c.__exit__(None, None, None)

    phase3()
    if upto < 4:
        em.wait("sync", "LD", em.cnt["LD"])
        em.finalize()
        return nc, dbg_out

    def phase4():
        cm = [
            nc.sbuf_tensor("p4_mixT", [128, 32 * 128], BF16),
            nc.sbuf_tensor("p4_w0", [128, 32 * 256], BF16),
            nc.sbuf_tensor("p4_w1", [128, 32 * 256], BF16),
            nc.sbuf_tensor("p4_xr0", [128, 256], F32),
            nc.sbuf_tensor("p4_xr1", [128, 256], F32),
            nc.sbuf_tensor("p4_xr2", [128, 256], F32),
            nc.sbuf_tensor("p4_xr3", [128, 256], F32),
            nc.sbuf_tensor("p4_x0", [128, 256], F32),
            nc.sbuf_tensor("p4_x1", [128, 256], F32),
            nc.sbuf_tensor("p4_x2", [128, 256], F32),
            nc.sbuf_tensor("p4_x3", [128, 256], F32),
        ]
        t = [c.__enter__() for c in cm]
        mixT, w0, w1, xr0, xr1, xr2, xr3, s0, s1, s2, s3 = t
        wcol = [w0, w1]; xr = [xr0, xr1, xr2, xr3]; x1s = [s0, s1, s2, s3]
        w_r = Rot(em, 2); xr_r = Rot(em, 4); x1_r = Rot(em, 4)
        for r in range(NT):
            em.wait("vector", "MM", mixT_guard[0])
            lastve = None
            for dc in range(32):
                lastve = emit_T(mix[:, r * D + dc * 128:r * D + dc * 128 + 128],
                                mixT[:, dc * 128:(dc + 1) * 128])
            mixT_ready = em.cnt["VE"]
            grow = 2 * r * 128
            for oc in range(16):
                i = r * 16 + oc
                wb = wcol[oc % 2]
                w_r.acquire(i, "gpsimd")
                src_w = ap["woT"][oc]
                gq = em.emit("gpsimd", lambda E, wb=wb, s=src_w: E.dma_start(
                    wb[:, :], s), "GQ", 16)
                pb = PSF[oc % 4]
                xr_r.acquire(i, "sync")
                ldr = em.emit("sync", lambda E, b=xr[oc % 4], grow=grow, oc=oc:
                              E.dma_start(b[:, :], x1d[grow:grow + 128,
                                                       oc * 256:(oc + 1) * 256]),
                              "LD", 16)
                em.wait("tensor", "GQ", gq)
                em.wait("tensor", "VE", mixT_ready)
                xr_r.acquire(i, "tensor")
                for dc in range(32):
                    em.emit("tensor", lambda E, pb=pb, dc=dc, wb=wb: E.matmul(
                        pb[:, 0:256], mixT[:, dc * 128:(dc + 1) * 128],
                        wb[:, dc * 256:(dc + 1) * 256],
                        start=(dc == 0), stop=(dc == 31)), "MM", 1 if dc == 31 else None)
                w_r.release(i, "MM", em.cnt["MM"])
                em.wait("vector", "MM", em.cnt["MM"])
                em.wait("vector", "LD", ldr)
                x1_r.acquire(i, "vector")
                ve = em.emit("vector", lambda E, d=x1s[oc % 4], pb=pb, b=xr[oc % 4]:
                             E.scalar_tensor_tensor(d[:, :], pb[:, 0:256], 1.0,
                                                    b[:, :], ALU.mult, ALU.add),
                             "VE", 1)
                xr_r.release(i, "VE", ve)
                em.wait("sync", "VE", ve)
                st = em.emit("sync", lambda E, d=x1s[oc % 4], r=r, oc=oc: E.dma_start(
                    x2d[r * 128:(r + 1) * 128, oc * 256:(oc + 1) * 256], d[:, :]),
                    "LD", 16)
                if d_x2 is not None:
                    st = em.emit("sync", lambda E, d=x1s[oc % 4], r=r, oc=oc:
                                 E.dma_start(d_x2[r * 128:(r + 1) * 128,
                                                  oc * 256:(oc + 1) * 256], d[:, :]),
                                 "LD", 16)
                x1_r.release(i, "LD", st)
            mixT_guard[0] = em.cnt["MM"]
        em.barrier()
        for c in reversed(cm):
            c.__exit__(None, None, None)

    mixT_guard = [0]
    phase4()
    mixcm.__exit__(None, None, None)
    if upto < 5:
        em.wait("sync", "LD", em.cnt["LD"])
        em.finalize()
        return nc, dbg_out

    phase_ffn(x2d, out, 2, tag="f5")
    em.wait("sync", "LD", em.cnt["LD"])
    em.finalize()
    return nc, dbg_out


def build_once(dbg=()):
    return build_program(dbg)


def kernel(**inputs):
    shared, per_core = prep_inputs(inputs)
    nc, _ = build_program()
    in_maps = []
    for c in range(8):
        m = dict(shared)
        m.update(per_core[c])
        in_maps.append(m)
    res = run_bass_kernel_spmd(nc, in_maps, core_ids=list(range(8)))
    outp = np.empty((B, S, D), dtype=np.float32)
    for c in range(8):
        b, hh = c // 2, c % 2
        o = np.asarray(res.results[c]["out"], dtype=np.float32)
        tb = (np.arange(NBLK) if hh == 0
              else np.arange(NBLK).reshape(-1, 2)[:, ::-1].reshape(-1))
        own_true = tb[0::2]
        outp[b].reshape(NBLK, 128, D)[own_true] = o.reshape(NT, 128, D)
    return outp


# version 11 (final)
# speedup vs baseline: 1.4515x; 1.1886x over previous
import numpy as np
import ml_dtypes
import concourse.bass as bass
import concourse.mybir as mybir
from concourse.bass_utils import run_bass_kernel_spmd

F32 = mybir.dt.float32
BF16 = mybir.dt.bfloat16
I32 = mybir.dt.int32
AF = mybir.ActivationFunctionType
ALU = mybir.AluOpType
AX = mybir.AxisListType

B, S, D = 4, 4096, 4096
H, KVH, G, DH = 32, 8, 4, 128
NBLK = S // 128
OWN = S // 2
NT = OWN // 128
NMEG, MEG = 4, 1024
SCALE = DH ** -0.5
ESH = 30.0
NEGB = -1.0e9
EPS = 1e-5

PI_POS = np.array([j // 2 if j % 2 == 0 else 64 + j // 2 for j in range(DH)])
PI_INV = np.argsort(PI_POS)


def prep_inputs(inp):
    w_qkv = np.asarray(inp["L0_attn_w_qkv"]); w_o = np.asarray(inp["L0_attn_w_o"])
    wffn = np.asarray(inp["L0_ffn_weight"])
    k_w1 = np.asarray(inp["L0_attn_k_w1"]); k_b1 = np.asarray(inp["L0_attn_k_b1"])
    k_w2 = np.asarray(inp["L0_attn_k_w2"]); k_b2 = np.asarray(inp["L0_attn_k_b2"])
    v_w1 = np.asarray(inp["L0_attn_v_w1"]); v_b1 = np.asarray(inp["L0_attn_v_b1"])
    v_w2 = np.asarray(inp["L0_attn_v_w2"]); v_b2 = np.asarray(inp["L0_attn_v_b2"])
    k_pos = np.asarray(inp["L0_attn_k_pos"]); v_pos = np.asarray(inp["L0_attn_v_pos"])
    mem_kv = np.asarray(inp["L0_attn_mem_kv"])
    w_gate = np.asarray(inp["L0_attn_w_gate"]); b_gate = np.asarray(inp["L0_attn_b_gate"])
    x = np.asarray(inp["L0_attn_hidden_states"])

    bf = ml_dtypes.bfloat16
    C = np.ascontiguousarray

    def tile_oc(wT, ocw):
        Din, O = wT.shape
        return C(wT.reshape(Din // 128, 128, O // ocw, ocw)
                 .transpose(2, 1, 0, 3).reshape(O // ocw, 128, -1))

    def tile_flat(wT, ocw):
        Din, O = wT.shape
        return C(wT.reshape(Din // 128, 128, O).transpose(1, 0, 2)
                 .reshape(128, -1))

    shared = {
        "wffnT": tile_oc(wffn.T, 256).astype(bf),
        "wffnT5": tile_oc(wffn.T, 512).astype(bf),
        "wqT": tile_oc(w_qkv[0:H * DH].T, 256).astype(bf),
        "wkT": tile_oc(w_qkv[H * DH:H * DH + KVH * DH].T, 256).astype(bf),
        "wvT": tile_oc(w_qkv[H * DH + KVH * DH:].T, 256).astype(bf),
        "wgT": tile_flat(w_gate.T, 96),
        "woT": tile_oc(w_o.T, 256),
        "w1kT": tile_oc(k_w1.T, 128),
        "w1vT": tile_oc(v_w1.T, 128),
        "w2kT": tile_flat(k_w2.T, 128),
        "w2vT": tile_flat(v_w2.T, 128),
        "kb1": C(k_b1.reshape(16, 128).T),
        "vb1": C(v_b1.reshape(16, 128).T),
        "kb2p": C(k_b2.reshape(128, 1)),
        "vb2": C(v_b2.reshape(128, 1)),
        "kpT": C(np.transpose(k_pos, (2, 0, 1)).reshape(128, 128)),
        "vpT": C(np.transpose(v_pos, (2, 0, 1)).reshape(128, 128)),
        "memk": C(mem_kv[0][:, 0, :].T),
        "memv": C(mem_kv[1][:, 0, :].reshape(1, 8 * 128)),
        "bgate": C(b_gate.reshape(1, 96)),
        "identb": np.eye(128, dtype=bf),
        "identf": np.eye(128, dtype=np.float32),
    }
    kk = np.arange(128)[:, None]; qq = np.arange(128)[None, :]
    diag = ((kk <= qq) & (kk >= qq - 64)).astype(np.float32)
    prev = (kk >= qq + 64).astype(np.float32)
    zero = np.zeros((128, 128), np.float32)
    shared["fmdiag"] = (((kk // 16) == (qq // 16)) & (kk <= qq)).astype(bf)

    inv = 1.0 / 10000.0 ** (np.arange(0, DH, 2, dtype=np.float64) / DH)
    per_core = []
    for c in range(8):
        b, hh = c // 2, c % 2
        if hh == 0:
            tb = np.arange(NBLK)
        else:
            tb = np.arange(NBLK).reshape(-1, 2)[:, ::-1].reshape(-1)
        xp = C(x[b].reshape(NBLK, 128, D)[tb].reshape(S, D))
        pos = (tb[:, None] * 128 + np.arange(128)[None, :]).reshape(-1)
        ang = pos[:, None].astype(np.float64) * inv[None, :]
        cosT = C(np.cos(ang).astype(bf)); sinT = C(np.sin(ang).astype(bf))
        gq = pos.reshape(NBLK, 128)[0::2].reshape(-1)
        jb = np.arange(256)
        tstart = tb[jb // 8] * 128 + (jb % 8) * 16
        vis = (tstart[:, None] + 16 <= gq[None, :])
        cmask = np.ones((257, OWN), np.float32); cmask[1:] = vis
        cneg = np.where(vis, 0.0, NEGB).astype(bf)
        per_core.append({
            "x_full": xp, "cosT": cosT, "sinT": sinT,
            "cmask": cmask.astype(bf), "cneg": C(cneg),
            "smA": (prev if hh == 0 else zero).astype(bf),
            "smB": diag.astype(bf),
            "smC": (zero if hh == 0 else prev).astype(bf),
        })
    return shared, per_core


ENGINES = ("sync", "tensor", "vector", "scalar", "gpsimd")


class Em:
    def __init__(self, nc):
        self.nc = nc
        self.ops = {e: [] for e in ENGINES}
        self.sem = {}
        self.cnt = {}
        for s in ("LD", "GQ", "MM", "VE", "SE", "PO", "WC"):
            self.sem[s] = nc.alloc_semaphore(s)
            self.cnt[s] = 0
        self.last_wait = {e: {} for e in ENGINES}

    def on(self, eng, fn):
        self.ops[eng].append(fn)

    def wait(self, eng, sem, val):
        if val is None or val <= 0:
            return
        lw = self.last_wait[eng]
        if lw.get(sem, 0) >= val:
            return
        lw[sem] = val
        s = self.sem[sem]
        self.on(eng, lambda E: E.wait_ge(s, val))

    def emit(self, eng, build, sem=None, n=1):
        if sem is None or n is None:
            self.on(eng, lambda E: build(E))
            return None
        s = self.sem[sem]
        self.cnt[sem] += n
        v = self.cnt[sem]
        self.on(eng, lambda E: build(E).then_inc(s, n))
        return v

    def barrier(self):
        vals = dict(self.cnt)
        for e in ENGINES:
            for s, v in vals.items():
                self.wait(e, s, v)

    def finalize(self):
        with self.nc.Block() as blk:
            blk.sync(lambda E: [f(E) for f in self.ops["sync"]])
            blk.tensor(lambda E: [f(E) for f in self.ops["tensor"]])
            blk.vector(lambda E: [f(E) for f in self.ops["vector"]])
            blk.scalar(lambda E: [f(E) for f in self.ops["scalar"]])
            blk.gpsimd(lambda E: [f(E) for f in self.ops["gpsimd"]])


class Rot:
    def __init__(self, em, n):
        self.em, self.n = em, n
        self.rel = [None] * n

    def acquire(self, i, eng):
        r = self.rel[i % self.n]
        if r:
            for s, v in r:
                self.em.wait(eng, s, v)

    def release(self, i, sem, val):
        self.rel[i % self.n] = [(sem, val)]

    def release_many(self, i, rels):
        self.rel[i % self.n] = list(rels)


def build_program(dbg=(), upto=5):
    nc = bass.Bass("TRN2", target_bir_lowering=False)
    dbg = set(dbg)
    em = Em(nc)

    ap = {}
    def ein(name, shape, dtype):
        ap[name] = nc.dram_tensor(name, shape, dtype, kind="ExternalInput").ap()
    ein("x_full", [S, D], F32)
    ein("wffnT", [16, 128, 32 * 256], BF16)
    ein("wffnT5", [8, 128, 32 * 512], BF16)
    ein("wqT", [16, 128, 32 * 256], BF16); ein("wkT", [4, 128, 32 * 256], BF16)
    ein("wvT", [4, 128, 32 * 256], BF16); ein("wgT", [128, 32 * 96], F32)
    ein("woT", [16, 128, 32 * 256], F32)
    ein("w1kT", [16, 128, 16 * 128], F32); ein("w1vT", [16, 128, 16 * 128], F32)
    ein("w2kT", [128, 16 * 128], F32); ein("w2vT", [128, 16 * 128], F32)
    ein("kb1", [128, 16], F32); ein("vb1", [128, 16], F32)
    ein("kb2p", [128, 1], F32); ein("vb2", [128, 1], F32)
    ein("kpT", [128, 128], F32); ein("vpT", [128, 128], F32)
    ein("memk", [128, 8], F32); ein("memv", [1, 8 * 128], F32)
    ein("bgate", [1, 96], F32)
    ein("cosT", [S, 64], BF16); ein("sinT", [S, 64], BF16)
    ein("identb", [128, 128], BF16); ein("identf", [128, 128], F32)
    ein("smA", [128, 128], BF16); ein("smB", [128, 128], BF16)
    ein("smC", [128, 128], BF16); ein("fmdiag", [128, 128], BF16)
    ein("cmask", [257, OWN], BF16); ein("cneg", [256, OWN], BF16)

    out = nc.dram_tensor("out", [OWN, D], F32, kind="ExternalOutput").ap()
    x1d = nc.dram_tensor("x1d", [S, D], F32, kind="Internal").ap()
    f1hd = nc.dram_tensor("f1hd", [S, D], F32, kind="Internal").ap()
    x2d = nc.dram_tensor("x2d", [OWN, D], F32, kind="Internal").ap()
    hs2Td = nc.dram_tensor("hs2Td", [NMEG, 128, 32 * MEG], BF16, kind="Internal").ap()
    qro3 = nc.dram_tensor("qro3", [8, KVH, 128, 1024], BF16, kind="Internal").ap()
    qno3 = nc.dram_tensor("qno3", [8, KVH, 128, 1024], BF16, kind="Internal").ap()
    rkd = nc.dram_tensor("rkd", [KVH, 128, S], BF16, kind="Internal").ap()
    vtok3d = nc.dram_tensor("vtok3", [KVH, 128, 32 * 132], BF16, kind="Internal").ap()
    maskfd = nc.dram_tensor("maskfd", [KVH, 256, OWN], BF16, kind="Internal").ap()
    wo_bfd = nc.dram_tensor("wo_bf", [16, 128, 32 * 256], BF16, kind="Internal").ap()
    attnd = nc.dram_tensor("attnd", [OWN, D], BF16, kind="Internal").ap()

    dbg_out = {}
    def dout(name, shape, dtype=F32):
        if name in dbg:
            dbg_out[name] = nc.dram_tensor("dbg_" + name, shape, dtype,
                                           kind="ExternalOutput").ap()
        return dbg_out.get(name)
    d_x1 = dout("x1", [S, D])
    d_hs0 = dout("hs0", [128, D], BF16)
    d_hsT0 = dout("hsT0", [128, 32 * MEG], BF16)
    d_w0 = dout("w0", [128, 32 * 256], BF16)
    d_q = dout("q", [8, KVH, 128, 1024], BF16)
    d_rk = dout("rk", [KVH, 128, S], BF16)
    d_vtok = dout("vtok", [KVH, 128, 32 * 132], BF16)
    d_ck = dout("ck", [128, KVH * 256], BF16)
    d_cv = dout("cv", [128, KVH * 256], BF16)
    d_gates = dout("gates", [128, NT * 96])
    d_maskf = dout("maskf", [KVH, 256, OWN], BF16)
    d_impq = dout("impq", [KVH, 8, 128, 512])
    d_csim = dout("csim", [G, 2, 128, 256])
    d_mix = dout("mix", [128, NT * D], BF16)
    d_x2 = dout("x2", [OWN, D])

    AR = lambda name, shape, dt: nc.alloc_sbuf_tensor(name, shape, dt, side="right")
    identb = AR("c_identb", [128, 128], BF16)
    identf = AR("c_identf", [128, 128], F32)
    onesrow = AR("c_onesrow", [1, 128], BF16)
    gates = AR("c_gates", [128, NT * 96], F32)
    ck_fm = AR("c_ck_fm", [128, KVH * 256], BF16)
    cv_fm = AR("c_cv_fm", [128, KVH * 256], BF16)
    memkb = AR("c_memkb", [128, 8], BF16)
    memvaug = AR("c_memvaug", [1, 8 * 132], BF16)
    eshv = AR("c_eshv", [128, 1], F32)

    VS = nc.alloc_semaphore("VS")
    SS = nc.alloc_semaphore("SS")
    _vsn = [0]; _ssn = [0]

    def vch(E, inst):
        _vsn[0] += 1
        inst.then_inc(VS, 1)
        E.wait_ge(VS, _vsn[0])
        return inst

    def sch(E, inst):
        _ssn[0] += 1
        inst.then_inc(SS, 1)
        E.wait_ge(SS, _ssn[0])
        return inst

    PS_ = nc.alloc_semaphore("PS")
    _psn = [0]

    def vchp(E, inst):
        _psn[0] += 1
        inst.then_inc(PS_, 1)
        E.wait_ge(PS_, _psn[0])
        return inst

    PSF = [nc.alloc_psum_tensor(f"psf{i}", [128, 512], F32) for i in range(5)]
    PTB = [nc.alloc_psum_tensor(f"ptb{i}", [128, 128], BF16) for i in range(2)]
    PTF = nc.alloc_psum_tensor("ptf", [128, 128], F32)
    tb_slots = Rot(em, 2)
    tf_slots = Rot(em, 1)

    em.emit("sync", lambda E: E.dma_start(identb[:, :], ap["identb"]), "LD", 16)
    em.emit("sync", lambda E: E.dma_start(identf[:, :], ap["identf"]), "LD", 16)
    em.emit("gpsimd", lambda E: E.dma_start(memkb[:, :], ap["memk"]), "GQ", 16)
    em.emit("gpsimd", lambda E: E.dma_start(
        memvaug.ap().rearrange("p (h c) -> p h c", c=132)[:, :, 0:128],
        ap["memv"].rearrange("p (h c) -> p h c", c=128)), "GQ", 16)
    def _ones(E):
        E.memset(onesrow[:, :], 1.0)
        E.memset(eshv[:, :], -ESH)
        for hh in range(8):
            E.memset(memvaug[:, hh * 132 + 128:hh * 132 + 129], 1.0)
        return E.engine_nop()
    em.emit("gpsimd", _ones, "PO", 1)
    em.barrier()

    def emit_ln(xin_t, hout_t, stat_t, sq_t, prev_se):
        if prev_se is not None:
            em.wait("vector", "SE", prev_se)
        def vstats(E):
            vch(E, E.reduce_sum(stat_t[:, 0:1], xin_t[:, :], axis=AX.X))
            vch(E, E.scalar_tensor_tensor(sq_t[:, :], xin_t[:, :], 1.0, xin_t[:, :],
                                          ALU.bypass, ALU.mult,
                                          accum_out=stat_t[:, 1:2]))
            E.tensor_scalar_mul(stat_t[:, 0:1], stat_t[:, 0:1], 1.0 / D)
            vch(E, E.tensor_scalar_mul(stat_t[:, 1:2], stat_t[:, 1:2], 1.0 / D))
            vch(E, E.scalar_tensor_tensor(stat_t[:, 2:3], stat_t[:, 0:1], 1.0,
                                          stat_t[:, 0:1], ALU.bypass, ALU.mult))
            vch(E, E.tensor_sub(stat_t[:, 1:2], stat_t[:, 1:2], stat_t[:, 2:3]))
            return E.tensor_scalar_add(stat_t[:, 1:2], stat_t[:, 1:2], EPS)
        ve = em.emit("vector", vstats, "VE", 1)
        em.wait("scalar", "VE", ve)
        def sr0(E):
            sch(E, E.activation(stat_t[:, 3:4], stat_t[:, 1:2], AF.Ln))
            return E.activation(stat_t[:, 3:4], stat_t[:, 3:4], AF.Exp, scale=-0.5)
        se0 = em.emit("scalar", sr0, "SE", 1)
        em.wait("vector", "SE", se0)
        def newton(E):
            vch(E, E.tensor_mul(stat_t[:, 5:6], stat_t[:, 3:4], stat_t[:, 3:4]))
            vch(E, E.tensor_mul(stat_t[:, 5:6], stat_t[:, 5:6], stat_t[:, 1:2]))
            vch(E, E.tensor_scalar(stat_t[:, 5:6], stat_t[:, 5:6], -0.5, 1.5,
                                   ALU.mult, ALU.add))
            return E.tensor_mul(stat_t[:, 3:4], stat_t[:, 3:4], stat_t[:, 5:6])
        ven = em.emit("vector", newton, "VE", 1)
        em.wait("scalar", "VE", ven)
        def sapply(E):
            sch(E, E.mul(stat_t[:, 4:5], stat_t[:, 0:1], -1.0))
            sch(E, E.mul(stat_t[:, 4:5], stat_t[:, 4:5], stat_t[:, 3:4]))
            return E.activation(hout_t[:, :], xin_t[:, :], AF.Identity,
                                bias=stat_t[:, 4:5], scale=stat_t[:, 3:4])
        return em.emit("scalar", sapply, "SE", 1)

    def emit_T(src, dst, se_wait=None, f32=False, eng="vector"):
        if f32:
            slots, ident = tf_slots, identf
            i = emit_T.ctr[True] = emit_T.ctr.get(True, 0) + 1
            psum = PTF
            eng = "vector"
        else:
            slots, ident = tb_slots, identb
            i = emit_T.ctr[False] = emit_T.ctr.get(False, 0) + 1
            psum = PTB[i % 2]
        slots.acquire(i, "tensor")
        if se_wait is not None:
            em.wait("tensor", "SE", se_wait)
        mmv = em.emit("tensor", lambda E, s=src, ps=psum, idt=ident:
                      E.transpose(ps[:, :], s, idt[:, :]), "MM", 1)
        if eng == "scalar":
            em.wait("scalar", "MM", mmv)
            se = em.emit("scalar", lambda E, d=dst, ps=psum:
                         E.activation(d, ps[:, :], AF.Identity), "SE", 1)
            slots.release(i, "SE", se)
            return se
        em.wait("vector", "MM", mmv)
        ve = em.emit("vector", lambda E, d=dst, ps=psum:
                     E.tensor_copy(d, ps[:, :]), "VE", 1)
        slots.release(i, "VE", ve)
        return ve
    emit_T.ctr = {}

    def phase_ffn(src_ap, dst_ap, nmeg, store_hsT=None, d_tap=None, tag="f",
                  row_src=None, chunk_src=None, src2_ap=None, x2_ap=None):
        OCW, NOC = 256, 16
        if row_src is None:
            row_src = lambda row0: src_ap[row0:row0 + 128, :]
        if chunk_src is None:
            chunk_src = src_ap
        xin = [nc.sbuf_tensor(f"{tag}_xin{i}", [128, D], F32) for i in range(2)]
        hs = [nc.sbuf_tensor(f"{tag}_hs{i}", [128, D], BF16) for i in range(2)]
        sqf = nc.sbuf_tensor(f"{tag}_sqf", [128, D], F32)
        stat = nc.sbuf_tensor(f"{tag}_stat", [128, 8], F32)
        hsT = nc.sbuf_tensor(f"{tag}_hsT", [128, 32 * MEG], BF16)
        wcol = [nc.sbuf_tensor(f"{tag}_w{i}", [128, 32 * OCW], BF16) for i in range(2)]
        xr = [nc.sbuf_tensor(f"{tag}_xr{i}", [128, OCW], F32) for i in range(4)]
        x1s = [nc.sbuf_tensor(f"{tag}_x1{i}", [128, OCW], F32) for i in range(4)]
        ctxs = xin + hs + [sqf, stat, hsT] + wcol + xr + x1s
        tens = [c.__enter__() for c in ctxs]
        (xin0, xin1, hs0, hs1b, sqf, stat, hsT, w0, w1, xr0, xr1, xr2, xr3,
         s0, s1, s2, s3) = tens
        xin = [xin0, xin1]; hs = [hs0, hs1b]; wcol = [w0, w1]
        xr = [xr0, xr1, xr2, xr3]; x1s = [s0, s1, s2, s3]

        xin_r = Rot(em, 2); hs_r = Rot(em, 2); w_r = Rot(em, 2)
        xr_r = Rot(em, 4); x1_r = Rot(em, 4)
        prev_se = None
        sq_ve = [0]
        x2st_meg = [0]
        for m in range(nmeg):
            em.wait("vector", "MM", em.cnt["MM"])
            em.wait("scalar", "MM", em.cnt["MM"])
            for r in range(8):
                i = m * 8 + r
                row = m * MEG + r * 128
                xin_r.acquire(i, "sync")
                ld = em.emit("sync", lambda E, row=row, b=xin[i % 2]:
                             E.dma_start(b[:, :], row_src(row)), "LD", 16)
                x2st = None
                if src2_ap is not None:
                    em.wait("gpsimd", "VE", sq_ve[0])
                    gq = em.emit("gpsimd", lambda E, row=row: E.dma_start(
                        sqf[:, :], src2_ap[row:row + 128, :]), "GQ", 16)
                    em.wait("vector", "LD", ld)
                    em.wait("vector", "GQ", gq)
                    ve = em.emit("vector", lambda E, b=xin[i % 2]: E.tensor_add(
                        b[:, :], b[:, :], sqf[:, :]), "VE", 1)
                    em.wait("sync", "VE", ve)
                    x2st = em.emit("sync", lambda E, row=row, b=xin[i % 2]:
                                   E.dma_start(x2_ap[row:row + 128, :], b[:, :]),
                                   "LD", 16)
                    x2st_meg[0] = x2st
                em.wait("vector", "LD", ld)
                hs_r.acquire(i, "vector")
                hs_r.acquire(i, "scalar")
                prev_se = emit_ln(xin[i % 2], hs[i % 2], stat, sqf, prev_se)
                sq_ve[0] = em.cnt["VE"]
                if x2st is None:
                    xin_r.release(i, "SE", prev_se)
                else:
                    xin_r.release_many(i, [("SE", prev_se), ("LD", x2st)])
                if i == 0 and tag == "f" and dbg_out.get("hs0") is not None:
                    em.wait("sync", "SE", prev_se)
                    em.emit("sync", lambda E: E.dma_start(dbg_out["hs0"], hs[0][:, :]),
                            "LD", 16)
                for dc in range(32):
                    emit_T(hs[i % 2][:, dc * 128:(dc + 1) * 128],
                           hsT[:, dc * MEG + r * 128:dc * MEG + r * 128 + 128],
                           se_wait=prev_se,
                           eng=("vector" if dc % 2 == 0 else "scalar"))
                hs_r.release_many(i, [("VE", em.cnt["VE"]), ("SE", em.cnt["SE"])])
            hsT_ready = em.cnt["VE"]
            hsT_ready_se = em.cnt["SE"]
            if m == 0 and tag == "f" and dbg_out.get("hsT0") is not None:
                em.wait("sync", "VE", hsT_ready)
                em.emit("sync", lambda E: E.dma_start(dbg_out["hsT0"], hsT[:, :]),
                        "LD", 16)
            if store_hsT is not None:
                em.wait("sync", "VE", hsT_ready)
                em.wait("sync", "SE", hsT_ready_se)
                em.emit("sync", lambda E, m=m: E.dma_start(store_hsT[m], hsT[:, :]),
                        "LD", 16)
            for oc in range(NOC):
                wb = wcol[oc % 2]
                w_r.acquire(m * NOC + oc, "gpsimd")
                src_w = ap["wffnT"][oc]
                gq = em.emit("gpsimd", lambda E, wb=wb, s=src_w: E.dma_start(
                    wb[:, :], s), "GQ", 16)
                if m == 0 and oc == 0 and tag == "f" and dbg_out.get("w0") is not None:
                    em.wait("sync", "GQ", gq)
                    em.emit("sync", lambda E, wb=wb: E.dma_start(
                        dbg_out["w0"], wb[:, :]), "LD", 16)
                em.wait("tensor", "GQ", gq)
                em.wait("tensor", "VE", hsT_ready)
                em.wait("tensor", "SE", hsT_ready_se)
                for r in range(8):
                    i = (m * NOC + oc) * 8 + r
                    pb = PSF[r % 4]
                    row = m * MEG + r * 128
                    xr_r.acquire(i, "sync")
                    if src2_ap is not None:
                        em.wait("sync", "LD", x2st_meg[0])
                    ldr = em.emit("sync", lambda E, b=xr[r % 4], row=row, oc=oc:
                                  E.dma_start(b[:, :], chunk_src[row:row + 128,
                                                                oc * OCW:(oc + 1) * OCW]),
                                  "LD", 16)
                    xr_r.acquire(i, "tensor")
                    for dc in range(32):
                        em.emit("tensor", lambda E, pb=pb, dc=dc, r=r, wb=wb:
                                E.matmul(pb[:, 0:OCW],
                                         hsT[:, dc * MEG + r * 128:dc * MEG + r * 128 + 128],
                                         wb[:, dc * OCW:(dc + 1) * OCW],
                                         start=(dc == 0), stop=(dc == 31)),
                                "MM", 1 if dc == 31 else None)
                    mmv = em.cnt["MM"]
                    em.wait("vector", "MM", mmv)
                    em.wait("vector", "LD", ldr)
                    x1_r.acquire(i, "vector")
                    ve = em.emit("vector", lambda E, d=x1s[r % 4], pb=pb, b=xr[r % 4]:
                                 E.scalar_tensor_tensor(d[:, :], pb[:, 0:OCW], 0.5,
                                                        b[:, :], ALU.mult, ALU.add),
                                 "VE", 1)
                    xr_r.release(i, "VE", ve)
                    em.wait("sync", "VE", ve)
                    st = em.emit("sync", lambda E, d=x1s[r % 4], row=row, oc=oc:
                                 E.dma_start(dst_ap[row:row + 128,
                                                    oc * OCW:(oc + 1) * OCW], d[:, :]),
                                 "LD", 16)
                    if d_tap is not None:
                        st = em.emit("sync", lambda E, d=x1s[r % 4], row=row, oc=oc:
                                     E.dma_start(d_tap[row:row + 128,
                                                       oc * OCW:(oc + 1) * OCW], d[:, :]),
                                     "LD", 16)
                    x1_r.release(i, "LD", st)
                w_r.release(m * NOC + oc, "MM", em.cnt["MM"])
        em.barrier()
        for c in reversed(ctxs):
            c.__exit__(None, None, None)

    def phase1():
        OCW, NOC = 512, 8
        cm = [
            nc.sbuf_tensor("f_xin0", [128, D], F32),
            nc.sbuf_tensor("f_xin1", [128, D], F32),
            nc.sbuf_tensor("f_hs", [128, D], BF16),
            nc.sbuf_tensor("f_stat", [128, 8], F32),
            nc.sbuf_tensor("f_hsT", [128, 32 * MEG], BF16),
            nc.sbuf_tensor("f_w0", [128, 32 * OCW], BF16),
            nc.sbuf_tensor("f_w1", [128, 32 * OCW], BF16),
            nc.sbuf_tensor("f_st0", [128, OCW], F32),
            nc.sbuf_tensor("f_st1", [128, OCW], F32),
            nc.sbuf_tensor("f_st2", [128, OCW], F32),
            nc.sbuf_tensor("f_st3", [128, OCW], F32),
        ]
        t = [c.__enter__() for c in cm]
        xin0, xin1, hs1, stat, hsT, w0, w1, st0, st1, st2, st3 = t
        xin = [xin0, xin1]
        wcol = [w0, w1]
        stg = [st0, st1, st2, st3]
        xin_r = Rot(em, 2)
        w_r = Rot(em, 2)
        st_r = Rot(em, 4)
        prev_se = None
        hsT_mm = [0]
        tmm_row = [0]
        for m in range(NMEG):
            em.wait("vector", "MM", hsT_mm[0])
            em.wait("scalar", "MM", hsT_mm[0])
            for r in range(8):
                i = m * 8 + r
                row = m * MEG + r * 128
                xin_r.acquire(i, "sync")
                ld = em.emit("sync", lambda E, row=row, b=xin[i % 2]:
                             E.dma_start(b[:, :], ap["x_full"][row:row + 128, :]),
                             "LD", 16)
                em.wait("vector", "LD", ld)
                em.wait("vector", "MM", tmm_row[0])
                prev_se = emit_ln(xin[i % 2], hs1, stat, hs1, prev_se)
                xin_r.release(i, "SE", prev_se)
                for dc in range(32):
                    emit_T(hs1[:, dc * 128:(dc + 1) * 128],
                           hsT[:, dc * MEG + r * 128:dc * MEG + r * 128 + 128],
                           se_wait=prev_se,
                           eng=("vector" if dc % 2 == 0 else "scalar"))
                tmm_row[0] = em.cnt["MM"]
            hsT_ready = em.cnt["VE"]
            hsT_ready_se = em.cnt["SE"]
            for oc in range(NOC):
                wb = wcol[oc % 2]
                w_r.acquire(m * NOC + oc, "gpsimd")
                src_w = ap["wffnT5"][oc]
                gq = em.emit("gpsimd", lambda E, wb=wb, s=src_w: E.dma_start(
                    wb[:, :], s), "GQ", 16)
                em.wait("tensor", "GQ", gq)
                em.wait("tensor", "VE", hsT_ready)
                em.wait("tensor", "SE", hsT_ready_se)
                for r in range(8):
                    i2 = (m * NOC + oc) * 8 + r
                    pb = PSF[r % 4]
                    row = m * MEG + r * 128
                    st_r.acquire(i2, "tensor")
                    st_r.acquire(i2, "scalar")
                    for dc in range(32):
                        em.emit("tensor", lambda E, pb=pb, dc=dc, r=r, wb=wb:
                                E.matmul(pb[:, 0:OCW],
                                         hsT[:, dc * MEG + r * 128:dc * MEG + r * 128 + 128],
                                         wb[:, dc * OCW:(dc + 1) * OCW],
                                         start=(dc == 0), stop=(dc == 31)),
                                "MM", 1 if dc == 31 else None)
                    mmv = em.cnt["MM"]
                    em.wait("scalar", "MM", mmv)
                    se = em.emit("scalar", lambda E, d=stg[r % 4], pb=pb:
                                 E.activation(d[:, :], pb[:, 0:OCW], AF.Identity,
                                              scale=0.5), "SE", 1)
                    em.wait("sync", "SE", se)
                    st = em.emit("sync", lambda E, d=stg[r % 4], row=row, oc=oc:
                                 E.dma_start(f1hd[row:row + 128,
                                                  oc * OCW:(oc + 1) * OCW], d[:, :]),
                                 "LD", 16)
                    st_r.release(i2, "LD", st)
                w_r.release(m * NOC + oc, "MM", em.cnt["MM"])
            hsT_mm[0] = em.cnt["MM"]
        em.barrier()
        for c in reversed(cm):
            c.__exit__(None, None, None)

    phase1()
    if upto < 2:
        em.wait("sync", "LD", em.cnt["LD"])
        em.finalize()
        return nc, dbg_out

    def phase2():
        cm = [
            nc.sbuf_tensor("p2_xin", [128, D], F32),
            nc.sbuf_tensor("p2_hs", [128, D], BF16),
            nc.sbuf_tensor("p2_stat", [128, 8], F32),
            nc.sbuf_tensor("p2_hsT", [128, 32 * MEG], BF16),
            nc.sbuf_tensor("p2_w", [128, 32 * 256], BF16),
            nc.sbuf_tensor("p2_kt0", [128, 256], BF16),
            nc.sbuf_tensor("p2_kt1", [128, 256], BF16),
            nc.sbuf_tensor("p2_rkr", [128, 256], BF16),
            nc.sbuf_tensor("p2_cosm", [128, 8 * 128], BF16),
            nc.sbuf_tensor("p2_kfm", [128, KVH * MEG], BF16),
            nc.sbuf_tensor("p2_vfm", [128, KVH * MEG], BF16),
            nc.sbuf_tensor("p2_cs", [128, 16 * 512], BF16),
            nc.sbuf_tensor("p2_hid", [128, 8 * 512], BF16),
            nc.sbuf_tensor("p2_w10", [128, 16 * 128], BF16),
            nc.sbuf_tensor("p2_w11", [128, 16 * 128], BF16),
            nc.sbuf_tensor("p2_w2k", [128, 16 * 128], BF16),
            nc.sbuf_tensor("p2_w2v", [128, 16 * 128], BF16),
            nc.sbuf_tensor("p2_ckst", [128, 512], BF16),
            nc.sbuf_tensor("p2_stg", [128, 256], BF16),
            nc.sbuf_tensor("p2_gt", [128, 96], F32),
            nc.sbuf_tensor("p2_wg", [128, 32 * 96], BF16),
            nc.sbuf_tensor("p2_bg", [1, 96], BF16),
            nc.sbuf_tensor("p2_kb", [128, 40], F32),
            nc.sbuf_tensor("p2_kp", [128, 128], F32),
            nc.sbuf_tensor("p2_vp", [128, 128], F32),
        ]
        t = [c.__enter__() for c in cm]
        (xin, hs, stat, hsT, wcb, kt0, kt1, rkr, cosm, kfm, vfm, cs, hid,
         w10, w11, w2k, w2v, ckst, stg, gt, wgTb, bgateb, kbs, kpT, vpT) = t
        w1b = [w10, w11]
        ktb = [kt0, kt1]
        em.emit("sync", lambda E: E.dma_start(kbs[:, 0:16], ap["kb1"]), "LD", 16)
        em.emit("sync", lambda E: E.dma_start(kbs[:, 16:32], ap["vb1"]), "LD", 16)
        em.emit("sync", lambda E: E.dma_start(kbs[:, 32:33], ap["kb2p"]), "LD", 16)
        em.emit("sync", lambda E: E.dma_start(kbs[:, 33:34], ap["vb2"]), "LD", 16)
        em.emit("sync", lambda E: E.dma_start(kpT[:, :], ap["kpT"]), "LD", 16)
        em.emit("sync", lambda E: E.dma_start(vpT[:, :], ap["vpT"]), "LD", 16)
        em.emit("gpsimd", lambda E: E.dma_start(bgateb[:, :], ap["bgate"]), "GQ", 16)
        em.emit("gpsimd", lambda E: E.dma_start(wgTb[:, :], ap["wgT"]), "GQ", 16)
        for nm, tt in [("w2kT", w2k), ("w2vT", w2v)]:
            em.emit("gpsimd", lambda E, tt=tt, nm=nm: E.dma_start(
                tt[:, :], ap[nm]), "GQ", 16)
        em.barrier()

        w1_r = Rot(em, 2)
        prev_se = [None]
        x1st = [0]
        mlp1_se = [None, None]
        mlp2_mm = [0]
        ck_evac_se = [None]
        cs_mm_guard = [0]
        w_mm = [0]
        kt_mm = [0, 0]
        rkr_mm = [0]
        ktt_ld = [0, 0]
        stg_ld = [0]
        gse_ = [0, 0]

        def rope_tile(E, dstt, srct, nh, cosAP, sinAP):
            sv = srct.ap().rearrange("p (h i t) -> p h t i", i=64, t=2)
            dv = dstt.ap().rearrange("p (h i t) -> p h t i", i=64, t=2)
            cb = cosAP.unsqueeze(1).broadcast_to([128, nh, 64])
            sb = sinAP.unsqueeze(1).broadcast_to([128, nh, 64])
            e = sv[:, :, 0, :]; o = sv[:, :, 1, :]
            rt_ = rtmp.ap().rearrange("p (h i) -> p h i", i=64)[:, 0:nh]
            E.tensor_tensor(dv[:, :, 0, :], e, cb, ALU.mult)
            vch(E, E.tensor_tensor(rt_, o, sb, ALU.mult))
            vch(E, E.tensor_sub(dv[:, :, 0, :], dv[:, :, 0, :], rt_))
            E.tensor_tensor(dv[:, :, 1, :], o, cb, ALU.mult)
            vch(E, E.tensor_tensor(rt_, e, sb, ALU.mult))
            return E.tensor_add(dv[:, :, 1, :], dv[:, :, 1, :], rt_)

        def compress(m, side):
            fmsrc = kfm if side == "k" else vfm
            posT = kpT if side == "k" else vpT
            b1col = kbs[:, 0:16] if side == "k" else kbs[:, 16:32]
            b2b = kbs[:, 32:33] if side == "k" else kbs[:, 33:34]
            b2w = w2k if side == "k" else w2v
            dst = ck_fm if side == "k" else cv_fm
            em.wait("vector", "MM", cs_mm_guard[0])
            def csb(E):
                last = None
                for u in range(16):
                    for hh2 in range(KVH):
                        s2 = fmsrc.ap().rearrange("p (h a b) -> p h a b",
                                                  h=KVH, b=16)[:, hh2, :, u]
                        last = E.tensor_scalar_add(
                            cs[:, u * 512 + hh2 * 64:u * 512 + (hh2 + 1) * 64], s2,
                            posT[:, hh2 * 16 + u:hh2 * 16 + u + 1])
                return last
            csv = em.emit("vector", csb, "VE", 1)
            pb2 = PSF[4]
            for half in range(2):
                for oci in range(8):
                    oc = half * 8 + oci
                    wb1 = w1b[oc % 2]
                    w1_r.acquire(oc, "gpsimd")
                    src_w = ap["w1kT" if side == "k" else "w1vT"][oc]
                    gq = em.emit("gpsimd", lambda E, wb1=wb1, s=src_w: E.dma_start(
                        wb1[:, :], s), "GQ", 16)
                    pb = PSF[2 + oc % 2]
                    em.wait("tensor", "GQ", gq)
                    em.wait("tensor", "VE", csv)
                    em.wait("tensor", "SE", mlp1_se[oc % 2] or 0)
                    for u in range(16):
                        em.emit("tensor", lambda E, pb=pb, u=u, wb1=wb1: E.matmul(
                            pb[:, :], wb1[:, u * 128:(u + 1) * 128],
                            cs[:, u * 512:(u + 1) * 512],
                            start=(u == 0), stop=(u == 15)),
                            "MM", 1 if u == 15 else None)
                    w1_r.release(oc, "MM", em.cnt["MM"])
                    em.wait("scalar", "MM", em.cnt["MM"])
                    em.wait("scalar", "MM", mlp2_mm[0])
                    se = em.emit("scalar", lambda E, oci=oci, pb=pb, b1col=b1col,
                                 oc=oc: E.activation(
                                     hid[:, oci * 512:(oci + 1) * 512], pb[:, :],
                                     AF.Relu, bias=b1col[:, oc:oc + 1]), "SE", 1)
                    mlp1_se[oc % 2] = se
                em.wait("tensor", "SE", em.cnt["SE"])
                if half == 0:
                    em.wait("tensor", "SE", ck_evac_se[0] or 0)
                for oci in range(8):
                    oc = half * 8 + oci
                    em.emit("tensor", lambda E, pb2=pb2, oci=oci, oc=oc, b2w=b2w:
                            E.matmul(pb2[:, :], b2w[:, oc * 128:(oc + 1) * 128],
                                     hid[:, oci * 512:(oci + 1) * 512],
                                     start=(oc == 0), stop=(oc == 15)),
                            "MM", 1 if oc == 15 else None)
                mlp2_mm[0] = em.cnt["MM"]
            em.wait("scalar", "MM", em.cnt["MM"])
            se = em.emit("scalar", lambda E, pb2=pb2, b2b=b2b: E.activation(
                ckst[:, :], pb2[:, :], AF.Identity, bias=b2b), "SE", 1)
            ck_evac_se[0] = se
            em.wait("vector", "SE", se)
            def scat(E, dst=dst, m=m):
                last = None
                for hh2 in range(KVH):
                    last = E.tensor_copy(
                        dst[:, hh2 * 256 + m * 64:hh2 * 256 + (m + 1) * 64],
                        ckst[:, hh2 * 64:(hh2 + 1) * 64])
                return last
            em.emit("vector", scat, "VE", 1)
            cs_mm_guard[0] = em.cnt["MM"]

        for m in range(NMEG):
            em.wait("vector", "MM", em.cnt["MM"])
            em.wait("scalar", "MM", em.cnt["MM"])
            cm_ld = em.emit("sync", lambda E, m=m: E.dma_start(
                cosm.ap()[:, 0:512].rearrange("p (r i) -> p r i", i=64),
                ap["cosT"][m * MEG:(m + 1) * MEG].rearrange(
                    "(r p) i -> p r i", p=128)), "LD", 16)
            cm_ld = em.emit("sync", lambda E, m=m: E.dma_start(
                cosm.ap()[:, 512:1024].rearrange("p (r i) -> p r i", i=64),
                ap["sinT"][m * MEG:(m + 1) * MEG].rearrange(
                    "(r p) i -> p r i", p=128)), "LD", 16)
            for r in range(8):
                row = m * MEG + r * 128
                if prev_se[0] is not None:
                    em.wait("sync", "SE", prev_se[0])
                em.wait("sync", "LD", x1st[0])
                ld = em.emit("sync", lambda E, row=row: E.dma_start(
                    xin[:, :], ap["x_full"][row:row + 128, :]), "LD", 16)
                em.wait("sync", "MM", em.cnt["MM"])
                em.wait("sync", "VE", em.cnt["VE"])
                ld2 = em.emit("sync", lambda E, row=row: E.dma_start(
                    cs.ap().bitcast(F32)[:, 0:D], f1hd[row:row + 128, :]),
                    "LD", 16)
                em.wait("vector", "LD", ld2)
                em.wait("vector", "MM", em.cnt["MM"])
                em.wait("scalar", "MM", em.cnt["MM"])
                va = em.emit("vector", lambda E: E.tensor_add(
                    xin[:, :], xin[:, :], cs.ap().bitcast(F32)[:, 0:D]), "VE", 1)
                em.wait("sync", "VE", va)
                x1st[0] = em.emit("sync", lambda E, row=row: E.dma_start(
                    x1d[row:row + 128, :], xin[:, :]), "LD", 16)
                prev_se[0] = emit_ln(xin, hs, stat, cs.ap().bitcast(F32),
                                     prev_se[0])
                for dc in range(32):
                    emit_T(hs[:, dc * 128:(dc + 1) * 128],
                           hsT[:, dc * MEG + r * 128:dc * MEG + r * 128 + 128],
                           se_wait=prev_se[0],
                           eng=("vector" if dc % 2 == 0 else "scalar"))
            hsT_ready = em.cnt["VE"]
            hsT_ready_se = em.cnt["SE"]

            for kvc in range(8):
                src_w = (ap["wkT"][kvc] if kvc < 4 else ap["wvT"][kvc - 4])
                em.wait("gpsimd", "MM", w_mm[0])
                em.wait("sync", "MM", w_mm[0])
                gq = em.emit("gpsimd", lambda E, s=src_w: E.dma_start(
                    wcb[:, 0:4096], s[:, 0:4096]), "GQ", 16)
                ldw = em.emit("sync", lambda E, s=src_w: E.dma_start(
                    wcb[:, 4096:8192], s[:, 4096:8192]), "LD", 16)
                em.wait("tensor", "GQ", gq)
                em.wait("tensor", "LD", ldw)
                em.wait("tensor", "VE", hsT_ready)
                em.wait("tensor", "SE", hsT_ready_se)
                for r in range(8):
                    pb = PSF[r % 2]
                    ktt = ktb[r % 2]
                    em.wait("tensor", "VE", kv_evac[r % 2] or 0)
                    for dc in range(32):
                        em.emit("tensor", lambda E, pb=pb, dc=dc, r=r: E.matmul(
                            pb[:, 0:256],
                            hsT[:, dc * MEG + r * 128:dc * MEG + r * 128 + 128],
                            wcb[:, dc * 256:(dc + 1) * 256],
                            start=(dc == 0), stop=(dc == 31)),
                            "MM", 1 if dc == 31 else None)
                    w_mm[0] = em.cnt["MM"]
                    em.wait("vector", "MM", em.cnt["MM"])
                    em.wait("vector", "MM", kt_mm[r % 2])
                    em.wait("vector", "LD", ktt_ld[r % 2])
                    em.wait("vector", "LD", cm_ld)
                    ve = em.emit("vector", lambda E, pb=pb, ktt=ktt:
                                 E.tensor_copy(ktt[:, :], pb[:, 0:256]), "VE", 1)
                    kv_evac[r % 2] = ve
                    h0 = 2 * kvc if kvc < 4 else 2 * (kvc - 4)
                    if kvc < 4:
                        em.wait("vector", "VE", ve)
                        em.wait("vector", "MM", rkr_mm[0])
                        ve2 = em.emit("vector", lambda E, ktt=ktt, r=r:
                                      rope_tile(E, rkr, ktt, 2,
                                                cosm.ap()[:, r * 64:r * 64 + 64],
                                                cosm.ap()[:, 512 + r * 64:
                                                          512 + r * 64 + 64]),
                                      "VE", 1)
                        em.wait("tensor", "VE", ve)
                        for j in range(2):
                            emit_T(ktt[:, j * 128:(j + 1) * 128],
                                   kfm[:, (h0 + j) * MEG + r * 128:
                                       (h0 + j) * MEG + r * 128 + 128])
                        em.wait("tensor", "VE", ve2)
                        em.wait("vector", "LD", stg_ld[0])
                        for j in range(2):
                            lastve = emit_T(rkr[:, j * 128:(j + 1) * 128],
                                            stg[:, j * 128:(j + 1) * 128])
                        rkr_mm[0] = em.cnt["MM"]
                        em.wait("sync", "VE", lastve)
                        st = em.emit("sync", lambda E, h0=h0, m=m, r=r: E.dma_start(
                            rkd[h0:h0 + 2, :, m * MEG + r * 128:
                                m * MEG + r * 128 + 128].rearrange(
                                    "h p c -> p h c"),
                            stg.ap().rearrange("p (h c) -> p h c", c=128)), "LD", 16)
                        stg_ld[0] = st
                        kt_mm[r % 2] = em.cnt["MM"]
                    else:
                        em.wait("sync", "VE", ve)
                        cpos = m * 8 + r
                        st = em.emit("sync", lambda E, h0=h0, cpos=cpos, ktt=ktt:
                                     E.dma_start(
                                         vtok3d[h0:h0 + 2, :,
                                                cpos * 132:cpos * 132 + 128]
                                         .rearrange("h p d -> p h d"),
                                         ktt.ap().rearrange("p (h d) -> p h d",
                                                            d=128)), "LD", 16)
                        em.wait("tensor", "VE", ve)
                        for j in range(2):
                            emit_T(ktt[:, j * 128:(j + 1) * 128],
                                   vfm[:, (h0 + j) * MEG + r * 128:
                                       (h0 + j) * MEG + r * 128 + 128])
                        kt_mm[r % 2] = em.cnt["MM"]
                        ktt_ld[r % 2] = st
            compress(m, "k")
            compress(m, "v")
            for qc in range(16):
                src_w = ap["wqT"][qc]
                em.wait("gpsimd", "MM", w_mm[0])
                em.wait("sync", "MM", w_mm[0])
                gq = em.emit("gpsimd", lambda E, s=src_w: E.dma_start(
                    wcb[:, 0:4096], s[:, 0:4096]), "GQ", 16)
                ldw = em.emit("sync", lambda E, s=src_w: E.dma_start(
                    wcb[:, 4096:8192], s[:, 4096:8192]), "LD", 16)
                em.wait("tensor", "GQ", gq)
                em.wait("tensor", "LD", ldw)
                for a in range(4):
                    r = 2 * a
                    tloc = m * 4 + a
                    pb = PSF[a % 2]
                    ktt = ktb[a % 2]
                    em.wait("tensor", "VE", kv_evac[a % 2] or 0)
                    for dc in range(32):
                        em.emit("tensor", lambda E, pb=pb, dc=dc, r=r: E.matmul(
                            pb[:, 0:256],
                            hsT[:, dc * MEG + r * 128:dc * MEG + r * 128 + 128],
                            wcb[:, dc * 256:(dc + 1) * 256],
                            start=(dc == 0), stop=(dc == 31)),
                            "MM", 1 if dc == 31 else None)
                    w_mm[0] = em.cnt["MM"]
                    em.wait("vector", "MM", em.cnt["MM"])
                    em.wait("vector", "MM", kt_mm[a % 2])
                    em.wait("vector", "LD", ktt_ld[a % 2])
                    ve = em.emit("vector", lambda E, pb=pb, ktt=ktt:
                                 E.tensor_copy(ktt[:, :], pb[:, 0:256]), "VE", 1)
                    kv_evac[a % 2] = ve
                    em.wait("vector", "VE", ve)
                    em.wait("vector", "MM", rkr_mm[0])
                    ve2 = em.emit("vector", lambda E, ktt=ktt, r=r:
                                  rope_tile(E, rkr, ktt, 2,
                                            cosm.ap()[:, r * 64:r * 64 + 64],
                                            cosm.ap()[:, 512 + r * 64:
                                                      512 + r * 64 + 64]),
                                  "VE", 1)
                    em.wait("tensor", "VE", ve2)
                    em.wait("vector", "LD", stg_ld[0])
                    for j in range(2):
                        lastve = emit_T(rkr[:, j * 128:(j + 1) * 128],
                                        stg[:, j * 128:(j + 1) * 128])
                    rkr_mm[0] = em.cnt["MM"]
                    em.wait("sync", "VE", lastve)
                    hh2 = (2 * qc) // 4
                    g0 = (2 * qc) % 4
                    qt2 = tloc // 2
                    hlf = tloc % 2
                    qdst = qro3[qt2, hh2].rearrange(
                        "p (g q) -> p g q", q=256)[:, g0:g0 + 2,
                                                   hlf * 128:hlf * 128 + 128]
                    st = em.emit("sync", lambda E, qdst=qdst: E.dma_start(
                        qdst, stg.ap().rearrange("p (h c) -> p h c", c=128)),
                        "LD", 16)
                    stg_ld[0] = st
                    em.wait("tensor", "VE", ve)
                    em.wait("vector", "LD", st)
                    for j in range(2):
                        lastve = emit_T(ktt[:, j * 128:(j + 1) * 128],
                                        stg[:, j * 128:(j + 1) * 128])
                    em.wait("sync", "VE", lastve)
                    qdst = qno3[qt2, hh2].rearrange(
                        "p (g q) -> p g q", q=256)[:, g0:g0 + 2,
                                                   hlf * 128:hlf * 128 + 128]
                    st = em.emit("sync", lambda E, qdst=qdst: E.dma_start(
                        qdst, stg.ap().rearrange("p (h c) -> p h c", c=128)),
                        "LD", 16)
                    stg_ld[0] = st
                    kt_mm[a % 2] = em.cnt["MM"]
            for a in range(4):
                tloc = m * 4 + a
                pb = PSF[4]
                em.wait("tensor", "VE", gse_[0])
                em.wait("tensor", "SE", ck_evac_se[0] or 0)
                for dc in range(32):
                    em.emit("tensor", lambda E, pb=pb, dc=dc, a=a: E.matmul(
                        pb[:, 0:96],
                        hsT[:, dc * MEG + 2 * a * 128:dc * MEG + 2 * a * 128 + 128],
                        wgTb[:, dc * 96:(dc + 1) * 96],
                        start=(dc == 0), stop=False), "MM", None)
                em.emit("tensor", lambda E, pb=pb: E.matmul(
                    pb[:, 0:96], onesrow[0:1, :], bgateb[0:1, :],
                    start=False, stop=True), "MM", 1)
                em.wait("scalar", "MM", em.cnt["MM"])
                em.wait("scalar", "VE", gse_[1])
                se = em.emit("scalar", lambda E, pb=pb: E.activation(
                    gt[:, :], pb[:, 0:96], AF.Exp, scale=-1.0), "SE", 1)
                em.wait("vector", "SE", se)
                def sig(E, tloc=tloc):
                    vch(E, E.tensor_scalar_add(gt[:, :], gt[:, :], 1.0))
                    return E.reciprocal(gates[:, tloc * 96:(tloc + 1) * 96], gt[:, :])
                gse_[1] = em.emit("vector", sig, "VE", 1)
                gse_[0] = gse_[1]
        em.barrier()
        if d_rk is not None:
            em.emit("sync", lambda E: E.dma_start(d_rk, rkd), "LD", 16)
        if d_vtok is not None:
            em.emit("sync", lambda E: E.dma_start(d_vtok, vtok3d), "LD", 16)
        if d_ck is not None:
            em.emit("sync", lambda E: E.dma_start(d_ck, ck_fm[:, :]), "LD", 16)
        if d_cv is not None:
            em.emit("sync", lambda E: E.dma_start(d_cv, cv_fm[:, :]), "LD", 16)
        if d_q is not None:
            em.emit("sync", lambda E: E.dma_start(d_q, qro3), "LD", 16)
        if d_gates is not None:
            em.emit("sync", lambda E: E.dma_start(d_gates, gates[:, :]), "LD", 16)
        em.barrier()
        for c in reversed(cm):
            c.__exit__(None, None, None)

    kv_evac = [None, None]
    rtmp = nc.alloc_sbuf_tensor("c_rtmp", [128, 128], BF16, side="right")
    phase2()
    if upto < 3:
        em.wait("sync", "LD", em.cnt["LD"])
        em.finalize()
        return nc, dbg_out

    mixcm = nc.sbuf_tensor("mix", [128, NT * D], BF16)
    mix = mixcm.__enter__()

    def phase3():
        cm = [
            nc.sbuf_tensor("a_rk", [128, S], BF16),
            nc.sbuf_tensor("a_vaug", [128, 32 * 132], BF16),
            nc.sbuf_tensor("a_cvaug", [128, 2 * 132], BF16),
            nc.sbuf_tensor("a_rq0", [128, G * 256], BF16),
            nc.sbuf_tensor("a_rq1", [128, G * 256], BF16),
            nc.sbuf_tensor("a_rqu", [128, G * 256], BF16),
            nc.sbuf_tensor("a_pr0", [128, 256], BF16),
            nc.sbuf_tensor("a_pr1", [128, 256], BF16),
            nc.sbuf_tensor("a_pr2", [128, 256], BF16),
            nc.sbuf_tensor("a_pr3", [128, 256], BF16),
            nc.sbuf_tensor("a_pf0", [128, 256], BF16),
            nc.sbuf_tensor("a_pf1", [128, 256], BF16),
            nc.sbuf_tensor("a_pf2", [128, 256], BF16),
            nc.sbuf_tensor("a_ps0", [128, 256], BF16),
            nc.sbuf_tensor("a_ps1", [128, 256], BF16),
            nc.sbuf_tensor("a_ps2", [128, 256], BF16),
            nc.sbuf_tensor("a_pmem", [1, 2 * 256], BF16),
            nc.sbuf_tensor("a_qsum", [128, 256], BF16),
            nc.sbuf_tensor("a_maskc", [128, 32 * 256], BF16),
            nc.sbuf_tensor("a_impac", [128, 2 * 256], F32),
            nc.sbuf_tensor("a_impq", [128, 512], F32),
            nc.sbuf_tensor("a_sele", [128, 256], F32),
            nc.sbuf_tensor("a_selw", [128, 256], F32),
            nc.sbuf_tensor("a_selm", [128, 8], F32),
            nc.sbuf_tensor("a_selq", [128, 512], BF16),
            nc.sbuf_tensor("a_mfT", [128, 512], BF16),
            nc.sbuf_tensor("a_nrm", [128, 8], F32),
            nc.sbuf_tensor("a_mixt", [128, 256], F32),
            nc.sbuf_tensor("a_cmq", [128, 512], BF16),
            nc.sbuf_tensor("a_cnq", [128, 512], BF16),
            nc.sbuf_tensor("a_sm", [128, 4 * 128], BF16),
        ]
        t = [c.__enter__() for c in cm]
        (rk, vaug, cvaug, rq0, rq1, rqu, pr0, pr1, pr2, pr3,
         pf0, pf1, pf2, ps0, ps1, ps2, pmem, qsum, maskc, impac, impq,
         sele, selw, selm, selq, mfT, nrm, mixt, cmq, cnq, sm) = t
        rq2 = [rq0, rq1]
        prb = [pr0, pr1, pr2, pr3]
        pfin = [pf0, pf1, pf2]
        psli = [ps0, ps1, ps2]

        em.emit("sync", lambda E: E.dma_start(sm[:, 0:128], ap["smA"]), "LD", 16)
        em.emit("sync", lambda E: E.dma_start(sm[:, 128:256], ap["smB"]), "LD", 16)
        em.emit("sync", lambda E: E.dma_start(sm[:, 256:384], ap["smC"]), "LD", 16)
        em.emit("sync", lambda E: E.dma_start(sm[:, 384:512], ap["fmdiag"]), "LD", 16)
        em.barrier()

        st = {
            "qk_se": [0, 0], "qk_ve": [0, 0],
            "prb_mve": [0, 0, 0, 0], "prb_mm": [0, 0, 0, 0],
            "pf_ve": [0, 0, 0], "pf_mm": [0, 0, 0], "qk_po": [0, 0], "qsum_mm": 0,
            "pmem_se": [0, 0], "pmem_mm": [0, 0],
            "bank": {2: ("VE", 0), 3: ("VE", 0), 4: ("VE", 0)},
            "impacT_mm": 0, "selq_mm": 0, "stm_ld": 0, "mask_ve": 0,
            "rq_mm": [0, 0], "rq_ld": [0, 0],
            "rqu_mm": 0, "rqu_ld": 0, "cmq_ve": 0, "cmq_ld": 0,
            "cnq_ve": 0, "cnq_ld": 0,
            "rk_mm": 0, "vaug_mm": 0, "cvaug_mm": 0,
            "cv_ve": 0, "pov": 0, "vaug_gq": 0, "rk_ld": 0,
            "dgv": 0, "fmask_ve": 0,
        }

        def wbank(eng, b):
            sname, v = st["bank"][b]
            em.wait(eng, sname, v)

        def comp(hh, qt):
            s = qt % 2
            qcol = qt * 256
            em.wait("sync", "MM", st["rq_mm"][s])
            st["rq_ld"][s] = em.emit("sync", lambda E, hh=hh, qt=qt, s=s:
                                     E.dma_start(rq2[s][:, :], qro3[qt, hh]),
                                     "LD", 16)
            em.wait("sync", "MM", st["rqu_mm"])
            st["rqu_ld"] = em.emit("sync", lambda E, hh=hh, qt=qt: E.dma_start(
                rqu[:, :], qno3[qt, hh]), "LD", 16)
            em.wait("sync", "VE", st["cmq_ve"])
            for sec in range(2):
                em.emit("sync", lambda E, sec=sec, qcol=qcol: E.dma_start(
                    cmq[:, sec * 256:(sec + 1) * 256],
                    ap["cmask"][1 + sec * 128:1 + (sec + 1) * 128,
                                qcol:qcol + 256]), "LD", 16)
            st["cmq_ld"] = em.cnt["LD"]
            em.wait("sync", "VE", st["cnq_ve"])
            for sec in range(2):
                em.emit("sync", lambda E, sec=sec, qcol=qcol: E.dma_start(
                    cnq[:, sec * 256:(sec + 1) * 256],
                    ap["cneg"][sec * 128:(sec + 1) * 128, qcol:qcol + 256]),
                    "LD", 16)
            st["cnq_ld"] = em.cnt["LD"]

            em.wait("gpsimd", "LD", st["rqu_ld"])
            em.wait("gpsimd", "MM", st["qsum_mm"])
            def qsb(E):
                vchp(E, E.tensor_add(qsum[:, :], rqu[:, 0:256], rqu[:, 256:512]))
                vchp(E, E.tensor_add(qsum[:, :], qsum[:, :], rqu[:, 512:768]))
                return E.tensor_add(qsum[:, :], qsum[:, :], rqu[:, 768:1024])
            qs_po = em.emit("gpsimd", qsb, "PO", 1)
            for sec in range(2):
                pb = PSF[sec]
                em.wait("tensor", "SE", st["qk_se"][sec])
                em.wait("tensor", "PO", qs_po)
                mmv = em.emit("tensor", lambda E, pb=pb, hh=hh, sec=sec:
                              E.matmul(pb[:, 0:256],
                                       ck_fm[:, hh * 256 + sec * 128:
                                             hh * 256 + (sec + 1) * 128],
                                       qsum[:, :], start=True, stop=True), "MM", 1)
                em.wait("vector", "MM", mmv)
                em.wait("vector", "MM", st["impacT_mm"])
                iv = em.emit("vector", lambda E, pb=pb, sec=sec:
                             E.tensor_copy(impac[:, sec * 256:(sec + 1) * 256],
                                           pb[:, 0:256]), "VE", 1)
                st["qk_ve"][sec] = iv
            st["qsum_mm"] = em.cnt["MM"]

            def comp_pv(g):
                psl = (g % 2) * 2
                sl = g % 2
                em.wait("tensor", "VE", st["prb_mve"][psl])
                em.wait("tensor", "VE", st["prb_mve"][psl + 1])
                em.wait("tensor", "SE", st["pmem_se"][sl])
                em.wait("tensor", "VE", st["cv_ve"])
                em.wait("tensor", "PO", st["pov"])
                wbank("tensor", 3)
                wbank("tensor", 4)
                blds = []
                for sj in range(2):
                    pbo = PSF[3 + sj]
                    for sec in range(2):
                        blds.append(lambda E, pbo=pbo, sj=sj, sec=sec, psl=psl:
                                    E.matmul(pbo[:, 0:129],
                                             prb[psl + sec][:, sj * 128:(sj + 1) * 128],
                                             cvaug[:, sec * 132:sec * 132 + 129],
                                             start=(sec == 0), stop=False,
                                             skip_group_check=True))
                    blds.append(lambda E, pbo=pbo, sj=sj, sl=sl, hh=hh: E.matmul(
                        pbo[:, 0:129],
                        pmem[0:1, sl * 256 + sj * 128:sl * 256 + (sj + 1) * 128],
                        memvaug[0:1, hh * 132:hh * 132 + 129],
                        start=False, stop=(sj == 1), skip_group_check=True))
                for b in blds[:-1]:
                    em.emit("tensor", b, "MM", None)
                mmv = em.emit("tensor", blds[-1], "MM", 1)
                st["prb_mm"][psl] = mmv
                st["prb_mm"][psl + 1] = mmv
                st["pmem_mm"][sl] = mmv
                st["cvaug_mm"] = mmv
                em.wait("vector", "MM", mmv)
                ves = []
                for sj, sub in enumerate((2 * qt, 2 * qt + 1)):
                    pbo = PSF[3 + sj]
                    head = hh * G + g
                    def cnorm(E, pbo=pbo, head=head, sub=sub):
                        vch(E, E.reciprocal(nrm[:, 0:1], pbo[:, 128:129]))
                        vch(E, E.tensor_mul(nrm[:, 1:2], nrm[:, 0:1],
                                            gates[:, sub * 96 + 3 * head:
                                                  sub * 96 + 3 * head + 1]))
                        mcol = sub * D + head * 128
                        return E.tensor_scalar(mix[:, mcol:mcol + 128],
                                               pbo[:, 0:128],
                                               nrm[:, 1:2], None, ALU.mult)
                    ves.append(em.emit("vector", cnorm, "VE", 1))
                st["bank"][3] = ("VE", ves[0])
                st["bank"][4] = ("VE", ves[1])

            for g in range(G):
                psl = (g % 2) * 2
                sl = g % 2
                for sec in range(2):
                    pb = PSF[sec]
                    em.wait("tensor", "SE", st["qk_se"][sec])
                    em.wait("tensor", "VE", st["qk_ve"][sec])
                    em.wait("tensor", "PO", st["qk_po"][sec])
                    em.wait("tensor", "LD", st["rqu_ld"])
                    mmv = em.emit("tensor", lambda E, pb=pb, hh=hh, sec=sec, g=g:
                                  E.matmul(pb[:, 0:256],
                                           ck_fm[:, hh * 256 + sec * 128:
                                                 hh * 256 + (sec + 1) * 128],
                                           rqu[:, g * 256:(g + 1) * 256],
                                           start=True, stop=True), "MM", 1)
                    em.wait("scalar", "MM", mmv)
                    em.wait("scalar", "MM", st["prb_mm"][psl + sec])
                    em.wait("scalar", "VE", st["prb_mve"][psl + sec])
                    se = em.emit("scalar", lambda E, pb=pb, psl=psl, sec=sec:
                                 E.activation(prb[psl + sec][:, :], pb[:, 0:256],
                                              AF.Exp, scale=SCALE,
                                              bias=eshv[:, :]), "SE", 1)
                    st["qk_se"][sec] = se
                    em.wait("vector", "MM", mmv)
                    em.wait("vector", "SE", se)
                    em.wait("vector", "LD", st["cmq_ld"])
                    mv = em.emit("vector", lambda E, psl=psl, sec=sec:
                                 E.tensor_mul(prb[psl + sec][:, :],
                                              prb[psl + sec][:, :],
                                              cmq[:, sec * 256:(sec + 1) * 256]),
                                 "VE", 1)
                    st["prb_mve"][psl + sec] = mv
                pbm = PSF[2]
                if g == 0:
                    wbank("tensor", 2)
                em.wait("tensor", "SE", st["pmem_se"][1 - sl])
                mmv = em.emit("tensor", lambda E, pbm=pbm, hh=hh, g=g: E.matmul(
                    pbm[0:1, 0:256], memkb[:, hh:hh + 1],
                    rqu[:, g * 256:(g + 1) * 256], start=True, stop=True), "MM", 1)
                em.wait("scalar", "MM", mmv)
                em.wait("scalar", "MM", st["pmem_mm"][sl])
                se = em.emit("scalar", lambda E, pbm=pbm, sl=sl: E.activation(
                    pmem[0:1, sl * 256:(sl + 1) * 256], pbm[0:1, 0:256],
                    AF.Exp, scale=SCALE, bias=eshv[0:1, :]), "SE", 1)
                st["pmem_se"][sl] = se
                if g >= 1:
                    comp_pv(g - 1)
            comp_pv(G - 1)
            st["bank"][2] = ("SE", st["pmem_se"][(G - 1) % 2])
            st["rqu_mm"] = em.cnt["MM"]
            st["cmq_ve"] = em.cnt["VE"]

        def selA(hh, qt):
            em.wait("vector", "LD", st["cnq_ld"])
            for sec in range(2):
                em.emit("vector", lambda E, sec=sec: E.scalar_tensor_tensor(
                    impac[:, sec * 256:(sec + 1) * 256],
                    impac[:, sec * 256:(sec + 1) * 256],
                    SCALE / G, cnq[:, sec * 256:(sec + 1) * 256],
                    ALU.mult, ALU.add), "VE", 1)
            fin_ve = em.cnt["VE"]
            st["cnq_ve"] = fin_ve
            em.wait("tensor", "VE", fin_ve)
            for sj in range(2):
                for sec in range(2):
                    emit_T(impac[:, sec * 256 + sj * 128:sec * 256 + sj * 128 + 128],
                           impq[:, sj * 256 + sec * 128:sj * 256 + sec * 128 + 128],
                           f32=True)
            st["impacT_mm"] = em.cnt["MM"]
            em.wait("vector", "MM", st["selq_mm"])
            for sj in range(2):
                iq = impq[:, sj * 256:(sj + 1) * 256]
                em.wait("vector", "VE", em.cnt["VE"])
                def topk(E, iq=iq):
                    vch(E, E.tensor_copy(sele[:, :], iq))
                    for it in range(4):
                        vch(E, E.reduce_max(selm[:, it:it + 1], sele[:, :],
                                            axis=AX.X))
                        if it < 3:
                            vch(E, E.tensor_scalar(selw[:, :], sele[:, :],
                                                   selm[:, it:it + 1], None,
                                                   ALU.is_equal))
                            vch(E, E.scalar_tensor_tensor(
                                sele[:, :], selw[:, :], NEGB,
                                sele[:, :], ALU.mult, ALU.add))
                    return E.engine_nop()
                ve = em.emit("vector", topk, "VE", 1)
                em.wait("scalar", "VE", ve)
                se = em.emit("scalar", lambda E, iq=iq: E.activation(
                    sele[:, :], iq, AF.Exp, bias=eshv[:, :],
                    accum_out=selm[:, 4:5]), "SE", 1)
                em.wait("vector", "SE", se)
                def fmask(E, iq=iq, sj=sj):
                    vch(E, E.tensor_scalar_mul(selm[:, 5:6], selm[:, 4:5], 1e-10))
                    vch(E, E.tensor_scalar(selw[:, :], sele[:, :], selm[:, 5:6],
                                           None, ALU.is_gt))
                    vch(E, E.tensor_scalar(selq[:, sj * 256:(sj + 1) * 256], iq,
                                           selm[:, 3:4], None, ALU.is_ge))
                    return E.tensor_mul(selq[:, sj * 256:(sj + 1) * 256],
                                        selq[:, sj * 256:(sj + 1) * 256],
                                        selw[:, :])
                em.emit("vector", fmask, "VE", 1)
            st["fmask_ve"] = em.cnt["VE"]

        def selB(hh, qt):
            qcol = qt * 256
            ncH = 4 * qt + 4
            em.wait("tensor", "VE", st["fmask_ve"])
            em.wait("tensor", "LD", st["stm_ld"])
            em.wait("vector", "LD", st["stm_ld"])
            for sec in range(2):
                for sj in range(2):
                    emit_T(selq[:, sj * 256 + sec * 128:sj * 256 + sec * 128 + 128],
                           mfT[:, sec * 256 + sj * 128:sec * 256 + sj * 128 + 128])
            st["selq_mm"] = em.cnt["MM"]
            em.wait("sync", "VE", em.cnt["VE"])
            stm = em.emit("sync", lambda E, hh=hh, qcol=qcol: E.dma_start(
                maskfd[hh][:, qcol:qcol + 256].rearrange(
                    "(s p) q -> p s q", s=2),
                mfT.ap().rearrange("p (s q) -> p s q", s=2)), "LD", 16)
            st["stm_ld"] = stm
            em.wait("sync", "LD", stm)
            em.wait("sync", "VE", st["mask_ve"])
            em.wait("gpsimd", "VE", st["mask_ve"])
            em.wait("gpsimd", "LD", stm)
            for bb in range(8):
                src = maskfd[hh].rearrange("(c b) q -> b c q", b=8)[bb, 0:ncH]
                src = src.unsqueeze(0).broadcast_to([16, ncH, OWN])[
                    :, :, qcol:qcol + 256]
                eng, sem = ("sync", "LD") if bb < 4 else ("gpsimd", "GQ")
                em.emit(eng, lambda E, src=src, bb=bb, ncH=ncH: E.dma_start(
                    maskc[16 * bb:16 * (bb + 1), :].rearrange(
                        "p (c q) -> p c q", q=256)[:, 0:ncH], src), sem, 16)
            ldm_ld, ldm_gq = em.cnt["LD"], em.cnt["GQ"]
            em.wait("vector", "LD", ldm_ld)
            em.wait("vector", "GQ", ldm_gq)
            def adddiag(E, qt=qt):
                for sj in range(2):
                    c = 4 * qt + 2 * sj
                    E.tensor_add(
                        maskc[:, c * 256 + sj * 128:c * 256 + sj * 128 + 128],
                        maskc[:, c * 256 + sj * 128:c * 256 + sj * 128 + 128],
                        sm[:, 384:512])
                return E.engine_nop()
            st["dgv"] = em.emit("vector", adddiag, "VE", 1)

        def fine(hh, qt):
            s = qt % 2
            ncH = 4 * qt + 4

            def slide_hits(c):
                hits = []
                for sj in range(2):
                    base = 4 * qt + 2 * sj
                    if c == base - 1:
                        hits.append((sj, sm[:, 0:128]))
                    elif c == base:
                        hits.append((sj, sm[:, 128:256]))
                    elif c == base + 1:
                        hits.append((sj, sm[:, 256:384]))
                return hits

            for g in range(G):
                def fine_pv(c):
                    cs = c % 3
                    qlo = 0 if c <= 4 * qt + 1 else 1
                    em.wait("tensor", "VE", st["pf_ve"][cs])
                    em.wait("tensor", "GQ", st["vaug_gq"])
                    em.wait("tensor", "PO", st["pov"])
                    if c == 0:
                        wbank("tensor", 2)
                        wbank("tensor", 3)
                    blds = []
                    for sj in range(2):
                        if qlo <= sj and c <= 4 * qt + 2 * sj + 1:
                            blds.append(lambda E, sj=sj, c=c, cs=cs: E.matmul(
                                PSF[2 + sj][:, 0:129],
                                pfin[cs][:, sj * 128:(sj + 1) * 128],
                                vaug[:, c * 132:c * 132 + 129],
                                start=(c == 0), stop=False,
                                skip_group_check=True))
                    for sj, _w in slide_hits(c):
                        blds.append(lambda E, sj=sj, c=c, cs=cs: E.matmul(
                            PSF[2 + sj][:, 132:261],
                            psli[cs][:, sj * 128:(sj + 1) * 128],
                            vaug[:, c * 132:c * 132 + 129],
                            start=False, stop=False, skip_group_check=True))
                    for b in blds[:-1]:
                        em.emit("tensor", b, "MM", None)
                    mmv = em.emit("tensor", blds[-1], "MM", 1)
                    st["pf_mm"][cs] = mmv

                for c in range(ncH):
                    cs = c % 3
                    ps2 = c % 2
                    qlo = 0 if c <= 4 * qt + 1 else 1
                    pb = PSF[ps2]
                    em.wait("tensor", "SE", st["qk_se"][ps2])
                    em.wait("tensor", "VE", st["qk_ve"][ps2])
                    em.wait("tensor", "PO", st["qk_po"][ps2])
                    em.wait("tensor", "LD", st["rq_ld"][s])
                    mmv = em.emit("tensor", lambda E, pb=pb, c=c, g=g, qlo=qlo, s=s:
                                  E.matmul(pb[:, 0:256 - qlo * 128],
                                           rk[:, c * 128:(c + 1) * 128],
                                           rq2[s][:, g * 256 + qlo * 128:
                                                  (g + 1) * 256],
                                           start=True, stop=True), "MM", 1)
                    em.wait("scalar", "MM", mmv)
                    em.wait("scalar", "MM", st["prb_mm"][ps2])
                    em.wait("scalar", "VE", st["prb_mve"][ps2])
                    se = em.emit("scalar", lambda E, pb=pb, ps2=ps2, qlo=qlo:
                                 E.activation(prb[ps2][:, qlo * 128:256],
                                              pb[:, 0:256 - qlo * 128], AF.Exp,
                                              scale=SCALE, bias=eshv[:, :]),
                                 "SE", 1)
                    st["qk_se"][ps2] = se
                    em.wait("vector", "SE", se)
                    em.wait("vector", "VE", st["dgv"])
                    em.wait("vector", "MM", st["pf_mm"][cs])
                    ve = em.emit("vector", lambda E, cs=cs, ps2=ps2, c=c, qlo=qlo:
                                 E.tensor_mul(pfin[cs][:, qlo * 128:256],
                                              prb[ps2][:, qlo * 128:256],
                                              maskc[:, c * 256 + qlo * 128:
                                                    (c + 1) * 256]), "VE", 1)
                    for sj, w in slide_hits(c):
                        ve = em.emit("vector", lambda E, cs=cs, ps2=ps2, sj=sj, w=w:
                                     E.tensor_mul(psli[cs][:, sj * 128:(sj + 1) * 128],
                                                  prb[ps2][:, sj * 128:(sj + 1) * 128],
                                                  w), "VE", 1)
                    st["prb_mve"][ps2] = ve
                    st["pf_ve"][cs] = ve
                    if c >= 2:
                        fine_pv(c - 2)
                if ncH >= 2:
                    fine_pv(ncH - 2)
                fine_pv(ncH - 1)
                mmv = em.cnt["MM"]
                em.wait("vector", "MM", mmv)
                ves = []
                for sj, sub in enumerate((2 * qt, 2 * qt + 1)):
                    pbo = PSF[2 + sj]
                    head = hh * G + g
                    def fsev(E, pbo=pbo, sub=sub, head=head):
                        gb = sub * 96 + 3 * head
                        vch(E, E.reciprocal(nrm[:, 0:1], pbo[:, 128:129]))
                        vch(E, E.tensor_mul(nrm[:, 1:2], nrm[:, 0:1],
                                            gates[:, gb + 1:gb + 2]))
                        vch(E, E.reciprocal(nrm[:, 2:3], pbo[:, 260:261]))
                        vch(E, E.tensor_mul(nrm[:, 3:4], nrm[:, 2:3],
                                            gates[:, gb + 2:gb + 3]))
                        vch(E, E.tensor_scalar(mixt[:, 0:128], pbo[:, 0:128],
                                               nrm[:, 1:2], None, ALU.mult))
                        vch(E, E.tensor_scalar(mixt[:, 128:256], pbo[:, 132:260],
                                               nrm[:, 3:4], None, ALU.mult))
                        vch(E, E.tensor_add(mixt[:, 0:128], mixt[:, 0:128],
                                            mixt[:, 128:256]))
                        mcol = sub * D + head * 128
                        return E.tensor_add(mix[:, mcol:mcol + 128],
                                            mix[:, mcol:mcol + 128],
                                            mixt[:, 0:128])
                    ves.append(em.emit("vector", fsev, "VE", 1))
                st["bank"][2] = ("VE", ves[0])
                st["bank"][3] = ("VE", ves[1])
            st["mask_ve"] = em.cnt["VE"]
            st["rq_mm"][s] = em.cnt["MM"]
            st["rk_mm"] = em.cnt["MM"]
            st["vaug_mm"] = em.cnt["MM"]

        for hh in range(KVH):
            em.wait("sync", "MM", st["rk_mm"])
            st["rk_ld"] = em.emit("sync", lambda E, hh=hh: E.dma_start(
                rk[:, :], rkd[hh]), "LD", 16)
            em.wait("gpsimd", "MM", st["vaug_mm"])
            st["vaug_gq"] = em.emit("gpsimd", lambda E, hh=hh: E.dma_start(
                vaug[:, :], vtok3d[hh]), "GQ", 16)
            em.wait("gpsimd", "GQ", st["vaug_gq"])
            def vones(E):
                for c in range(32):
                    E.memset(vaug[:, c * 132 + 128:c * 132 + 129], 1.0)
                E.memset(cvaug[:, 128:129], 1.0)
                return E.memset(cvaug[:, 132 + 128:132 + 129], 1.0)
            st["pov"] = em.emit("gpsimd", vones, "PO", 1)
            em.wait("vector", "MM", st["cvaug_mm"])
            for sec in range(2):
                emit_T(cv_fm[:, hh * 256 + sec * 128:hh * 256 + (sec + 1) * 128],
                       cvaug[:, sec * 132:sec * 132 + 128])
            st["cv_ve"] = em.cnt["VE"]

            comp(hh, 0)
            selA(hh, 0)
            for qt in range(8):
                if qt < 7:
                    comp(hh, qt + 1)
                selB(hh, qt)
                fine(hh, qt)
                if qt < 7:
                    selA(hh, qt + 1)
            for u in range(2):
                oc = hh * 2 + u
                em.emit("gpsimd", lambda E, oc=oc: E.dma_start(
                    wo_bfd[oc], ap["woT"][oc]), "WC", 16)
        em.barrier()
        if d_maskf is not None:
            em.emit("sync", lambda E: E.dma_start(d_maskf, maskfd), "LD", 16)
        if d_mix is not None:
            em.emit("sync", lambda E: E.dma_start(d_mix, mix[:, :]), "LD", 16)
        em.barrier()
        for c in reversed(cm):
            c.__exit__(None, None, None)

    phase3()
    if upto < 4:
        em.wait("sync", "LD", em.cnt["LD"])
        em.finalize()
        return nc, dbg_out

    def phase4():
        cm = [
            nc.sbuf_tensor("p4_w0", [128, 32 * 256], BF16),
            nc.sbuf_tensor("p4_w1", [128, 32 * 256], BF16),
            nc.sbuf_tensor("p4_s0", [128, D], BF16),
            nc.sbuf_tensor("p4_s1", [128, D], BF16),
        ]
        t = [c.__enter__() for c in cm]
        w0, w1, s0, s1 = t
        wcol = [w0, w1]
        strips = [s0, s1]
        w_r = Rot(em, 2)
        strip_st = [0, 0]
        evac_ve = [0, 0, 0, 0]
        for pp in range(8):
            for r in (2 * pp, 2 * pp + 1):
                for dc in range(32):
                    emit_T(mix[:, r * D + dc * 128:r * D + dc * 128 + 128],
                           mix[:, r * D + dc * 128:r * D + dc * 128 + 128])
            mixT_ready = em.cnt["VE"]
            for oc in range(16):
                i = pp * 16 + oc
                wb = wcol[oc % 2]
                w_r.acquire(i, "gpsimd")
                gq = em.emit("gpsimd", lambda E, wb=wb, oc=oc: E.dma_start(
                    wb[:, :], wo_bfd[oc]), "GQ", 16)
                em.wait("tensor", "GQ", gq)
                em.wait("tensor", "VE", mixT_ready)
                for rj in range(2):
                    r = 2 * pp + rj
                    pb = PSF[2 * rj + oc % 2]
                    em.wait("tensor", "VE", evac_ve[2 * rj + oc % 2])
                    for dc in range(32):
                        em.emit("tensor", lambda E, pb=pb, dc=dc, r=r, wb=wb:
                                E.matmul(pb[:, 0:256],
                                         mix[:, r * D + dc * 128:r * D + dc * 128 + 128],
                                         wb[:, dc * 256:(dc + 1) * 256],
                                         start=(dc == 0), stop=(dc == 31)),
                                "MM", 1 if dc == 31 else None)
                    mmv = em.cnt["MM"]
                    em.wait("vector", "MM", mmv)
                    if oc < 2:
                        em.wait("vector", "LD", strip_st[rj])
                    ve = em.emit("vector", lambda E, rj=rj, pb=pb, oc=oc:
                                 E.tensor_copy(strips[rj][:, oc * 256:(oc + 1) * 256],
                                               pb[:, 0:256]), "VE", 1)
                    evac_ve[2 * rj + oc % 2] = ve
                w_r.release(i, "MM", em.cnt["MM"])
            for rj in range(2):
                r = 2 * pp + rj
                em.wait("sync", "VE", evac_ve[2 * rj]);
                em.wait("sync", "VE", evac_ve[2 * rj + 1])
                strip_st[rj] = em.emit("sync", lambda E, rj=rj, r=r: E.dma_start(
                    attnd[r * 128:(r + 1) * 128, :], strips[rj][:, :]), "LD", 16)
        em.barrier()
        for c in reversed(cm):
            c.__exit__(None, None, None)

    phase4()
    mixcm.__exit__(None, None, None)
    if upto < 5:
        em.wait("sync", "LD", em.cnt["LD"])
        em.finalize()
        return nc, dbg_out

    phase_ffn(None, out, 2, tag="f5",
              row_src=lambda row0: x1d[(row0 // 128) * 256:
                                       (row0 // 128) * 256 + 128, :],
              chunk_src=x2d, src2_ap=attnd, x2_ap=x2d)
    em.wait("sync", "LD", em.cnt["LD"])
    em.finalize()
    return nc, dbg_out


def build_once(dbg=()):
    return build_program(dbg)


def kernel(**inputs):
    shared, per_core = prep_inputs(inputs)
    nc, _ = build_program()
    in_maps = []
    for c in range(8):
        m = dict(shared)
        m.update(per_core[c])
        in_maps.append(m)
    res = run_bass_kernel_spmd(nc, in_maps, core_ids=list(range(8)))
    outp = np.empty((B, S, D), dtype=np.float32)
    for c in range(8):
        b, hh = c // 2, c % 2
        o = np.asarray(res.results[c]["out"], dtype=np.float32)
        tb = (np.arange(NBLK) if hh == 0
              else np.arange(NBLK).reshape(-1, 2)[:, ::-1].reshape(-1))
        own_true = tb[0::2]
        outp[b].reshape(NBLK, 128, D)[own_true] = o.reshape(NT, 128, D)
    return outp
